# version 06
# baseline (speedup 1.0000x reference)
import contextlib
import numpy as np
import concourse.bass as bass
import concourse.mybir as mybir
from concourse.bass_utils import run_bass_kernel_spmd

F32 = mybir.dt.float32
BF16 = mybir.dt.bfloat16
AF = mybir.ActivationFunctionType
ALU = mybir.AluOpType
AX = mybir.AxisListType

N = 4096
D = 2048
H = 16
HD = 128
E = 8
I = 1024
TP = 512
NPASS = N // TP
EPS = 1e-5
NEG = -1e30
STAGE = 3


class Sem:
    def __init__(self, handle):
        self.h = handle
        self.n = 0

    def inc(self, instr, k=1):
        instr.then_inc(self.h, k)
        self.n += k
        return self.n

    def wait(self, eng, n=None):
        eng.wait_ge(self.h, self.n if n is None else n)


def build():
    nc = bass.Bass("TRN2", target_bir_lowering=False, debug=False)

    x_in = nc.dram_tensor("x_in", [N, D], F32, kind="ExternalInput").ap()
    ld_in = nc.dram_tensor("ld_in", [1, H], F32, kind="ExternalInput").ap()
    wq_in = nc.dram_tensor("wq_in", [D, D], F32, kind="ExternalInput").ap()
    wk_in = nc.dram_tensor("wk_in", [D, D], F32, kind="ExternalInput").ap()
    wv_in = nc.dram_tensor("wv_in", [D, D], F32, kind="ExternalInput").ap()
    wo_in = nc.dram_tensor("wo_in", [D, D], F32, kind="ExternalInput").ap()
    wg1_in = nc.dram_tensor("wg1_in", [D, HD], F32, kind="ExternalInput").ap()
    wg2_in = nc.dram_tensor("wg2_in", [HD, D], F32, kind="ExternalInput").ap()
    nw_in = nc.dram_tensor("nw_in", [1, D], F32, kind="ExternalInput").ap()
    gw_in = nc.dram_tensor("gw_in", [E, D], F32, kind="ExternalInput").ap()
    gb_in = nc.dram_tensor("gb_in", [1, E], F32, kind="ExternalInput").ap()
    w1_in = nc.dram_tensor("w1_in", [E * I, D], F32, kind="ExternalInput").ap()
    w2_in = nc.dram_tensor("w2_in", [E * D, I], F32, kind="ExternalInput").ap()
    w3_in = nc.dram_tensor("w3_in", [E * I, D], F32, kind="ExternalInput").ap()
    ws1_in = nc.dram_tensor("ws1_in", [I, D], F32, kind="ExternalInput").ap()
    ws2_in = nc.dram_tensor("ws2_in", [D, I], F32, kind="ExternalInput").ap()
    ws3_in = nc.dram_tensor("ws3_in", [I, D], F32, kind="ExternalInput").ap()
    ident_in = nc.dram_tensor("ident_in", [128, 128], BF16, kind="ExternalInput").ap()
    identf_in = nc.dram_tensor("identf_in", [128, 128], F32, kind="ExternalInput").ap()
    ones_in = nc.dram_tensor("ones_in", [1, 128], F32, kind="ExternalInput").ap()
    difft_in = nc.dram_tensor("difft_in", [128, 128], F32, kind="ExternalInput").ap()
    cj1_in = nc.dram_tensor("cj1_in", [128, 1], F32, kind="ExternalInput").ap()
    cj127_in = nc.dram_tensor("cj127_in", [128, 1], F32, kind="ExternalInput").ap()

    y_out = nc.dram_tensor("y_out", [N, D], F32, kind="ExternalOutput").ap()

    x1s = nc.dram_tensor("x1s", [N, D], F32, kind="Internal").ap()
    x2s = nc.dram_tensor("x2s", [N, D], F32, kind="Internal").ap()
    w1ts = nc.dram_tensor("w1ts", [9 * D, I], BF16, kind="Internal").ap()
    w3ts = nc.dram_tensor("w3ts", [9 * D, I], BF16, kind="Internal").ap()
    w2ts = nc.dram_tensor("w2ts", [9 * I, D], BF16, kind="Internal").ap()
    wqs = nc.dram_tensor("wqs", [D, D], BF16, kind="Internal").ap()
    wks = nc.dram_tensor("wks", [D, D], BF16, kind="Internal").ap()
    wvs = nc.dram_tensor("wvs", [D, D], BF16, kind="Internal").ap()
    wos = nc.dram_tensor("wos", [D, D], BF16, kind="Internal").ap()
    wg1s = nc.dram_tensor("wg1s", [D, HD], BF16, kind="Internal").ap()
    wg2s = nc.dram_tensor("wg2s", [HD, D], BF16, kind="Internal").ap()

    es = contextlib.ExitStack()
    with es:
        def sb(name, shape, dt=F32, stack=None):
            return (stack or es).enter_context(nc.sbuf_tensor(name, shape, dt))

        def psum(name, shape, dt=F32):
            return es.enter_context(nc.psum_tensor(name, shape, dt))

        def sem(name):
            return Sem(es.enter_context(nc.semaphore(name)))

        ident = sb("ident", [128, 128], BF16)
        identf = sb("identf", [128, 128], F32)
        onesf = sb("onesf", [1, 128], F32)
        epst = sb("epst", [128, 1], F32)
        dwS = sb("dwS", [128, 1], F32)
        gbb = sb("gbb", [128, E], F32)
        gwT = sb("gwT", [128, 16 * E], BF16)
        nwb = sb("nwb", [128, D], BF16)
        ldb = sb("ldb", [128, H], F32)
        dt16 = sb("dt16", [128, H * 128], BF16)
        qdec = sb("qdec", [128, H], F32)
        kdec = sb("kdec", [128, H], F32)
        cdec = sb("cdec", [128, H], F32)
        Sst = sb("Sst", [128, H * HD], F32)
        Sbf = sb("Sbf", [128, H * HD], BF16)
        xacc = sb("xacc", [128, 4 * D], F32)
        u_fm = sb("u_fm", [128, 16 * TP], BF16)
        u_tm = sb("u_tm", [128, D], BF16)
        dwt = sb("dwt", [128, 4 * E], F32)
        lgs = sb("lgs", [128, E], F32)
        msk = sb("msk", [128, E], F32)
        lg2 = sb("lg2", [128, E], F32)
        smv = sb("smv", [128, E], F32)
        sm1 = sb("sm1", [128, 1], F32)
        sm1n = sb("sm1n", [128, 1], F32)
        sm2 = sb("sm2", [128, 1], F32)
        bnb = sb("bnb", [128, 96], F32)
        mvb = sb("mvb", [128, 8], F32)
        rstd = sb("rstd", [128, 4], F32)
        negb = sb("negb", [128, 4], F32)
        sTbf = sb("sTbf", [128, 2 * 128], BF16)
        o_tmp = sb("o_tmp", [128, 2 * 128], F32)
        msq = sb("msq", [128, 4 * H], F32)
        rsg = sb("rsg", [128, 4 * H], F32)
        sqd = sb("sqd", [128, 128], F32)

        pA = psum("pA", [128, 512], F32)
        pB = psum("pB", [128, 512], F32)
        pC = psum("pC", [128, 512], F32)
        pD = psum("pD", [128, 512], F32)
        pE_ = psum("pE_", [128, 512], F32)
        pF_ = psum("pF_", [128, 512], F32)
        pG = psum("pG", [128, 1024], BF16)
        pH = psum("pH", [128, 1024], BF16)

        s_cin = sem("s_cin")
        s_cinO = sem("s_cinO")
        s_cout = sem("s_cout")
        s_coutO = sem("s_coutO")
        s_x = sem("s_x")
        s_wl = sem("s_wl")
        s_w2l = sem("s_w2l")
        s_mm = sem("s_mm")
        s_ev = sem("s_ev")
        s_sl = sem("s_sl")
        s_st = sem("s_st")
        s_tp = sem("s_tp")
        s_te = sem("s_te")
        s_wr0 = sem("s_wr0")
        s_wr1 = sem("s_wr1")
        s_p0c = sem("s_p0c")

        bank_free = {}

        def wait_bank(eng, bank):
            if bank in bank_free:
                s, n = bank_free[bank]
                eng.wait_ge(s.h, n)

        def set_bank(bank, s):
            bank_free[bank] = (s, s.n)

        p0s = contextlib.ExitStack()
        cbuf = [sb("cbuf0", [128, 2048], BF16, p0s), sb("cbuf1", [128, 2048], BF16, p0s)]
        cast_jobs = []
        for csrc, cdst, rows, cols in [
            (wq_in, wqs, D, D), (wk_in, wks, D, D), (wv_in, wvs, D, D),
            (wg1_in, wg1s, D, HD), (wg2_in, wg2s, HD, D), (wo_in, wos, D, D),
        ]:
            if rows < 128:
                cast_jobs.append((csrc[0:rows, :], cdst[0:rows, :], rows, cols))
            else:
                for r in range(0, rows, 128):
                    cast_jobs.append((csrc[r:r + 128, :], cdst[r:r + 128, :], 128, cols))
        n_cast_out = [0, 0]
        for t in range(len(cast_jobs)):
            n_cast_out[t % 2] += 16

        su = contextlib.ExitStack()
        gwsb = sb("gwsb", [E, D], F32, su)
        ldrow = sb("ldrow", [1, H], F32, su)
        nwrow = sb("nwrow", [1, D], F32, su)
        gbrow = sb("gbrow", [1, E], F32, su)
        difft = sb("difft", [128, 128], F32, su)
        cj1 = sb("cj1", [128, 1], F32, su)
        cj127 = sb("cj127", [128, 1], F32, su)

        with nc.Block() as blk:
            @blk.gpsimd
            def _(g):
                for dst, srca in [(ident, ident_in), (identf, identf_in), (onesf, ones_in), (ldrow, ld_in),
                                  (nwrow, nw_in), (gbrow, gb_in), (difft, difft_in),
                                  (cj1, cj1_in), (cj127, cj127_in), (gwsb, gw_in)]:
                    s_x.inc(g.dma_start(dst[:], srca), 16)
                g.memset(epst[:], EPS)
                g.memset(dwS[:], 0.5)
                g.memset(Sst[:], 0.0)
                g.memset(Sbf[:], 0.0)

            @blk.vector
            def _(v):
                s_x.wait(v)

        gw_te = 0
        with nc.Block() as blk:
            gw_te = s_te.n
            gw_tp = s_tp.n
            @blk.tensor
            def _(t):
                s_mm.inc(t.matmul(pA[:, 0:E], onesf[:], gbrow[:], start=True, stop=True))
                s_mm.inc(t.matmul(pA[:, 16:16 + H], onesf[:], ldrow[:], start=True, stop=True))
                for i in range(4):
                    s_mm.inc(t.matmul([pB, pC, pD, pE_][i][:], onesf[:],
                                      nwrow[:, i * 512:(i + 1) * 512], start=True, stop=True))
                t.wait_ge(s_te.h, gw_te)
                t.wait_ge(s_ev.h, s_ev.n + 1)
                for dc in range(16):
                    dstp = [pA, pB][dc % 2][:, 0:8]
                    if dc >= 2:
                        t.wait_ge(s_te.h, gw_te + dc - 1)
                    s_tp.inc(t.transpose(dstp, gwsb[0:E, dc * 128:(dc + 1) * 128],
                                         identf[0:E, 0:E]))

            @blk.vector
            def _(v):
                s_mm.wait(v)
                v.tensor_copy(gbb[:], pA[:, 0:E])
                v.tensor_copy(ldb[:], pA[:, 16:16 + H])
                for i in range(4):
                    v.tensor_copy(nwb[:, i * 512:(i + 1) * 512], [pB, pC, pD, pE_][i][:])
                v.drain()
                s_ev.inc(v.tensor_copy(ldb[:], pA[:, 16:16 + H]))
                for dc in range(16):
                    srcp = [pA, pB][dc % 2][:, 0:8]
                    v.wait_ge(s_tp.h, gw_tp + dc + 1)
                    ci = v.tensor_copy(gwT[:, dc * E:(dc + 1) * E], srcp)
                    s_te.inc(ci)

            @blk.scalar
            def _(sc):
                s_te.wait(sc)
                s_ev.wait(sc)
                for h in range(H):
                    sc.activation(dt16[:, h * 128:(h + 1) * 128], difft[:], AF.Exp,
                                  scale=ldb[:, h:h + 1])
                    sc.activation(qdec[:, h:h + 1], cj1[:], AF.Exp, scale=ldb[:, h:h + 1])
                    sc.activation(kdec[:, h:h + 1], cj127[:], AF.Exp, scale=ldb[:, h:h + 1])
                    sc.activation(cdec[:, h:h + 1], ldb[:, h:h + 1], AF.Exp, scale=float(HD))
        su.close()

        p0c_mark = {}
        chunk_tp_done = {}
        asm_free = [0]
        p0c = contextlib.ExitStack()
        natb = [sb("natb0", [128, D], BF16, p0c), sb("natb1", [128, D], BF16, p0c)]
        asmb = sb("asmb", [128, 16 * I], BF16, p0c)
        p0c_cnt = [0, 0]
        with nc.Block() as blk:
            groups = []
            for e in range(9):
                w1src = w1_in[e * I:(e + 1) * I, :] if e < E else ws1_in
                w3src = w3_in[e * I:(e + 1) * I, :] if e < E else ws3_in
                w2src = w2_in[e * D:(e + 1) * D, :] if e < E else ws2_in
                groups.append((e, "w1", w1src, 8, D, 16, I, w1ts, e * 16))
                groups.append((e, "w3", w3src, 8, D, 16, I, w3ts, e * 16))
                groups.append((e, "w2", w2src, 16, I, 8, D, w2ts, e * 8))
            tp0 = s_tp.n
            cum = tp0
            for (_, _, _, nch_, _, och_, _, _, _) in groups:
                for c in range(nch_):
                    chunk_tp_done[len(chunk_tp_done)] = cum + och_ * (c + 1)
                cum += nch_ * och_
            for gi, (e, nm, fsrc, nch, ncol, och, ocol, tdst, orow0) in enumerate(groups):
                tp_base = s_tp.n
                te_base = s_te.n
                ld_marks = []

                @blk.gpsimd
                def _(g):
                    for c in range(nch):
                        t_idx = sum(x[3] for x in groups[:gi]) + c
                        if t_idx >= 2:
                            g.wait_ge(s_tp.h, chunk_tp_done[t_idx - 2])
                        cin = [s_cin, s_cinO][t_idx % 2]
                        cin.inc(g.dma_start(natb[t_idx % 2][:, 0:ncol],
                                            fsrc[c * 128:(c + 1) * 128, :]), 16)
                        ld_marks.append((cin, cin.n))

                @blk.tensor
                def _(t):
                    t.wait_ge(s_te.h, te_base)
                    for c in range(nch):
                        t_idx = sum(x[3] for x in groups[:gi]) + c
                        cs, cn = ld_marks[c]
                        t.wait_ge(cs.h, cn)
                        for k in range(och):
                            i = c * och + k
                            bank = [pG, pH][i % 2]
                            if i >= 2:
                                t.wait_ge(s_te.h, te_base + i - 1)
                            s_tp.inc(t.transpose(bank[:, 0:128],
                                                 natb[t_idx % 2][:, k * 128:(k + 1) * 128],
                                                 ident[:]))

                @blk.vector
                def _(v):
                    v.wait_ge(s_p0c.h, asm_free[0])
                    for c in range(nch):
                        for k in range(och):
                            i = c * och + k
                            bank = [pG, pH][i % 2]
                            v.wait_ge(s_tp.h, tp_base + i + 1)
                            s_te.inc(v.tensor_copy(
                                asmb[:, k * ocol + c * 128:k * ocol + (c + 1) * 128],
                                bank[:, 0:128]))
                ev_end = s_te.n

                @blk.sync
                def _(sy):
                    sy.wait_ge(s_te.h, ev_end)
                    sy.wait_ge(s_p0c.h, s_p0c.n)
                    for k in range(och):
                        s_p0c.inc(sy.dma_start(
                            tdst[(orow0 + k) * 128:(orow0 + k + 1) * 128, :],
                            asmb[:, k * ocol:(k + 1) * ocol]), 16)
                asm_free[0] = s_p0c.n
                if nm == "w2":
                    p0c_mark[e] = s_p0c.n
        p0c.close()

        def emit_ln_tile(blk, j, xw_done):
            @blk.vector
            def _(v):
                v.wait_ge(s_x.h, xw_done)
                for k in range(4):
                    v.bn_stats(bnb[:, (4 * j + k) * 6:(4 * j + k + 1) * 6],
                               xacc[:, j * D + k * 512:j * D + (k + 1) * 512])
                v.drain()
                s_ev.inc(v.bn_aggr(mvb[:, 2 * j:2 * j + 2], bnb[:, 24 * j:24 * (j + 1)]))
                v.drain()
            stats_j = s_ev.n

            @blk.scalar
            def _(sc):
                sc.wait_ge(s_ev.h, stats_j)
                sc.activation(rstd[:, j:j + 1], mvb[:, 2 * j + 1:2 * j + 2], AF.Ln, bias=epst[:])
                sc.drain()
                s_sl.inc(sc.activation(rstd[:, j:j + 1], rstd[:, j:j + 1], AF.Exp, scale=-0.5))
            rstd_j = s_sl.n

            @blk.vector
            def _(v):
                v.wait_ge(s_sl.h, rstd_j)
                v.drain()
                s_ev.inc(v.tensor_scalar(negb[:, j:j + 1], mvb[:, 2 * j:2 * j + 1],
                                         rstd[:, j:j + 1], -1.0,
                                         op0=ALU.mult, op1=ALU.mult))
            negb_j = s_ev.n

            @blk.scalar
            def _(sc):
                sc.wait_ge(s_ev.h, negb_j)
                sc.wait_ge(s_te.h, s_te.n)
                s_sl.inc(sc.activation(u_tm[:], xacc[:, j * D:(j + 1) * D], AF.Identity,
                                       scale=rstd[:, j:j + 1], bias=negb[:, j:j + 1]))
            u_j = s_sl.n

            te_base = s_te.n
            tp_base = s_tp.n

            @blk.tensor
            def _(t):
                t.wait_ge(s_sl.h, u_j)
                t.wait_ge(s_te.h, te_base)
                for dc in range(16):
                    bank = [pG, pH][dc % 2]
                    if dc >= 2:
                        t.wait_ge(s_te.h, te_base + dc - 1)
                    s_tp.inc(t.transpose(bank[:, 0:128], u_tm[:, dc * 128:(dc + 1) * 128],
                                         ident[:]))

            @blk.vector
            def _(v):
                for dc in range(16):
                    bank = [pG, pH][dc % 2]
                    v.wait_ge(s_tp.h, tp_base + dc + 1)
                    s_te.inc(v.tensor_copy(u_fm[:, dc * TP + j * 128:dc * TP + (j + 1) * 128],
                                           bank[:, 0:128]))

        cin_base = [0, 0]

        def emit_ffn_pass(fb, x_src, x_dst, p, first_pass, last_of_first=False,
                          cast_slice=()):
            w13, w2t, h_e, asl = fb["w13"], fb["w2t"], fb["h_e"], fb["asl"]
            t0 = p * TP

            bank_free.clear()
            with nc.Block() as blk:
                st_prev = s_st.n

                @blk.gpsimd
                def _(g):
                    g.wait_ge(s_st.h, st_prev)
                    for j in range(4):
                        s_x.inc(g.dma_start(xacc[:, j * D:(j + 1) * D],
                                            x_src[t0 + j * 128:t0 + (j + 1) * 128, :]), 16)
                xw = s_x.n
                ev0 = s_ev.n
                sl0 = s_sl.n
                te0 = s_te.n
                tp0 = s_tp.n
                mm0 = s_mm.n
                stats_j = lambda j: ev0 + 5 * j + 1
                negb_j = lambda j: ev0 + 5 * j + 2
                lgs_j = lambda j: ev0 + 5 * j + 3
                sm1n_j = lambda j: ev0 + 5 * j + 4
                rstd_j = lambda j: sl0 + 3 * j + 1
                apply_j = lambda j: sl0 + 3 * j + 2
                exp_j = lambda j: sl0 + 3 * j + 3
                logit_j = lambda j: mm0 + j + 1

                @blk.scalar
                def _(sc):
                    for j in range(4):
                        sc.wait_ge(s_ev.h, stats_j(j))
                        sc.activation(rstd[:, j:j + 1], mvb[:, 2 * j + 1:2 * j + 2], AF.Ln,
                                      bias=epst[:])
                        sc.drain()
                        s_sl.inc(sc.activation(rstd[:, j:j + 1], rstd[:, j:j + 1], AF.Exp,
                                               scale=-0.5))
                        sc.wait_ge(s_ev.h, negb_j(j))
                        sc.wait_ge(s_te.h, te0 + 16 * j)
                        s_sl.inc(sc.activation(u_tm[:], xacc[:, j * D:(j + 1) * D],
                                               AF.Identity, scale=rstd[:, j:j + 1],
                                               bias=negb[:, j:j + 1]))
                        sc.wait_ge(s_ev.h, sm1n_j(j))
                        s_sl.inc(sc.activation(smv[:], lgs[:], AF.Exp, bias=sm1n[:]))

                @blk.tensor
                def _(t):
                    for j in range(4):
                        t.wait_ge(s_sl.h, apply_j(j))
                        for dc in range(16):
                            bank = [pG, pH][dc % 2]
                            t.wait_ge(s_te.h, te0 + 16 * j + max(0, dc - 1))
                            s_tp.inc(t.transpose(bank[:, 0:128],
                                                 u_tm[:, dc * 128:(dc + 1) * 128], ident[:]))
                        if j > 0:
                            t.wait_ge(s_ev.h, lgs_j(j - 1))
                        t.wait_ge(s_te.h, te0 + 16 * (j + 1))
                        for dc in range(16):
                            mi = t.matmul(pE_[:, 0:E],
                                          u_fm[:, dc * TP + j * 128:dc * TP + (j + 1) * 128],
                                          gwT[:, dc * E:(dc + 1) * E],
                                          start=(dc == 0), stop=(dc == 15))
                        s_mm.inc(mi)

                @blk.vector
                def _(v):
                    v.wait_ge(s_x.h, xw)
                    for j in range(4):
                        for k in range(4):
                            v.bn_stats(bnb[:, (4 * j + k) * 6:(4 * j + k + 1) * 6],
                                       xacc[:, j * D + k * 512:j * D + (k + 1) * 512])
                        v.drain()
                        s_ev.inc(v.bn_aggr(mvb[:, 2 * j:2 * j + 2],
                                           bnb[:, 24 * j:24 * (j + 1)]))
                        v.drain()
                        v.wait_ge(s_sl.h, rstd_j(j))
                        v.drain()
                        s_ev.inc(v.tensor_scalar(negb[:, j:j + 1], mvb[:, 2 * j:2 * j + 1],
                                                 rstd[:, j:j + 1], -1.0,
                                                 op0=ALU.mult, op1=ALU.mult))
                        for dc in range(16):
                            bank = [pG, pH][dc % 2]
                            v.wait_ge(s_tp.h, tp0 + 16 * j + dc + 1)
                            s_te.inc(v.tensor_copy(
                                u_fm[:, dc * TP + j * 128:dc * TP + (j + 1) * 128],
                                bank[:, 0:128]))
                        v.wait_ge(s_mm.h, logit_j(j))
                        s_ev.inc(v.tensor_copy(lgs[:], pE_[:, 0:E]))
                        v.drain()
                        v.tensor_reduce(sm1[:], lgs[:], axis=AX.X, op=ALU.max)
                        v.drain()
                        s_ev.inc(v.tensor_scalar(sm1n[:], sm1[:], -1.0, None, op0=ALU.mult))
                        v.drain()
                        v.wait_ge(s_sl.h, exp_j(j))
                        v.tensor_reduce(sm1[:], smv[:], axis=AX.X, op=ALU.add)
                        v.drain()
                        v.reciprocal(sm2[:], sm1[:])
                        v.drain()
                        v.tensor_scalar(smv[:], smv[:], sm2[:], None, op0=ALU.mult)
                        v.drain()
                        v.tensor_tensor(lgs[:], smv[:], gbb[:], op=ALU.add)
                        v.drain()
                        v.tensor_reduce(sm1[:], lgs[:], axis=AX.X, op=ALU.max)
                        v.drain()
                        v.tensor_scalar(msk[:], lgs[:], sm1[:], None, op0=ALU.is_ge)
                        v.drain()
                        v.scalar_tensor_tensor(lg2[:], msk[:], NEG, lgs[:],
                                               op0=ALU.mult, op1=ALU.add)
                        v.drain()
                        v.tensor_reduce(sm1[:], lg2[:], axis=AX.X, op=ALU.max)
                        v.drain()
                        v.tensor_scalar(msk[:], lgs[:], sm1[:], None, op0=ALU.is_ge)
                        v.drain()
                        v.tensor_tensor(smv[:], smv[:], msk[:], op=ALU.mult)
                        v.drain()
                        s_ev.inc(v.tensor_scalar(dwt[:, j * E:(j + 1) * E], smv[:], 0.5,
                                                 None, op0=ALU.mult))

            bank_free.clear()
            with nc.Block() as blk:
                mmB = s_mm.n
                slB = s_sl.n
                evB = s_ev.n
                wlB = s_wl.n
                w2lB = s_w2l.n
                h1_mark = lambda e, ic: mmB + 32 * e + 2 * ic + 1
                h3_mark = lambda e, ic: mmB + 32 * e + 2 * ic + 2
                hend = lambda e: mmB + 32 * e + 16
                y_mark = lambda e, k: mmB + 32 * e + 16 + k + 1
                yend = lambda e: mmB + 32 * (e + 1)
                silu_mark = lambda e, ic: slB + 8 * e + ic + 1
                hmul_mark = lambda e, ic: evB + 24 * e + ic + 1
                hmul_end = lambda e: evB + 24 * e + 8
                yev_mark = lambda e, k: evB + 24 * e + 8 + k + 1
                yev_end = lambda e: evB + 24 * (e + 1)
                wl_mark = lambda e: wlB + 512 * (e + 1)
                w2l_mark = lambda e: w2lB + 128 * (e + 1)

                gate_p0c = first_pass

                @blk.sync
                def _(sy):
                    for e in range(9):
                        if e >= 1:
                            sy.wait_ge(s_mm.h, hend(e - 1))
                        for dc in range(16):
                            if gate_p0c and e == 0:
                                sy.wait_ge(s_p0c.h, s_p0c.n)
                            s_wl.inc(sy.dma_start(w13[:, dc * I:(dc + 1) * I],
                                                  w1ts[(e * 16 + dc) * 128:(e * 16 + dc + 1) * 128, :]), 16)
                        for dc in range(16):
                            if gate_p0c and e == 0:
                                sy.wait_ge(s_p0c.h, s_p0c.n)
                            s_wl.inc(sy.dma_start(w13[:, (16 + dc) * I:(17 + dc) * I],
                                                  w3ts[(e * 16 + dc) * 128:(e * 16 + dc + 1) * 128, :]), 16)
                        if e >= 1:
                            sy.wait_ge(s_mm.h, yend(e - 1))
                        for ic in range(8):
                            if gate_p0c and e == 0:
                                sy.wait_ge(s_p0c.h, s_p0c.n)
                            s_w2l.inc(sy.dma_start(w2t[:, ic * D:(ic + 1) * D],
                                                   w2ts[(e * 8 + ic) * 128:(e * 8 + ic + 1) * 128, :]), 16)

                @blk.tensor
                def _(t):
                    for e in range(9):
                        t.wait_ge(s_wl.h, wl_mark(e))
                        if e == 0:
                            t.wait_ge(s_ev.h, evB)
                        for ic in range(8):
                            pa = [pA, pC][ic % 2]
                            pb = [pB, pD][ic % 2]
                            if ic >= 2:
                                t.wait_ge(s_sl.h, silu_mark(e, ic - 2))
                                t.wait_ge(s_ev.h, hmul_mark(e, ic - 2))
                            elif e >= 1:
                                t.wait_ge(s_sl.h, silu_mark(e - 1, 7))
                                t.wait_ge(s_ev.h, yev_end(e - 1))
                            for dc in range(16):
                                mi = t.matmul(pa[:], w13[:, dc * I + ic * 128:dc * I + (ic + 1) * 128],
                                              u_fm[:, dc * TP:(dc + 1) * TP],
                                              start=(dc == 0), stop=(dc == 15))
                            s_mm.inc(mi)
                            for dc in range(16):
                                mi = t.matmul(pb[:], w13[:, (16 + dc) * I + ic * 128:(16 + dc) * I + (ic + 1) * 128],
                                              u_fm[:, dc * TP:(dc + 1) * TP],
                                              start=(dc == 0), stop=(dc == 15))
                            s_mm.inc(mi)
                        t.wait_ge(s_w2l.h, w2l_mark(e))
                        t.wait_ge(s_ev.h, hmul_end(e))
                        for k, (tb, d2s) in enumerate([(a, b) for a in range(4) for b in range(4)]):
                            py = [pE_, pF_][k % 2]
                            if k >= 2:
                                t.wait_ge(s_ev.h, yev_mark(e, k - 2))
                            elif e >= 1:
                                t.wait_ge(s_ev.h, yev_end(e - 1))
                            for ic in range(8):
                                mi = t.matmul(py[:],
                                              h_e[:, ic * TP + tb * 128:ic * TP + (tb + 1) * 128],
                                              w2t[:, ic * D + d2s * 512:ic * D + (d2s + 1) * 512],
                                              start=(ic == 0), stop=(ic == 7))
                            s_mm.inc(mi)

                @blk.scalar
                def _(sc):
                    for e in range(9):
                        for ic in range(8):
                            pa = [pA, pC][ic % 2]
                            sc.wait_ge(s_mm.h, h1_mark(e, ic))
                            if ic >= 2:
                                sc.wait_ge(s_ev.h, hmul_mark(e, ic - 2))
                            sc.wait_ge(s_ev.h, hmul_mark(e - 1, 7) if (ic < 2 and e >= 1) else 0)
                            s_sl.inc(sc.activation(asl[:, (ic % 2) * TP:(ic % 2 + 1) * TP],
                                                   pa[:], AF.Silu))

                @blk.vector
                def _(v):
                    for e in range(9):
                        for ic in range(8):
                            pb = [pB, pD][ic % 2]
                            v.wait_ge(s_sl.h, silu_mark(e, ic))
                            v.wait_ge(s_mm.h, h3_mark(e, ic))
                            s_ev.inc(v.tensor_tensor(h_e[:, ic * TP:(ic + 1) * TP],
                                                     asl[:, (ic % 2) * TP:(ic % 2 + 1) * TP],
                                                     pb[:], op=ALU.mult))
                        for k, (tb, d2s) in enumerate([(a, b) for a in range(4) for b in range(4)]):
                            py = [pE_, pF_][k % 2]
                            v.wait_ge(s_mm.h, y_mark(e, k))
                            dcol = dwS[:] if e == E else dwt[:, tb * E + e:tb * E + e + 1]
                            xs_ = xacc[:, tb * D + d2s * 512:tb * D + (d2s + 1) * 512]
                            s_ev.inc(v.scalar_tensor_tensor(xs_, py[:], dcol, xs_,
                                                            op0=ALU.mult, op1=ALU.add))

                fin = s_ev.n

                @blk.gpsimd
                def _(g):
                    g.wait_ge(s_ev.h, fin)
                    for j in range(4):
                        s_st.inc(g.dma_start(x_dst[t0 + j * 128:t0 + (j + 1) * 128, :],
                                             xacc[:, j * D:(j + 1) * D]), 16)
                    for t, (sa, da, rr, cc) in cast_slice:
                        cin = [s_cin, s_cinO][t % 2]
                        cout = [s_cout, s_coutO][t % 2]
                        if t >= 2:
                            g.wait_ge(cout.h, 16 * (t // 2))
                        cin.inc(g.dma_start(cbuf[t % 2][0:rr, 0:cc], sa), 16)

                if cast_slice:
                    @blk.sync
                    def _(sy):
                        for t, (sa, da, rr, cc) in cast_slice:
                            cin = [s_cin, s_cinO][t % 2]
                            cout = [s_cout, s_coutO][t % 2]
                            sy.wait_ge(cin.h, cin_base[t % 2] + 16 * (t // 2 + 1)),
                            cout.inc(sy.dma_start(da, cbuf[t % 2][0:rr, 0:cc]), 16)
                if last_of_first:
                    @blk.vector
                    def _(v):
                        v.wait_ge(s_cout.h, n_cast_out[0])
                        v.wait_ge(s_coutO.h, n_cast_out[1])

        aw_free = {}

        def run_ffn_phase(x_src, x_dst, first):
            fs = contextlib.ExitStack()
            fb = {
                "w13": sb("w13" + ("a" if first else "b"), [128, 32 * I], BF16, fs),
                "w2t": sb("w2t" + ("a" if first else "b"), [128, 8 * D], BF16, fs),
                "h_e": sb("h_e" + ("a" if first else "b"), [128, 8 * TP], BF16, fs),
                "asl": sb("asl" + ("a" if first else "b"), [128, 2 * TP], F32, fs),
            }
            cin_base[0] = s_cin.n
            cin_base[1] = s_cinO.n
            per = (len(cast_jobs) + NPASS - 1) // NPASS
            for p in range(NPASS):
                cs = (list(enumerate(cast_jobs))[p * per:(p + 1) * per] if first else ())
                emit_ffn_pass(fb, x_src, x_dst, p, first_pass=(first and p == 0),
                              last_of_first=(first and p == NPASS - 1), cast_slice=cs)
            fs.close()

        def emit_attention(x_src, x_dst):
            ats = contextlib.ExitStack()
            qk_fm = sb("qk_fm", [128, 2 * 16 * TP], BF16, ats)
            vk2 = sb("vk2", [128, 2 * 4 * D], BF16, ats)
            wring = sb("wring", [128, 2 * 16 * 512], BF16, ats)
            o_tm = sb("o_tm", [128, 4 * D], BF16, ats)
            wg1b = sb("wg1b", [128, 16 * 128], BF16, ats)
            wg2b = sb("wg2b", [128, D], BF16, ats)
            g1f = sb("g1f", [128, TP], BF16, ats)
            gsc = sb("gsc", [128, 512], F32, ats)
            u_tm2 = sb("u_tm2", [128, D], BF16, ats)

            def q_at(m, tj=None):
                c0 = m * TP
                return qk_fm[:, c0 + (0 if tj is None else tj * 128):
                             c0 + (TP if tj is None else (tj + 1) * 128)]

            def k_at(m, tj=None):
                c0 = 16 * TP + m * TP
                return qk_fm[:, c0 + (0 if tj is None else tj * 128):
                             c0 + (TP if tj is None else (tj + 1) * 128)]

            ogn_at = k_at

            def v_at(tj, m):
                return vk2[:, tj * D + m * 128:tj * D + (m + 1) * 128]

            def k2_at(tj, m):
                return vk2[:, 4 * D + tj * D + m * 128:4 * D + tj * D + (m + 1) * 128]

            ring_free = [(s_mm, 0), (s_mm, 0)]

            for sc_i in range(NPASS):
                t0 = sc_i * TP
                bank_free.clear()
                with nc.Block() as blk:
                    st_prev = s_st.n

                    @blk.gpsimd
                    def _(g):
                        g.wait_ge(s_st.h, st_prev)
                        for j in range(4):
                            s_x.inc(g.dma_start(xacc[:, j * D:(j + 1) * D],
                                                x_src[t0 + j * 128:t0 + (j + 1) * 128, :]), 16)
                    xw = s_x.n

                    if sc_i == 0:
                        @blk.sync
                        def _(sy):
                            sy.wait_ge(s_cout.h, s_cout.n)
                            sy.wait_ge(s_coutO.h, s_coutO.n)
                            for dc in range(16):
                                s_w2l.inc(sy.dma_start(wg1b[:, dc * 128:(dc + 1) * 128],
                                                      wg1s[dc * 128:(dc + 1) * 128, :]), 16)
                            s_w2l.inc(sy.dma_start(wg2b[:], wg2s), 16)
                    wg_done = s_w2l.n

                    for j in range(4):
                        emit_ln_tile(blk, j, xw)

                    vstage_marks = []
                    for pi, (wsrc, kind) in enumerate([(wqs, "q"), (wks, "k"), (wvs, "v")]):
                        for mb in range(4):
                            slot = (pi * 4 + mb) % 2

                            wr = [s_wr0, s_wr1][slot]

                            @blk.sync
                            def _(sy):
                                s, nmark = ring_free[slot]
                                sy.wait_ge(s.h, nmark)
                                for dc in range(16):
                                    wr.inc(sy.dma_start(
                                        wring[:, slot * 8192 + dc * 512:slot * 8192 + (dc + 1) * 512],
                                        wsrc[dc * 128:(dc + 1) * 128, mb * 512:(mb + 1) * 512]), 16)
                            wl_mark = wr.n
                            sl_base = s_sl.n

                            @blk.tensor
                            def _(t):
                                t.wait_ge(wr.h, wl_mark)
                                if pi == 0 and mb == 0:
                                    t.wait_ge(s_te.h, s_te.n)
                                for mc in range(4):
                                    m = mb * 4 + mc
                                    pp = [pA, pB, pC, pD][mc]
                                    wait_bank(t, ("proj", mc))
                                    for dc in range(16):
                                        mi = t.matmul(pp[:],
                                                      wring[:, slot * 8192 + dc * 512 + mc * 128:
                                                            slot * 8192 + dc * 512 + (mc + 1) * 128],
                                                      u_fm[:, dc * TP:(dc + 1) * TP],
                                                      start=(dc == 0), stop=(dc == 15))
                                    s_mm.inc(mi)
                            ring_free[slot] = (s_mm, s_mm.n)
                            mm_marks = [s_mm.n - 3, s_mm.n - 2, s_mm.n - 1, s_mm.n]

                            if kind != "v":
                                @blk.scalar
                                def _(sc):
                                    for mc in range(4):
                                        m = mb * 4 + mc
                                        pp = [pA, pB, pC, pD][mc]
                                        sc.wait_ge(s_mm.h, mm_marks[mc])
                                        dst = q_at(m) if kind == "q" else k_at(m)
                                        s_sl.inc(sc.activation(dst, pp[:], AF.Silu))
                                        set_bank(("proj", mc), s_sl)
                            else:
                                te_base = s_te.n
                                tp_base = s_tp.n
                                for mc in range(4):
                                    m = mb * 4 + mc
                                    pp = [pA, pB, pC, pD][mc]

                                    @blk.scalar
                                    def _(sc):
                                        sc.wait_ge(s_mm.h, mm_marks[mc])
                                        if mc >= 2:
                                            sc.wait_ge(s_te.h, te_base + 4 * mc - 4)
                                        else:
                                            sc.wait_ge(s_te.h, te_base)
                                        s_sl.inc(sc.activation(
                                            u_tm[:, (mc % 2) * 512:(mc % 2 + 1) * 512],
                                            pp[:], AF.Silu))
                                        set_bank(("proj", mc), s_sl)
                                    silu_mc = s_sl.n

                                    @blk.tensor
                                    def _(t):
                                        t.wait_ge(s_sl.h, silu_mc)
                                        t.wait_ge(s_te.h, te_base)
                                        for tj in range(4):
                                            i = mc * 4 + tj
                                            bank = [pG, pH][i % 2]
                                            if i >= 2:
                                                t.wait_ge(s_te.h, te_base + i - 1)
                                            s_tp.inc(t.transpose(
                                                bank[:, 0:128],
                                                u_tm[:, (mc % 2) * 512 + tj * 128:(mc % 2) * 512 + (tj + 1) * 128],
                                                ident[:]))

                                    @blk.vector
                                    def _(v):
                                        for tj in range(4):
                                            i = mc * 4 + tj
                                            bank = [pG, pH][i % 2]
                                            v.wait_ge(s_tp.h, tp_base + i + 1)
                                            s_te.inc(v.tensor_copy(v_at(tj, m), bank[:, 0:128]))

                    @blk.tensor
                    def _(t):
                        if sc_i == 0:
                            t.wait_ge(s_w2l.h, wg_done)
                        wait_bank(t, ("proj", 0))
                        for dc in range(16):
                            mi = t.matmul(pA[:], wg1b[:, dc * 128:(dc + 1) * 128],
                                          u_fm[:, dc * TP:(dc + 1) * TP],
                                          start=(dc == 0), stop=(dc == 15))
                        s_mm.inc(mi)
                    g1_mark = s_mm.n

                    @blk.vector
                    def _(v):
                        v.wait_ge(s_mm.h, g1_mark)
                        s_ev.inc(v.tensor_copy(g1f[:], pA[:]))
                        set_bank(("proj", 0), s_ev)
                    g1f_done = s_ev.n
                    v_copies_done = s_te.n
                    ksilu_done = s_sl.n

                    u2 = [u_tm, u_tm2]
                    k2e_mark = {}
                    Lmm = {}
                    scT = {}
                    evT = {}
                    Tend = {}
                    Tcend = {}

                    def emit_k2(tj):
                        tpb = s_tp.n
                        slb = s_sl.n

                        @blk.tensor
                        def _(t):
                            if tj == 0:
                                t.wait_ge(s_te.h, v_copies_done)
                                t.wait_ge(s_sl.h, ksilu_done)
                            for h in range(H):
                                if h >= 2:
                                    t.wait_ge(s_sl.h, slb + h - 1)
                                else:
                                    wait_bank(t, ("T", h % 2))
                                s_tp.inc(t.transpose([pG, pH][h % 2][:, 0:128],
                                                     k_at(h, tj), ident[:]))

                        @blk.scalar
                        def _(sc):
                            for h in range(H):
                                sc.wait_ge(s_tp.h, tpb + h + 1)
                                s_sl.inc(sc.activation(k2_at(tj, h), [pG, pH][h % 2][:, 0:128],
                                                       AF.Identity, scale=kdec[:, h:h + 1]))
                                set_bank(("T", h % 2), s_sl)
                        k2e_mark[tj] = s_sl.n

                    def emit_L(tj):
                        mmT = s_mm.n
                        evTj = s_ev.n
                        scTj = s_sl.n
                        Lmm[tj] = mmT
                        evT[tj] = evTj
                        scT[tj] = scTj

                        @blk.tensor
                        def _(t):
                            t.wait_ge(s_sl.h, k2e_mark[tj])
                            if tj == 0:
                                t.wait_ge(s_ev.h, g1f_done)
                            else:
                                t.wait_ge(s_ev.h, evTj)
                            for h in range(H):
                                if h >= 1:
                                    t.wait_ge(s_ev.h, evTj + 4 * (h - 1) + 1)
                                mi = t.matmul(pA[:, 0:128], k_at(h, tj), q_at(h, tj),
                                              start=True, stop=True)
                                s_mm.inc(mi)
                                if h >= 1:
                                    t.wait_ge(s_sl.h, scTj + h)
                                mi = t.matmul(pB[:, 0:128], q_at(h, tj),
                                              Sbf[:, h * HD:(h + 1) * HD],
                                              start=True, stop=True)
                                s_mm.inc(mi)
                                t.wait_ge(s_ev.h, evTj + 4 * h + 1)
                                mi = t.matmul(pC[:, 0:128], sTbf[:, (h % 2) * 128:(h % 2 + 1) * 128],
                                              v_at(tj, h), start=True, stop=True)
                                s_mm.inc(mi)
                                if h >= 1:
                                    t.wait_ge(s_ev.h, evTj + 4 * (h - 1) + 3)
                                mi = t.matmul(pD[:, 0:128], k2_at(tj, h), v_at(tj, h),
                                              start=True, stop=True)
                                s_mm.inc(mi)
                            for ns in range(4):
                                if tj == 0 and ns == 0:
                                    t.wait_ge(s_ev.h, g1f_done)
                                if ns >= 2:
                                    t.wait_ge(s_ev.h, evTj + 64 + (ns - 2) + 1)
                                elif tj > 0:
                                    t.wait_ge(s_ev.h, evT[tj - 1] + 64 + 4)
                                else:
                                    wait_bank(t, ("gate", ns % 2))
                                s_mm.inc(t.matmul([pE_, pF_][ns % 2][:], g1f[:, tj * 128:(tj + 1) * 128],
                                                  wg2b[:, ns * 512:(ns + 1) * 512],
                                                  start=True, stop=True))

                    def emit_sc(tj):
                        scTj = scT[tj]
                        mmT = Lmm[tj]
                        evTj = evT[tj]

                        @blk.scalar
                        def _(sc):
                            for h in range(H):
                                sc.wait_ge(s_mm.h, mmT + 4 * h + 2)
                                if h >= 2:
                                    sc.wait_ge(s_ev.h, evTj + 4 * (h - 2) + 2)
                                s_sl.inc(sc.activation(o_tmp[:, (h % 2) * 128:(h % 2 + 1) * 128],
                                                       pB[:, 0:128], AF.Identity,
                                                       scale=qdec[:, h:h + 1]))
                            for ns in range(4):
                                sc.wait_ge(s_mm.h, mmT + 64 + ns + 1)
                                if ns >= 1:
                                    sc.wait_ge(s_ev.h, evTj + 64 + ns)
                                s_sl.inc(sc.activation(gsc[:], [pE_, pF_][ns % 2][:],
                                                       AF.Tanh, scale=0.5))
                            sc.wait_ge(s_ev.h, evTj + 64 + 4)
                            if tj >= 2:
                                sc.wait_ge(s_tp.h, s_tp.n + 16)
                            sc.drain()
                            for h in range(H):
                                sc.activation(u2[tj % 2][:, h * 128:(h + 1) * 128],
                                              o_tm[:, tj * D + h * 128:tj * D + (h + 1) * 128],
                                              AF.Square,
                                              accum_out=msq[:, tj * H + h:tj * H + h + 1])
                            sc.drain()
                            sc.activation(rsg[:, tj * H:(tj + 1) * H], msq[:, tj * H:(tj + 1) * H],
                                          AF.Ln, scale=1.0 / HD, bias=epst[:])
                            sc.drain()
                            s_sl.inc(sc.activation(rsg[:, tj * H:(tj + 1) * H],
                                                   rsg[:, tj * H:(tj + 1) * H], AF.Exp, scale=-0.5))
                            sc.drain()
                            for h in range(H):
                                mi = sc.activation(u2[tj % 2][:, h * 128:(h + 1) * 128],
                                                   o_tm[:, tj * D + h * 128:tj * D + (h + 1) * 128],
                                                   AF.Identity, scale=rsg[:, tj * H + h:tj * H + h + 1])
                            s_sl.inc(mi)

                    def emit_vec(tj):
                        scTj = scT[tj]
                        mmT = Lmm[tj]

                        @blk.vector
                        def _(v):
                            for h in range(H):
                                v.wait_ge(s_mm.h, mmT + 4 * h + 1)
                                s_ev.inc(v.tensor_tensor(sTbf[:, (h % 2) * 128:(h % 2 + 1) * 128],
                                                         pA[:, 0:128],
                                                         dt16[:, h * 128:(h + 1) * 128],
                                                         op=ALU.mult))
                                v.wait_ge(s_mm.h, mmT + 4 * h + 3)
                                v.wait_ge(s_sl.h, scTj + h + 1)
                                s_ev.inc(v.tensor_tensor(o_tm[:, tj * D + h * 128:tj * D + (h + 1) * 128],
                                                         pC[:, 0:128],
                                                         o_tmp[:, (h % 2) * 128:(h % 2 + 1) * 128],
                                                         op=ALU.add))
                                v.wait_ge(s_mm.h, mmT + 4 * h + 4)
                                s_ev.inc(v.scalar_tensor_tensor(Sst[:, h * HD:(h + 1) * HD],
                                                                Sst[:, h * HD:(h + 1) * HD],
                                                                cdec[:, h:h + 1],
                                                                pD[:, 0:128],
                                                                op0=ALU.mult, op1=ALU.add))
                                v.drain()
                                s_ev.inc(v.tensor_copy(Sbf[:, h * HD:(h + 1) * HD],
                                                       Sst[:, h * HD:(h + 1) * HD]))
                            for ns in range(4):
                                v.wait_ge(s_sl.h, scTj + 16 + ns + 1)
                                v.tensor_scalar(gsc[:], gsc[:], 0.5, 0.5,
                                                op0=ALU.mult, op1=ALU.add)
                                v.drain()
                                s_ev.inc(v.tensor_tensor(
                                    o_tm[:, tj * D + ns * 512:tj * D + (ns + 1) * 512],
                                    o_tm[:, tj * D + ns * 512:tj * D + (ns + 1) * 512],
                                    gsc[:], op=ALU.mult))

                    def emit_vecB(tj):
                        scTj = scT[tj]

                        @blk.vector
                        def _(v):
                            v.wait_ge(s_sl.h, scTj + 22)
                            s_ev.inc(v.tensor_tensor(u2[tj % 2][:], u2[tj % 2][:], nwb[:],
                                                     op=ALU.mult))

                    def emit_T(tj):
                        tpb = s_tp.n

                        teb = s_te.n

                        @blk.tensor
                        def _(t):
                            t.wait_ge(s_ev.h, evT[tj] + 69)
                            for dc in range(16):
                                if dc >= 2:
                                    t.wait_ge(s_te.h, teb + dc - 1)
                                else:
                                    wait_bank(t, ("T", dc % 2))
                                s_tp.inc(t.transpose([pG, pH][dc % 2][:, 0:128],
                                                     u2[tj % 2][:, dc * 128:(dc + 1) * 128],
                                                     ident[:]))
                        Tend[tj] = s_tp.n

                        @blk.vector
                        def _(v):
                            for dc in range(16):
                                v.wait_ge(s_tp.h, tpb + dc + 1)
                                s_te.inc(v.tensor_copy(ogn_at(dc, tj), [pG, pH][dc % 2][:, 0:128]))
                                set_bank(("T", dc % 2), s_te)
                        Tcend[tj] = s_te.n

                    emit_k2(0)
                    emit_k2(1)
                    emit_L(0); emit_sc(0); emit_vec(0); emit_vecB(0)
                    emit_k2(2)
                    emit_L(1); emit_sc(1); emit_vec(1); emit_vecB(1)
                    emit_k2(3)
                    emit_L(2); emit_sc(2); emit_vec(2)
                    emit_T(0)
                    emit_vecB(2)
                    emit_L(3); emit_sc(3); emit_vec(3)
                    emit_T(1)
                    emit_vecB(3)
                    emit_T(2)
                    emit_T(3)

                    for ns in range(4):
                        slot = ns % 2
                        wr = [s_wr0, s_wr1][slot]

                        @blk.sync
                        def _(sy):
                            s, nmark = ring_free[slot]
                            sy.wait_ge(s.h, nmark)
                            for dc in range(16):
                                wr.inc(sy.dma_start(
                                    wring[:, slot * 8192 + dc * 512:slot * 8192 + (dc + 1) * 512],
                                    wos[dc * 128:(dc + 1) * 128, ns * 512:(ns + 1) * 512]), 16)
                        wl_mark = wr.n

                        for tj in range(4):
                            k = ns * 4 + tj
                            pp = [pE_, pF_][k % 2]

                            @blk.tensor
                            def _(t):
                                if tj == 0:
                                    t.wait_ge(wr.h, wl_mark)
                                if ns == 0 and tj == 0:
                                    t.wait_ge(s_te.h, s_te.n)
                                wait_bank(t, ("gate", k % 2))
                                for dc in range(16):
                                    mi = t.matmul(pp[:], ogn_at(dc, tj),
                                                  wring[:, slot * 8192 + dc * 512:slot * 8192 + (dc + 1) * 512],
                                                  start=(dc == 0), stop=(dc == 15))
                                s_mm.inc(mi)
                            omm = s_mm.n

                            @blk.vector
                            def _(v):
                                v.wait_ge(s_mm.h, omm)
                                xs_ = xacc[:, tj * D + ns * 512:tj * D + (ns + 1) * 512]
                                s_ev.inc(v.tensor_tensor(xs_, xs_, pp[:], op=ALU.add))
                                set_bank(("gate", k % 2), s_ev)
                        ring_free[slot] = (s_mm, s_mm.n)

                    fin = s_ev.n

                    @blk.gpsimd
                    def _(g):
                        g.wait_ge(s_ev.h, fin)
                        for j in range(4):
                            s_st.inc(g.dma_start(x_dst[t0 + j * 128:t0 + (j + 1) * 128, :],
                                                 xacc[:, j * D:(j + 1) * D]), 16)
            ats.close()

        run_ffn_phase(x_in, x1s if STAGE > 1 else y_out, True)
        p0s.close()

        if STAGE > 1:
            emit_attention(x1s, x2s if STAGE > 2 else y_out)

        if STAGE > 2:
            run_ffn_phase(x2s, y_out, False)

        with nc.Block() as blk:
            @blk.gpsimd
            def _(g):
                g.wait_ge(s_st.h, s_st.n)

    return nc


def _consts():
    identf = np.eye(128, dtype=np.float32)
    import ml_dtypes
    ident = identf.astype(ml_dtypes.bfloat16)
    ones = np.ones((1, 128), dtype=np.float32)
    jj = np.arange(128, dtype=np.float32)
    diff = jj[None, :] - jj[:, None]
    difft = np.where(diff >= 0, diff, 1e9).astype(np.float32)
    cj1 = (jj + 1.0).reshape(128, 1).astype(np.float32)
    cj127 = (127.0 - jj).reshape(128, 1).astype(np.float32)
    return ident, identf, ones, difft, cj1, cj127


def kernel(L0_attn_x, L0_attn_log_decay, L0_attn_Wq, L0_attn_Wk, L0_attn_Wv,
           L0_attn_Wo, L0_attn_Wg1, L0_attn_Wg2, L0_attn_norm_w,
           L0_ffn_gate_w, L0_ffn_gate_b, L0_ffn_W1, L0_ffn_W2, L0_ffn_W3,
           L0_ffn_Ws1, L0_ffn_Ws2, L0_ffn_Ws3):
    ident, identf, ones, difft, cj1, cj127 = _consts()
    x = np.ascontiguousarray(np.asarray(L0_attn_x, dtype=np.float32))
    shared = {
        "ld_in": np.asarray(L0_attn_log_decay, dtype=np.float32).reshape(1, H),
        "wq_in": np.ascontiguousarray(L0_attn_Wq, dtype=np.float32),
        "wk_in": np.ascontiguousarray(L0_attn_Wk, dtype=np.float32),
        "wv_in": np.ascontiguousarray(L0_attn_Wv, dtype=np.float32),
        "wo_in": np.ascontiguousarray(L0_attn_Wo, dtype=np.float32),
        "wg1_in": np.ascontiguousarray(L0_attn_Wg1, dtype=np.float32),
        "wg2_in": np.ascontiguousarray(L0_attn_Wg2, dtype=np.float32),
        "nw_in": np.asarray(L0_attn_norm_w, dtype=np.float32).reshape(1, D),
        "gw_in": np.ascontiguousarray(L0_ffn_gate_w, dtype=np.float32),
        "gb_in": np.asarray(L0_ffn_gate_b, dtype=np.float32).reshape(1, E),
        "w1_in": np.ascontiguousarray(L0_ffn_W1, dtype=np.float32).reshape(E * I, D),
        "w2_in": np.ascontiguousarray(L0_ffn_W2, dtype=np.float32).reshape(E * D, I),
        "w3_in": np.ascontiguousarray(L0_ffn_W3, dtype=np.float32).reshape(E * I, D),
        "ws1_in": np.ascontiguousarray(L0_ffn_Ws1, dtype=np.float32),
        "ws2_in": np.ascontiguousarray(L0_ffn_Ws2, dtype=np.float32),
        "ws3_in": np.ascontiguousarray(L0_ffn_Ws3, dtype=np.float32),
        "ident_in": ident, "identf_in": identf, "ones_in": ones, "difft_in": difft,
        "cj1_in": cj1, "cj127_in": cj127,
    }
    nc = build()
    in_maps = [dict(shared, x_in=np.ascontiguousarray(x[b])) for b in range(8)]
    res = run_bass_kernel_spmd(nc, in_maps, core_ids=list(range(8)))
    out = np.stack([np.asarray(r["y_out"], dtype=np.float32) for r in res.results], axis=0)
    return out


# version 07
# speedup vs baseline: 1.1301x; 1.1301x over previous
import contextlib
import numpy as np
import concourse.bass as bass
import concourse.mybir as mybir
from concourse.bass_utils import run_bass_kernel_spmd

F32 = mybir.dt.float32
BF16 = mybir.dt.bfloat16
AF = mybir.ActivationFunctionType
ALU = mybir.AluOpType
AX = mybir.AxisListType

N = 4096
D = 2048
H = 16
HD = 128
E = 8
I = 1024
TP = 512
NPASS = N // TP
EPS = 1e-5
NEG = -1e30
STAGE = 3


class Sem:
    def __init__(self, handle):
        self.h = handle
        self.n = 0

    def inc(self, instr, k=1):
        instr.then_inc(self.h, k)
        self.n += k
        return self.n

    def wait(self, eng, n=None):
        eng.wait_ge(self.h, self.n if n is None else n)


def build():
    nc = bass.Bass("TRN2", target_bir_lowering=False, debug=False)

    x_in = nc.dram_tensor("x_in", [N, D], F32, kind="ExternalInput").ap()
    ld_in = nc.dram_tensor("ld_in", [1, H], F32, kind="ExternalInput").ap()
    wq_in = nc.dram_tensor("wq_in", [D, D], F32, kind="ExternalInput").ap()
    wk_in = nc.dram_tensor("wk_in", [D, D], F32, kind="ExternalInput").ap()
    wv_in = nc.dram_tensor("wv_in", [D, D], F32, kind="ExternalInput").ap()
    wo_in = nc.dram_tensor("wo_in", [D, D], F32, kind="ExternalInput").ap()
    wg1_in = nc.dram_tensor("wg1_in", [D, HD], F32, kind="ExternalInput").ap()
    wg2_in = nc.dram_tensor("wg2_in", [HD, D], F32, kind="ExternalInput").ap()
    nw_in = nc.dram_tensor("nw_in", [1, D], F32, kind="ExternalInput").ap()
    gw_in = nc.dram_tensor("gw_in", [E, D], F32, kind="ExternalInput").ap()
    gb_in = nc.dram_tensor("gb_in", [1, E], F32, kind="ExternalInput").ap()
    w1_in = nc.dram_tensor("w1_in", [E * I, D], F32, kind="ExternalInput").ap()
    w2_in = nc.dram_tensor("w2_in", [E * D, I], F32, kind="ExternalInput").ap()
    w3_in = nc.dram_tensor("w3_in", [E * I, D], F32, kind="ExternalInput").ap()
    ws1_in = nc.dram_tensor("ws1_in", [I, D], F32, kind="ExternalInput").ap()
    ws2_in = nc.dram_tensor("ws2_in", [D, I], F32, kind="ExternalInput").ap()
    ws3_in = nc.dram_tensor("ws3_in", [I, D], F32, kind="ExternalInput").ap()
    ident_in = nc.dram_tensor("ident_in", [128, 128], BF16, kind="ExternalInput").ap()
    ones_in = nc.dram_tensor("ones_in", [1, 128], F32, kind="ExternalInput").ap()
    difft_in = nc.dram_tensor("difft_in", [128, 128], F32, kind="ExternalInput").ap()
    cj1_in = nc.dram_tensor("cj1_in", [128, 1], F32, kind="ExternalInput").ap()
    cj127_in = nc.dram_tensor("cj127_in", [128, 1], F32, kind="ExternalInput").ap()

    y_out = nc.dram_tensor("y_out", [N, D], F32, kind="ExternalOutput").ap()

    x1s = nc.dram_tensor("x1s", [N, D], F32, kind="Internal").ap()
    x2s = nc.dram_tensor("x2s", [N, D], F32, kind="Internal").ap()
    w1ts = nc.dram_tensor("w1ts", [9 * D, I], BF16, kind="Internal").ap()
    w3ts = nc.dram_tensor("w3ts", [9 * D, I], BF16, kind="Internal").ap()
    w2ts = nc.dram_tensor("w2ts", [9 * I, D], BF16, kind="Internal").ap()
    gws = nc.dram_tensor("gws", [E, D], BF16, kind="Internal").ap()
    wqs = nc.dram_tensor("wqs", [D, D], BF16, kind="Internal").ap()
    wks = nc.dram_tensor("wks", [D, D], BF16, kind="Internal").ap()
    wvs = nc.dram_tensor("wvs", [D, D], BF16, kind="Internal").ap()
    wos = nc.dram_tensor("wos", [D, D], BF16, kind="Internal").ap()
    wg1s = nc.dram_tensor("wg1s", [D, HD], BF16, kind="Internal").ap()
    wg2s = nc.dram_tensor("wg2s", [HD, D], BF16, kind="Internal").ap()

    es = contextlib.ExitStack()
    with es:
        def sb(name, shape, dt=F32, stack=None):
            return (stack or es).enter_context(nc.sbuf_tensor(name, shape, dt))

        def psum(name, shape, dt=F32):
            return es.enter_context(nc.psum_tensor(name, shape, dt))

        def sem(name):
            return Sem(es.enter_context(nc.semaphore(name)))

        ident = sb("ident", [128, 128], BF16)
        onesf = sb("onesf", [1, 128], F32)
        epst = sb("epst", [128, 1], F32)
        dwS = sb("dwS", [128, 1], F32)
        gbb = sb("gbb", [128, E], F32)
        gwT = sb("gwT", [128, 16 * E], BF16)
        nwb = sb("nwb", [128, D], BF16)
        ldb = sb("ldb", [128, H], F32)
        dt16 = sb("dt16", [128, H * 128], BF16)
        qdec = sb("qdec", [128, H], F32)
        kdec = sb("kdec", [128, H], F32)
        cdec = sb("cdec", [128, H], F32)
        xacc = sb("xacc", [128, 4 * D], F32)
        u_fm = sb("u_fm", [128, 16 * TP], BF16)
        u_tm = sb("u_tm", [128, D], BF16)
        dwt = sb("dwt", [128, 4 * E], F32)
        lgs = sb("lgs", [128, E], F32)
        msk = sb("msk", [128, E], F32)
        lg2 = sb("lg2", [128, E], F32)
        smv = sb("smv", [128, E], F32)
        sm1 = sb("sm1", [128, 1], F32)
        sm1n = sb("sm1n", [128, 1], F32)
        sm2 = sb("sm2", [128, 1], F32)
        bnb = sb("bnb", [128, 24], F32)
        mvb = sb("mvb", [128, 2], F32)
        rstd = sb("rstd", [128, 1], F32)
        negb = sb("negb", [128, 1], F32)
        sTbf = sb("sTbf", [128, 2 * 128], BF16)
        o_tmp = sb("o_tmp", [128, 2 * 128], F32)
        msq = sb("msq", [128, 4 * H], F32)
        rsg = sb("rsg", [128, 4 * H], F32)
        sqd = sb("sqd", [128, 128], F32)

        pA = psum("pA", [128, 512], F32)
        pB = psum("pB", [128, 512], F32)
        pC = psum("pC", [128, 512], F32)
        pD = psum("pD", [128, 512], F32)
        pE_ = psum("pE_", [128, 512], F32)
        pF_ = psum("pF_", [128, 512], F32)
        pG = psum("pG", [128, 1024], BF16)
        pH = psum("pH", [128, 1024], BF16)

        s_cin = sem("s_cin")
        s_cinO = sem("s_cinO")
        s_cout = sem("s_cout")
        s_coutO = sem("s_coutO")
        s_x = sem("s_x")
        s_wl = sem("s_wl")
        s_w2l = sem("s_w2l")
        s_mm = sem("s_mm")
        s_ev = sem("s_ev")
        s_sl = sem("s_sl")
        s_st = sem("s_st")
        s_tp = sem("s_tp")
        s_te = sem("s_te")
        s_wr0 = sem("s_wr0")
        s_wr1 = sem("s_wr1")
        s_p0c = sem("s_p0c")

        bank_free = {}

        def wait_bank(eng, bank):
            if bank in bank_free:
                s, n = bank_free[bank]
                eng.wait_ge(s.h, n)

        def set_bank(bank, s):
            bank_free[bank] = (s, s.n)

        p0 = contextlib.ExitStack()
        cbuf = [sb("cbuf0", [128, 2048], BF16, p0), sb("cbuf1", [128, 2048], BF16, p0)]
        cast_jobs = []
        for csrc, cdst, rows, cols in [
            (gw_in, gws, E, D),
            (wq_in, wqs, D, D), (wk_in, wks, D, D), (wv_in, wvs, D, D),
            (wg1_in, wg1s, D, HD), (wg2_in, wg2s, HD, D), (wo_in, wos, D, D),
        ]:
            if rows < 128:
                cast_jobs.append((csrc[0:rows, :], cdst[0:rows, :], rows, cols))
            else:
                for r in range(0, rows, 128):
                    cast_jobs.append((csrc[r:r + 128, :], cdst[r:r + 128, :], 128, cols))

        n_cast_out = [0, 0]
        with nc.Block() as blk:
            @blk.gpsimd
            def _(g):
                for t, (sa, da, rr, cc) in enumerate(cast_jobs):
                    cin = [s_cin, s_cinO][t % 2]
                    cout = [s_cout, s_coutO][t % 2]
                    if t >= 2:
                        g.wait_ge(cout.h, 16 * (t // 2))
                    cin.inc(g.dma_start(cbuf[t % 2][0:rr, 0:cc], sa), 16)

            @blk.sync
            def _(sy):
                for t, (sa, da, rr, cc) in enumerate(cast_jobs):
                    cin = [s_cin, s_cinO][t % 2]
                    cout = [s_cout, s_coutO][t % 2]
                    sy.wait_ge(cin.h, 16 * (t // 2 + 1))
                    cout.inc(sy.dma_start(da, cbuf[t % 2][0:rr, 0:cc]), 16)
                    n_cast_out[t % 2] += 16

            @blk.vector
            def _(v):
                v.wait_ge(s_cout.h, n_cast_out[0])
                v.wait_ge(s_coutO.h, n_cast_out[1])
        p0.close()

        su = contextlib.ExitStack()
        gwsb = sb("gwsb", [E, D], BF16, su)
        ldrow = sb("ldrow", [1, H], F32, su)
        nwrow = sb("nwrow", [1, D], F32, su)
        gbrow = sb("gbrow", [1, E], F32, su)
        difft = sb("difft", [128, 128], F32, su)
        cj1 = sb("cj1", [128, 1], F32, su)
        cj127 = sb("cj127", [128, 1], F32, su)

        with nc.Block() as blk:
            @blk.gpsimd
            def _(g):
                g.wait_ge(s_cout.h, n_cast_out[0])
                g.wait_ge(s_coutO.h, n_cast_out[1])
                for dst, srca in [(ident, ident_in), (onesf, ones_in), (ldrow, ld_in),
                                  (nwrow, nw_in), (gbrow, gb_in), (difft, difft_in),
                                  (cj1, cj1_in), (cj127, cj127_in), (gwsb, gws)]:
                    s_x.inc(g.dma_start(dst[:], srca), 16)
                g.memset(epst[:], EPS)
                g.memset(dwS[:], 0.5)

            @blk.vector
            def _(v):
                s_x.wait(v)

        gw_te = 0
        with nc.Block() as blk:
            gw_te = s_te.n
            gw_tp = s_tp.n
            @blk.tensor
            def _(t):
                s_mm.inc(t.matmul(pA[:, 0:E], onesf[:], gbrow[:], start=True, stop=True))
                s_mm.inc(t.matmul(pA[:, 16:16 + H], onesf[:], ldrow[:], start=True, stop=True))
                for i in range(4):
                    s_mm.inc(t.matmul([pB, pC, pD, pE_][i][:], onesf[:],
                                      nwrow[:, i * 512:(i + 1) * 512], start=True, stop=True))
                t.wait_ge(s_te.h, gw_te)
                for dc in range(16):
                    dstp = [pG, pH][dc % 2][:, 0:8]
                    if dc >= 2:
                        t.wait_ge(s_te.h, gw_te + dc - 1)
                    s_tp.inc(t.transpose(dstp, gwsb[0:E, dc * 128:(dc + 1) * 128], ident[0:E, 0:E]))

            @blk.vector
            def _(v):
                s_mm.wait(v)
                v.tensor_copy(gbb[:], pA[:, 0:E])
                v.tensor_copy(ldb[:], pA[:, 16:16 + H])
                for i in range(4):
                    ni = v.tensor_copy(nwb[:, i * 512:(i + 1) * 512], [pB, pC, pD, pE_][i][:])
                s_ev.inc(ni)
                for dc in range(16):
                    srcp = [pG, pH][dc % 2][:, 0:8]
                    v.wait_ge(s_tp.h, gw_tp + dc + 1)
                    ci = v.tensor_copy(gwT[:, dc * E:(dc + 1) * E], srcp)
                    s_te.inc(ci)

            @blk.scalar
            def _(sc):
                s_te.wait(sc)
                s_ev.wait(sc)
                for h in range(H):
                    sc.activation(dt16[:, h * 128:(h + 1) * 128], difft[:], AF.Exp,
                                  scale=ldb[:, h:h + 1])
                    sc.activation(qdec[:, h:h + 1], cj1[:], AF.Exp, scale=ldb[:, h:h + 1])
                    sc.activation(kdec[:, h:h + 1], cj127[:], AF.Exp, scale=ldb[:, h:h + 1])
                    sc.activation(cdec[:, h:h + 1], ldb[:, h:h + 1], AF.Exp, scale=float(HD))
        su.close()

        p0c_mark = {}
        chunk_tp_done = {}
        asm_free = [0]
        p0c = contextlib.ExitStack()
        natb = [sb("natb0", [128, D], BF16, p0c), sb("natb1", [128, D], BF16, p0c)]
        asmb = sb("asmb", [128, 16 * I], BF16, p0c)
        p0c_cnt = [0, 0]
        with nc.Block() as blk:
            groups = []
            for e in range(9):
                w1src = w1_in[e * I:(e + 1) * I, :] if e < E else ws1_in
                w3src = w3_in[e * I:(e + 1) * I, :] if e < E else ws3_in
                w2src = w2_in[e * D:(e + 1) * D, :] if e < E else ws2_in
                groups.append((e, "w1", w1src, 8, D, 16, I, w1ts, e * 16))
                groups.append((e, "w3", w3src, 8, D, 16, I, w3ts, e * 16))
                groups.append((e, "w2", w2src, 16, I, 8, D, w2ts, e * 8))
            tp0 = s_tp.n
            cum = tp0
            for (_, _, _, nch_, _, och_, _, _, _) in groups:
                for c in range(nch_):
                    chunk_tp_done[len(chunk_tp_done)] = cum + och_ * (c + 1)
                cum += nch_ * och_
            for gi, (e, nm, fsrc, nch, ncol, och, ocol, tdst, orow0) in enumerate(groups):
                tp_base = s_tp.n
                te_base = s_te.n
                ld_marks = []

                @blk.gpsimd
                def _(g):
                    for c in range(nch):
                        t_idx = sum(x[3] for x in groups[:gi]) + c
                        if t_idx >= 2:
                            g.wait_ge(s_tp.h, chunk_tp_done[t_idx - 2])
                        cin = [s_cin, s_cinO][t_idx % 2]
                        cin.inc(g.dma_start(natb[t_idx % 2][:, 0:ncol],
                                            fsrc[c * 128:(c + 1) * 128, :]), 16)
                        ld_marks.append((cin, cin.n))

                @blk.tensor
                def _(t):
                    t.wait_ge(s_te.h, te_base)
                    for c in range(nch):
                        t_idx = sum(x[3] for x in groups[:gi]) + c
                        cs, cn = ld_marks[c]
                        t.wait_ge(cs.h, cn)
                        for k in range(och):
                            i = c * och + k
                            bank = [pG, pH][i % 2]
                            if i >= 2:
                                t.wait_ge(s_te.h, te_base + i - 1)
                            s_tp.inc(t.transpose(bank[:, 0:128],
                                                 natb[t_idx % 2][:, k * 128:(k + 1) * 128],
                                                 ident[:]))

                @blk.vector
                def _(v):
                    v.wait_ge(s_p0c.h, asm_free[0])
                    for c in range(nch):
                        for k in range(och):
                            i = c * och + k
                            bank = [pG, pH][i % 2]
                            v.wait_ge(s_tp.h, tp_base + i + 1)
                            s_te.inc(v.tensor_copy(
                                asmb[:, k * ocol + c * 128:k * ocol + (c + 1) * 128],
                                bank[:, 0:128]))
                ev_end = s_te.n

                @blk.sync
                def _(sy):
                    sy.wait_ge(s_te.h, ev_end)
                    sy.wait_ge(s_p0c.h, s_p0c.n)
                    for k in range(och):
                        s_p0c.inc(sy.dma_start(
                            tdst[(orow0 + k) * 128:(orow0 + k + 1) * 128, :],
                            asmb[:, k * ocol:(k + 1) * ocol]), 16)
                asm_free[0] = s_p0c.n
                if nm == "w2":
                    p0c_mark[e] = s_p0c.n
        p0c.close()

        def emit_ln_tile(blk, j, xw_done):
            @blk.vector
            def _(v):
                v.wait_ge(s_x.h, xw_done)
                if j > 0:
                    v.wait_ge(s_sl.h, s_sl.n)
                for k in range(4):
                    v.bn_stats(bnb[:, k * 6:(k + 1) * 6],
                               xacc[:, j * D + k * 512:j * D + (k + 1) * 512])
                v.drain()
                s_ev.inc(v.bn_aggr(mvb[:], bnb[:]))
                v.drain()
            stats_j = s_ev.n

            @blk.scalar
            def _(sc):
                sc.wait_ge(s_ev.h, stats_j)
                sc.activation(rstd[:], mvb[:, 1:2], AF.Ln, bias=epst[:])
                sc.drain()
                s_sl.inc(sc.activation(rstd[:], rstd[:], AF.Exp, scale=-0.5))
            rstd_j = s_sl.n

            @blk.vector
            def _(v):
                v.wait_ge(s_sl.h, rstd_j)
                v.drain()
                s_ev.inc(v.tensor_scalar(negb[:], mvb[:, 0:1], rstd[:], -1.0,
                                         op0=ALU.mult, op1=ALU.mult))
            negb_j = s_ev.n

            @blk.scalar
            def _(sc):
                sc.wait_ge(s_ev.h, negb_j)
                sc.wait_ge(s_te.h, s_te.n)
                s_sl.inc(sc.activation(u_tm[:], xacc[:, j * D:(j + 1) * D], AF.Identity,
                                       scale=rstd[:], bias=negb[:]))
            u_j = s_sl.n

            te_base = s_te.n
            tp_base = s_tp.n

            @blk.tensor
            def _(t):
                t.wait_ge(s_sl.h, u_j)
                t.wait_ge(s_te.h, te_base)
                for dc in range(16):
                    bank = [pG, pH][dc % 2]
                    if dc >= 2:
                        t.wait_ge(s_te.h, te_base + dc - 1)
                    s_tp.inc(t.transpose(bank[:, 0:128], u_tm[:, dc * 128:(dc + 1) * 128],
                                         ident[:]))

            @blk.vector
            def _(v):
                for dc in range(16):
                    bank = [pG, pH][dc % 2]
                    v.wait_ge(s_tp.h, tp_base + dc + 1)
                    s_te.inc(v.tensor_copy(u_fm[:, dc * TP + j * 128:dc * TP + (j + 1) * 128],
                                           bank[:, 0:128]))

        def emit_b1(fb, blk, x_src, p, X, dwtX, ufm_free, gate_free):
            t0 = p * TP
            st_prev = st_done[p % 2]

            @blk.sync
            def _(sy):
                sy.wait_ge(s_st.h, st_prev)
                for j in range(4):
                    s_wr0.inc(sy.dma_start(X[:, j * D:(j + 1) * D],
                                           x_src[t0 + j * 128:t0 + (j + 1) * 128, :]), 16)
            xw = s_wr0.n
            ev0 = s_ev.n
            sl0 = s_sl.n
            te0 = s_te.n
            tp0 = s_tp.n
            mm0 = s_mm.n
            stats_j = lambda j: ev0 + 5 * j + 1
            negb_j = lambda j: ev0 + 5 * j + 2
            lgs_j = lambda j: ev0 + 5 * j + 3
            sm1n_j = lambda j: ev0 + 5 * j + 4
            rstd_j = lambda j: sl0 + 3 * j + 1
            apply_j = lambda j: sl0 + 3 * j + 2
            exp_j = lambda j: sl0 + 3 * j + 3
            logit_j = lambda j: mm0 + j + 1

            @blk.scalar
            def _(sc):
                for j in range(4):
                    sc.wait_ge(s_ev.h, stats_j(j))
                    sc.activation(rstd[:], mvb[:, 1:2], AF.Ln, bias=epst[:])
                    sc.drain()
                    s_sl.inc(sc.activation(rstd[:], rstd[:], AF.Exp, scale=-0.5))
                    sc.wait_ge(s_ev.h, negb_j(j))
                    sc.wait_ge(s_te.h, te0 + 16 * j)
                    s_sl.inc(sc.activation(u_tm[:], X[:, j * D:(j + 1) * D],
                                           AF.Identity, scale=rstd[:], bias=negb[:]))
                    sc.wait_ge(s_ev.h, sm1n_j(j))
                    s_sl.inc(sc.activation(smv[:], lgs[:], AF.Exp, bias=sm1n[:]))

            @blk.tensor
            def _(t):
                for j in range(4):
                    t.wait_ge(s_sl.h, apply_j(j))
                    for dc in range(16):
                        bank = [pG, pH][dc % 2]
                        t.wait_ge(s_te.h, te0 + 16 * j + max(0, dc - 1))
                        s_tp.inc(t.transpose(bank[:, 0:128],
                                             u_tm[:, dc * 128:(dc + 1) * 128], ident[:]))
                    if j > 0:
                        t.wait_ge(s_ev.h, lgs_j(j - 1))
                    else:
                        t.wait_ge(s_ev.h, gate_free)
                    t.wait_ge(s_te.h, te0 + 16 * (j + 1))
                    for dc in range(16):
                        mi = t.matmul(pE_[:, 0:E],
                                      u_fm[:, dc * TP + j * 128:dc * TP + (j + 1) * 128],
                                      gwT[:, dc * E:(dc + 1) * E],
                                      start=(dc == 0), stop=(dc == 15))
                    s_mm.inc(mi)

            @blk.vector
            def _(v):
                v.wait_ge(s_wr0.h, xw)
                for j in range(4):
                    if j > 0:
                        v.wait_ge(s_sl.h, rstd_j(j - 1))
                    for k in range(4):
                        v.bn_stats(bnb[:, k * 6:(k + 1) * 6],
                                   X[:, j * D + k * 512:j * D + (k + 1) * 512])
                    v.drain()
                    s_ev.inc(v.bn_aggr(mvb[:], bnb[:]))
                    v.drain()
                    v.wait_ge(s_sl.h, rstd_j(j))
                    v.drain()
                    s_ev.inc(v.tensor_scalar(negb[:], mvb[:, 0:1], rstd[:], -1.0,
                                             op0=ALU.mult, op1=ALU.mult))
                    if j == 0:
                        v.wait_ge(s_mm.h, ufm_free)
                    for dc in range(16):
                        bank = [pG, pH][dc % 2]
                        v.wait_ge(s_tp.h, tp0 + 16 * j + dc + 1)
                        s_te.inc(v.tensor_copy(
                            u_fm[:, dc * TP + j * 128:dc * TP + (j + 1) * 128],
                            bank[:, 0:128]))
                    v.wait_ge(s_mm.h, logit_j(j))
                    s_ev.inc(v.tensor_copy(lgs[:], pE_[:, 0:E]))
                    v.drain()
                    v.tensor_reduce(sm1[:], lgs[:], axis=AX.X, op=ALU.max)
                    v.drain()
                    s_ev.inc(v.tensor_scalar(sm1n[:], sm1[:], -1.0, None, op0=ALU.mult))
                    v.drain()
                    v.wait_ge(s_sl.h, exp_j(j))
                    v.tensor_reduce(sm1[:], smv[:], axis=AX.X, op=ALU.add)
                    v.drain()
                    v.reciprocal(sm2[:], sm1[:])
                    v.drain()
                    v.tensor_scalar(smv[:], smv[:], sm2[:], None, op0=ALU.mult)
                    v.drain()
                    v.tensor_tensor(lgs[:], smv[:], gbb[:], op=ALU.add)
                    v.drain()
                    v.tensor_reduce(sm1[:], lgs[:], axis=AX.X, op=ALU.max)
                    v.drain()
                    v.tensor_scalar(msk[:], lgs[:], sm1[:], None, op0=ALU.is_ge)
                    v.drain()
                    v.scalar_tensor_tensor(lg2[:], msk[:], NEG, lgs[:],
                                           op0=ALU.mult, op1=ALU.add)
                    v.drain()
                    v.tensor_reduce(sm1[:], lg2[:], axis=AX.X, op=ALU.max)
                    v.drain()
                    v.tensor_scalar(msk[:], lgs[:], sm1[:], None, op0=ALU.is_ge)
                    v.drain()
                    v.tensor_tensor(smv[:], smv[:], msk[:], op=ALU.mult)
                    v.drain()
                    s_ev.inc(v.tensor_scalar(dwtX[:, j * E:(j + 1) * E], smv[:], 0.5,
                                             None, op0=ALU.mult))

        def emit_b2(fb, blk, x_dst, p, X, dwtX, first_pass):
            w13, w2t, h_e, asl = fb["w13"], fb["w2t"], fb["h_e"], fb["asl"]
            t0 = p * TP
            mmB = s_mm.n
            slB = s_sl.n
            evB = s_ev.n
            wlB = s_wl.n
            w2lB = s_w2l.n
            h1_mark = lambda e, ic: mmB + 32 * e + 2 * ic + 1
            h3_mark = lambda e, ic: mmB + 32 * e + 2 * ic + 2
            hend = lambda e: mmB + 32 * e + 16
            y_mark = lambda e, k: mmB + 32 * e + 16 + k + 1
            yend = lambda e: mmB + 32 * (e + 1)
            silu_mark = lambda e, ic: slB + 8 * e + ic + 1
            hmul_mark = lambda e, ic: evB + 24 * e + ic + 1
            hmul_end = lambda e: evB + 24 * e + 8
            yev_mark = lambda e, k: evB + 24 * e + 8 + k + 1
            yev_end = lambda e: evB + 24 * (e + 1)
            wl_mark = lambda e: wlB + 512 * (e + 1)
            w2l_mark = lambda e: w2lB + 128 * (e + 1)
            gate_p0c = first_pass

            @blk.sync
            def _(sy):
                for e in range(9):
                    if e >= 1:
                        sy.wait_ge(s_mm.h, hend(e - 1))
                    for dc in range(16):
                        if gate_p0c and e == 0:
                            sy.wait_ge(s_p0c.h, s_p0c.n)
                        s_wl.inc(sy.dma_start(w13[:, dc * I:(dc + 1) * I],
                                              w1ts[(e * 16 + dc) * 128:(e * 16 + dc + 1) * 128, :]), 16)
                    for dc in range(16):
                        if gate_p0c and e == 0:
                            sy.wait_ge(s_p0c.h, s_p0c.n)
                        s_wl.inc(sy.dma_start(w13[:, (16 + dc) * I:(17 + dc) * I],
                                              w3ts[(e * 16 + dc) * 128:(e * 16 + dc + 1) * 128, :]), 16)
                    if e >= 1:
                        sy.wait_ge(s_mm.h, yend(e - 1))
                    for ic in range(8):
                        if gate_p0c and e == 0:
                            sy.wait_ge(s_p0c.h, s_p0c.n)
                        s_w2l.inc(sy.dma_start(w2t[:, ic * D:(ic + 1) * D],
                                               w2ts[(e * 8 + ic) * 128:(e * 8 + ic + 1) * 128, :]), 16)

            @blk.tensor
            def _(t):
                for e in range(9):
                    t.wait_ge(s_wl.h, wl_mark(e))
                    if e == 0:
                        t.wait_ge(s_ev.h, evB)
                    for ic in range(8):
                        pa = [pA, pC][ic % 2]
                        pb = [pB, pD][ic % 2]
                        if ic >= 2:
                            t.wait_ge(s_sl.h, silu_mark(e, ic - 2))
                            t.wait_ge(s_ev.h, hmul_mark(e, ic - 2))
                        elif e >= 1:
                            t.wait_ge(s_sl.h, silu_mark(e - 1, 7))
                            t.wait_ge(s_ev.h, yev_end(e - 1))
                        for dc in range(16):
                            mi = t.matmul(pa[:], w13[:, dc * I + ic * 128:dc * I + (ic + 1) * 128],
                                          u_fm[:, dc * TP:(dc + 1) * TP],
                                          start=(dc == 0), stop=(dc == 15))
                        s_mm.inc(mi)
                        for dc in range(16):
                            mi = t.matmul(pb[:], w13[:, (16 + dc) * I + ic * 128:(16 + dc) * I + (ic + 1) * 128],
                                          u_fm[:, dc * TP:(dc + 1) * TP],
                                          start=(dc == 0), stop=(dc == 15))
                        s_mm.inc(mi)
                    t.wait_ge(s_w2l.h, w2l_mark(e))
                    t.wait_ge(s_ev.h, hmul_end(e))
                    for k, (tb, d2s) in enumerate([(a, b) for a in range(4) for b in range(4)]):
                        py = [pE_, pF_][k % 2]
                        if k >= 2:
                            t.wait_ge(s_ev.h, yev_mark(e, k - 2))
                        elif e >= 1:
                            t.wait_ge(s_ev.h, yev_end(e - 1))
                        for ic in range(8):
                            mi = t.matmul(py[:],
                                          h_e[:, ic * TP + tb * 128:ic * TP + (tb + 1) * 128],
                                          w2t[:, ic * D + d2s * 512:ic * D + (d2s + 1) * 512],
                                          start=(ic == 0), stop=(ic == 7))
                        s_mm.inc(mi)

            @blk.scalar
            def _(sc):
                for e in range(9):
                    for ic in range(8):
                        pa = [pA, pC][ic % 2]
                        sc.wait_ge(s_mm.h, h1_mark(e, ic))
                        if ic >= 2:
                            sc.wait_ge(s_ev.h, hmul_mark(e, ic - 2))
                        sc.wait_ge(s_ev.h, hmul_mark(e - 1, 7) if (ic < 2 and e >= 1) else 0)
                        s_sl.inc(sc.activation(asl[:, (ic % 2) * TP:(ic % 2 + 1) * TP],
                                               pa[:], AF.Silu))

            @blk.vector
            def _(v):
                for e in range(9):
                    for ic in range(8):
                        pb = [pB, pD][ic % 2]
                        v.wait_ge(s_sl.h, silu_mark(e, ic))
                        v.wait_ge(s_mm.h, h3_mark(e, ic))
                        s_ev.inc(v.tensor_tensor(h_e[:, ic * TP:(ic + 1) * TP],
                                                 asl[:, (ic % 2) * TP:(ic % 2 + 1) * TP],
                                                 pb[:], op=ALU.mult))
                    for k, (tb, d2s) in enumerate([(a, b) for a in range(4) for b in range(4)]):
                        py = [pE_, pF_][k % 2]
                        v.wait_ge(s_mm.h, y_mark(e, k))
                        dcol = dwS[:] if e == E else dwtX[:, tb * E + e:tb * E + e + 1]
                        xs_ = X[:, tb * D + d2s * 512:tb * D + (d2s + 1) * 512]
                        s_ev.inc(v.scalar_tensor_tensor(xs_, py[:], dcol, xs_,
                                                        op0=ALU.mult, op1=ALU.add))

            fin = s_ev.n
            return fin, yend(8), yev_end(8)

        aw_free = {}

        st_done = [0, 0]

        def run_ffn_phase(x_src, x_dst, first):
            st_done[0] = s_st.n
            st_done[1] = s_st.n
            sfx = "a" if first else "b"
            fs = contextlib.ExitStack()
            fb = {
                "w13": sb("w13" + sfx, [128, 32 * I], BF16, fs),
                "w2t": sb("w2t" + sfx, [128, 8 * D], BF16, fs),
                "h_e": sb("h_e" + sfx, [128, 8 * TP], BF16, fs),
                "asl": sb("asl" + sfx, [128, 2 * TP], F32, fs),
                "xacc2": sb("xacc2" + sfx, [128, 4 * D], F32, fs),
                "dwt2": sb("dwt2" + sfx, [128, 4 * E], F32, fs),
            }
            X = [xacc, fb["xacc2"]]
            DW = [dwt, fb["dwt2"]]
            bank_free.clear()
            with nc.Block() as blk:
                emit_b1(fb, blk, x_src, 0, X[0], DW[0], s_mm.n, s_ev.n)
            for p in range(NPASS):
                bank_free.clear()
                with nc.Block() as blk:
                    fin, ufm_free, gate_free = emit_b2(fb, blk, x_dst, p, X[p % 2],
                                                       DW[p % 2], first and p == 0)
                    if p + 1 < NPASS:
                        emit_b1(fb, blk, x_src, p + 1, X[(p + 1) % 2], DW[(p + 1) % 2],
                                ufm_free, gate_free)
                    t0 = p * TP

                    @blk.gpsimd
                    def _(g):
                        g.wait_ge(s_ev.h, fin)
                        for j in range(4):
                            s_st.inc(g.dma_start(x_dst[t0 + j * 128:t0 + (j + 1) * 128, :],
                                                 X[p % 2][:, j * D:(j + 1) * D]), 16)
                    st_done[p % 2] = s_st.n
            fs.close()

        def emit_attention(x_src, x_dst):
            ats = contextlib.ExitStack()
            Sst = sb("Sst", [128, H * HD], F32, ats)
            Sbf = sb("Sbf", [128, H * HD], BF16, ats)
            S_ready = [0]
            qk_fm = sb("qk_fm", [128, 2 * 16 * TP], BF16, ats)
            vk2 = sb("vk2", [128, 2 * 4 * D], BF16, ats)
            wring = sb("wring", [128, 2 * 16 * 512], BF16, ats)
            o_tm = sb("o_tm", [128, 4 * D], BF16, ats)
            wg1b = sb("wg1b", [128, 16 * 128], BF16, ats)
            wg2b = sb("wg2b", [128, D], BF16, ats)
            g1f = sb("g1f", [128, TP], BF16, ats)
            gsc = sb("gsc", [128, 512], F32, ats)
            u_tm2 = sb("u_tm2", [128, D], BF16, ats)

            def q_at(m, tj=None):
                c0 = m * TP
                return qk_fm[:, c0 + (0 if tj is None else tj * 128):
                             c0 + (TP if tj is None else (tj + 1) * 128)]

            def k_at(m, tj=None):
                c0 = 16 * TP + m * TP
                return qk_fm[:, c0 + (0 if tj is None else tj * 128):
                             c0 + (TP if tj is None else (tj + 1) * 128)]

            ogn_at = k_at

            def v_at(tj, m):
                return vk2[:, tj * D + m * 128:tj * D + (m + 1) * 128]

            def k2_at(tj, m):
                return vk2[:, 4 * D + tj * D + m * 128:4 * D + tj * D + (m + 1) * 128]

            ring_free = [(s_mm, 0), (s_mm, 0)]

            for sc_i in range(NPASS):
                t0 = sc_i * TP
                bank_free.clear()
                with nc.Block() as blk:
                    st_prev = s_st.n

                    @blk.gpsimd
                    def _(g):
                        if sc_i == 0:
                            g.memset(Sst[:], 0.0)
                            S_ready[0] = s_ev.inc(g.memset(Sbf[:], 0.0))
                        g.wait_ge(s_st.h, st_prev)
                        for j in range(4):
                            s_x.inc(g.dma_start(xacc[:, j * D:(j + 1) * D],
                                                x_src[t0 + j * 128:t0 + (j + 1) * 128, :]), 16)
                    xw = s_x.n

                    if sc_i == 0:
                        @blk.sync
                        def _(sy):
                            for dc in range(16):
                                s_w2l.inc(sy.dma_start(wg1b[:, dc * 128:(dc + 1) * 128],
                                                      wg1s[dc * 128:(dc + 1) * 128, :]), 16)
                            s_w2l.inc(sy.dma_start(wg2b[:], wg2s), 16)
                    wg_done = s_w2l.n

                    for j in range(4):
                        emit_ln_tile(blk, j, xw)

                    vstage_marks = []
                    for pi, (wsrc, kind) in enumerate([(wqs, "q"), (wks, "k"), (wvs, "v")]):
                        for mb in range(4):
                            slot = (pi * 4 + mb) % 2

                            wr = [s_wr0, s_wr1][slot]

                            @blk.sync
                            def _(sy):
                                s, nmark = ring_free[slot]
                                sy.wait_ge(s.h, nmark)
                                for dc in range(16):
                                    wr.inc(sy.dma_start(
                                        wring[:, slot * 8192 + dc * 512:slot * 8192 + (dc + 1) * 512],
                                        wsrc[dc * 128:(dc + 1) * 128, mb * 512:(mb + 1) * 512]), 16)
                            wl_mark = wr.n
                            sl_base = s_sl.n

                            @blk.tensor
                            def _(t):
                                t.wait_ge(wr.h, wl_mark)
                                if pi == 0 and mb == 0:
                                    t.wait_ge(s_te.h, s_te.n)
                                for mc in range(4):
                                    m = mb * 4 + mc
                                    pp = [pA, pB, pC, pD][mc]
                                    wait_bank(t, ("proj", mc))
                                    for dc in range(16):
                                        mi = t.matmul(pp[:],
                                                      wring[:, slot * 8192 + dc * 512 + mc * 128:
                                                            slot * 8192 + dc * 512 + (mc + 1) * 128],
                                                      u_fm[:, dc * TP:(dc + 1) * TP],
                                                      start=(dc == 0), stop=(dc == 15))
                                    s_mm.inc(mi)
                            ring_free[slot] = (s_mm, s_mm.n)
                            mm_marks = [s_mm.n - 3, s_mm.n - 2, s_mm.n - 1, s_mm.n]

                            if kind != "v":
                                @blk.scalar
                                def _(sc):
                                    for mc in range(4):
                                        m = mb * 4 + mc
                                        pp = [pA, pB, pC, pD][mc]
                                        sc.wait_ge(s_mm.h, mm_marks[mc])
                                        dst = q_at(m) if kind == "q" else k_at(m)
                                        s_sl.inc(sc.activation(dst, pp[:], AF.Silu))
                                        set_bank(("proj", mc), s_sl)
                            else:
                                te_base = s_te.n
                                tp_base = s_tp.n
                                for mc in range(4):
                                    m = mb * 4 + mc
                                    pp = [pA, pB, pC, pD][mc]

                                    @blk.scalar
                                    def _(sc):
                                        sc.wait_ge(s_mm.h, mm_marks[mc])
                                        if mc >= 2:
                                            sc.wait_ge(s_te.h, te_base + 4 * mc - 4)
                                        else:
                                            sc.wait_ge(s_te.h, te_base)
                                        s_sl.inc(sc.activation(
                                            u_tm[:, (mc % 2) * 512:(mc % 2 + 1) * 512],
                                            pp[:], AF.Silu))
                                        set_bank(("proj", mc), s_sl)
                                    silu_mc = s_sl.n

                                    @blk.tensor
                                    def _(t):
                                        t.wait_ge(s_sl.h, silu_mc)
                                        t.wait_ge(s_te.h, te_base)
                                        for tj in range(4):
                                            i = mc * 4 + tj
                                            bank = [pG, pH][i % 2]
                                            if i >= 2:
                                                t.wait_ge(s_te.h, te_base + i - 1)
                                            s_tp.inc(t.transpose(
                                                bank[:, 0:128],
                                                u_tm[:, (mc % 2) * 512 + tj * 128:(mc % 2) * 512 + (tj + 1) * 128],
                                                ident[:]))

                                    @blk.vector
                                    def _(v):
                                        for tj in range(4):
                                            i = mc * 4 + tj
                                            bank = [pG, pH][i % 2]
                                            v.wait_ge(s_tp.h, tp_base + i + 1)
                                            s_te.inc(v.tensor_copy(v_at(tj, m), bank[:, 0:128]))

                    @blk.tensor
                    def _(t):
                        if sc_i == 0:
                            t.wait_ge(s_w2l.h, wg_done)
                        wait_bank(t, ("proj", 0))
                        for dc in range(16):
                            mi = t.matmul(pA[:], wg1b[:, dc * 128:(dc + 1) * 128],
                                          u_fm[:, dc * TP:(dc + 1) * TP],
                                          start=(dc == 0), stop=(dc == 15))
                        s_mm.inc(mi)
                    g1_mark = s_mm.n

                    @blk.vector
                    def _(v):
                        v.wait_ge(s_mm.h, g1_mark)
                        s_ev.inc(v.tensor_copy(g1f[:], pA[:]))
                        set_bank(("proj", 0), s_ev)
                    g1f_done = s_ev.n
                    v_copies_done = s_te.n
                    ksilu_done = s_sl.n

                    u2 = [u_tm, u_tm2]
                    k2e_mark = {}
                    Lmm = {}
                    scT = {}
                    evT = {}
                    Tend = {}
                    Tcend = {}

                    def emit_k2(tj):
                        tpb = s_tp.n
                        slb = s_sl.n

                        @blk.tensor
                        def _(t):
                            if tj == 0:
                                t.wait_ge(s_te.h, v_copies_done)
                                t.wait_ge(s_sl.h, ksilu_done)
                            for h in range(H):
                                if h >= 2:
                                    t.wait_ge(s_sl.h, slb + h - 1)
                                else:
                                    wait_bank(t, ("T", h % 2))
                                s_tp.inc(t.transpose([pG, pH][h % 2][:, 0:128],
                                                     k_at(h, tj), ident[:]))

                        @blk.scalar
                        def _(sc):
                            for h in range(H):
                                sc.wait_ge(s_tp.h, tpb + h + 1)
                                s_sl.inc(sc.activation(k2_at(tj, h), [pG, pH][h % 2][:, 0:128],
                                                       AF.Identity, scale=kdec[:, h:h + 1]))
                                set_bank(("T", h % 2), s_sl)
                        k2e_mark[tj] = s_sl.n

                    def emit_L(tj):
                        mmT = s_mm.n
                        evTj = s_ev.n
                        scTj = s_sl.n
                        Lmm[tj] = mmT
                        evT[tj] = evTj
                        scT[tj] = scTj

                        @blk.tensor
                        def _(t):
                            t.wait_ge(s_sl.h, k2e_mark[tj])
                            if tj == 0:
                                t.wait_ge(s_ev.h, g1f_done)
                                if sc_i == 0:
                                    t.wait_ge(s_ev.h, S_ready[0])
                            else:
                                t.wait_ge(s_ev.h, evTj)
                            for h in range(H):
                                if h >= 1:
                                    t.wait_ge(s_ev.h, evTj + 4 * (h - 1) + 1)
                                mi = t.matmul(pA[:, 0:128], k_at(h, tj), q_at(h, tj),
                                              start=True, stop=True)
                                s_mm.inc(mi)
                                if h >= 1:
                                    t.wait_ge(s_sl.h, scTj + h)
                                mi = t.matmul(pB[:, 0:128], q_at(h, tj),
                                              Sbf[:, h * HD:(h + 1) * HD],
                                              start=True, stop=True)
                                s_mm.inc(mi)
                                t.wait_ge(s_ev.h, evTj + 4 * h + 1)
                                mi = t.matmul(pC[:, 0:128], sTbf[:, (h % 2) * 128:(h % 2 + 1) * 128],
                                              v_at(tj, h), start=True, stop=True)
                                s_mm.inc(mi)
                                if h >= 1:
                                    t.wait_ge(s_ev.h, evTj + 4 * (h - 1) + 3)
                                mi = t.matmul(pD[:, 0:128], k2_at(tj, h), v_at(tj, h),
                                              start=True, stop=True)
                                s_mm.inc(mi)
                            for ns in range(4):
                                if tj == 0 and ns == 0:
                                    t.wait_ge(s_ev.h, g1f_done)
                                if ns >= 2:
                                    t.wait_ge(s_ev.h, evTj + 64 + (ns - 2) + 1)
                                elif tj > 0:
                                    t.wait_ge(s_ev.h, evT[tj - 1] + 64 + 4)
                                else:
                                    wait_bank(t, ("gate", ns % 2))
                                s_mm.inc(t.matmul([pE_, pF_][ns % 2][:], g1f[:, tj * 128:(tj + 1) * 128],
                                                  wg2b[:, ns * 512:(ns + 1) * 512],
                                                  start=True, stop=True))

                    def emit_sc(tj):
                        scTj = scT[tj]
                        mmT = Lmm[tj]
                        evTj = evT[tj]

                        @blk.scalar
                        def _(sc):
                            for h in range(H):
                                sc.wait_ge(s_mm.h, mmT + 4 * h + 2)
                                if h >= 2:
                                    sc.wait_ge(s_ev.h, evTj + 4 * (h - 2) + 2)
                                s_sl.inc(sc.activation(o_tmp[:, (h % 2) * 128:(h % 2 + 1) * 128],
                                                       pB[:, 0:128], AF.Identity,
                                                       scale=qdec[:, h:h + 1]))
                            for ns in range(4):
                                sc.wait_ge(s_mm.h, mmT + 64 + ns + 1)
                                if ns >= 1:
                                    sc.wait_ge(s_ev.h, evTj + 64 + ns)
                                s_sl.inc(sc.activation(gsc[:], [pE_, pF_][ns % 2][:],
                                                       AF.Tanh, scale=0.5))
                            sc.wait_ge(s_ev.h, evTj + 64 + 4)
                            if tj >= 2:
                                sc.wait_ge(s_tp.h, s_tp.n + 16)
                            sc.drain()
                            for h in range(H):
                                sc.activation(u2[tj % 2][:, h * 128:(h + 1) * 128],
                                              o_tm[:, tj * D + h * 128:tj * D + (h + 1) * 128],
                                              AF.Square,
                                              accum_out=msq[:, tj * H + h:tj * H + h + 1])
                            sc.drain()
                            sc.activation(rsg[:, tj * H:(tj + 1) * H], msq[:, tj * H:(tj + 1) * H],
                                          AF.Ln, scale=1.0 / HD, bias=epst[:])
                            sc.drain()
                            s_sl.inc(sc.activation(rsg[:, tj * H:(tj + 1) * H],
                                                   rsg[:, tj * H:(tj + 1) * H], AF.Exp, scale=-0.5))
                            sc.drain()
                            for h in range(H):
                                mi = sc.activation(u2[tj % 2][:, h * 128:(h + 1) * 128],
                                                   o_tm[:, tj * D + h * 128:tj * D + (h + 1) * 128],
                                                   AF.Identity, scale=rsg[:, tj * H + h:tj * H + h + 1])
                            s_sl.inc(mi)

                    def emit_vec(tj):
                        scTj = scT[tj]
                        mmT = Lmm[tj]

                        @blk.vector
                        def _(v):
                            for h in range(H):
                                v.wait_ge(s_mm.h, mmT + 4 * h + 1)
                                s_ev.inc(v.tensor_tensor(sTbf[:, (h % 2) * 128:(h % 2 + 1) * 128],
                                                         pA[:, 0:128],
                                                         dt16[:, h * 128:(h + 1) * 128],
                                                         op=ALU.mult))
                                v.wait_ge(s_mm.h, mmT + 4 * h + 3)
                                v.wait_ge(s_sl.h, scTj + h + 1)
                                s_ev.inc(v.tensor_tensor(o_tm[:, tj * D + h * 128:tj * D + (h + 1) * 128],
                                                         pC[:, 0:128],
                                                         o_tmp[:, (h % 2) * 128:(h % 2 + 1) * 128],
                                                         op=ALU.add))
                                v.wait_ge(s_mm.h, mmT + 4 * h + 4)
                                s_ev.inc(v.scalar_tensor_tensor(Sst[:, h * HD:(h + 1) * HD],
                                                                Sst[:, h * HD:(h + 1) * HD],
                                                                cdec[:, h:h + 1],
                                                                pD[:, 0:128],
                                                                op0=ALU.mult, op1=ALU.add))
                                v.drain()
                                s_ev.inc(v.tensor_copy(Sbf[:, h * HD:(h + 1) * HD],
                                                       Sst[:, h * HD:(h + 1) * HD]))
                            for ns in range(4):
                                v.wait_ge(s_sl.h, scTj + 16 + ns + 1)
                                v.tensor_scalar(gsc[:], gsc[:], 0.5, 0.5,
                                                op0=ALU.mult, op1=ALU.add)
                                v.drain()
                                s_ev.inc(v.tensor_tensor(
                                    o_tm[:, tj * D + ns * 512:tj * D + (ns + 1) * 512],
                                    o_tm[:, tj * D + ns * 512:tj * D + (ns + 1) * 512],
                                    gsc[:], op=ALU.mult))

                    def emit_vecB(tj):
                        scTj = scT[tj]

                        @blk.vector
                        def _(v):
                            v.wait_ge(s_sl.h, scTj + 22)
                            s_ev.inc(v.tensor_tensor(u2[tj % 2][:], u2[tj % 2][:], nwb[:],
                                                     op=ALU.mult))

                    def emit_T(tj):
                        tpb = s_tp.n

                        teb = s_te.n

                        @blk.tensor
                        def _(t):
                            t.wait_ge(s_ev.h, evT[tj] + 69)
                            for dc in range(16):
                                if dc >= 2:
                                    t.wait_ge(s_te.h, teb + dc - 1)
                                else:
                                    wait_bank(t, ("T", dc % 2))
                                s_tp.inc(t.transpose([pG, pH][dc % 2][:, 0:128],
                                                     u2[tj % 2][:, dc * 128:(dc + 1) * 128],
                                                     ident[:]))
                        Tend[tj] = s_tp.n

                        @blk.vector
                        def _(v):
                            for dc in range(16):
                                v.wait_ge(s_tp.h, tpb + dc + 1)
                                s_te.inc(v.tensor_copy(ogn_at(dc, tj), [pG, pH][dc % 2][:, 0:128]))
                                set_bank(("T", dc % 2), s_te)
                        Tcend[tj] = s_te.n

                    emit_k2(0)
                    emit_k2(1)
                    emit_L(0); emit_sc(0); emit_vec(0); emit_vecB(0)
                    emit_k2(2)
                    emit_L(1); emit_sc(1); emit_vec(1); emit_vecB(1)
                    emit_k2(3)
                    emit_L(2); emit_sc(2); emit_vec(2)
                    emit_T(0)
                    emit_vecB(2)
                    emit_L(3); emit_sc(3); emit_vec(3)
                    emit_T(1)
                    emit_vecB(3)
                    emit_T(2)
                    emit_T(3)

                    for ns in range(4):
                        slot = ns % 2
                        wr = [s_wr0, s_wr1][slot]

                        @blk.sync
                        def _(sy):
                            s, nmark = ring_free[slot]
                            sy.wait_ge(s.h, nmark)
                            for dc in range(16):
                                wr.inc(sy.dma_start(
                                    wring[:, slot * 8192 + dc * 512:slot * 8192 + (dc + 1) * 512],
                                    wos[dc * 128:(dc + 1) * 128, ns * 512:(ns + 1) * 512]), 16)
                        wl_mark = wr.n

                        for tj in range(4):
                            k = ns * 4 + tj
                            pp = [pE_, pF_][k % 2]

                            @blk.tensor
                            def _(t):
                                if tj == 0:
                                    t.wait_ge(wr.h, wl_mark)
                                if ns == 0 and tj == 0:
                                    t.wait_ge(s_te.h, s_te.n)
                                wait_bank(t, ("gate", k % 2))
                                for dc in range(16):
                                    mi = t.matmul(pp[:], ogn_at(dc, tj),
                                                  wring[:, slot * 8192 + dc * 512:slot * 8192 + (dc + 1) * 512],
                                                  start=(dc == 0), stop=(dc == 15))
                                s_mm.inc(mi)
                            omm = s_mm.n

                            @blk.vector
                            def _(v):
                                v.wait_ge(s_mm.h, omm)
                                xs_ = xacc[:, tj * D + ns * 512:tj * D + (ns + 1) * 512]
                                s_ev.inc(v.tensor_tensor(xs_, xs_, pp[:], op=ALU.add))
                                set_bank(("gate", k % 2), s_ev)
                        ring_free[slot] = (s_mm, s_mm.n)

                    fin = s_ev.n

                    @blk.gpsimd
                    def _(g):
                        g.wait_ge(s_ev.h, fin)
                        for j in range(4):
                            s_st.inc(g.dma_start(x_dst[t0 + j * 128:t0 + (j + 1) * 128, :],
                                                 xacc[:, j * D:(j + 1) * D]), 16)
            ats.close()

        run_ffn_phase(x_in, x1s if STAGE > 1 else y_out, True)

        if STAGE > 1:
            emit_attention(x1s, x2s if STAGE > 2 else y_out)

        if STAGE > 2:
            run_ffn_phase(x2s, y_out, False)

        with nc.Block() as blk:
            @blk.gpsimd
            def _(g):
                g.wait_ge(s_st.h, s_st.n)

    return nc


def _consts():
    ident = np.eye(128, dtype=np.float32)
    import ml_dtypes
    ident = ident.astype(ml_dtypes.bfloat16)
    ones = np.ones((1, 128), dtype=np.float32)
    jj = np.arange(128, dtype=np.float32)
    diff = jj[None, :] - jj[:, None]
    difft = np.where(diff >= 0, diff, 1e9).astype(np.float32)
    cj1 = (jj + 1.0).reshape(128, 1).astype(np.float32)
    cj127 = (127.0 - jj).reshape(128, 1).astype(np.float32)
    return ident, ones, difft, cj1, cj127


def kernel(L0_attn_x, L0_attn_log_decay, L0_attn_Wq, L0_attn_Wk, L0_attn_Wv,
           L0_attn_Wo, L0_attn_Wg1, L0_attn_Wg2, L0_attn_norm_w,
           L0_ffn_gate_w, L0_ffn_gate_b, L0_ffn_W1, L0_ffn_W2, L0_ffn_W3,
           L0_ffn_Ws1, L0_ffn_Ws2, L0_ffn_Ws3):
    ident, ones, difft, cj1, cj127 = _consts()
    x = np.ascontiguousarray(np.asarray(L0_attn_x, dtype=np.float32))
    shared = {
        "ld_in": np.asarray(L0_attn_log_decay, dtype=np.float32).reshape(1, H),
        "wq_in": np.ascontiguousarray(L0_attn_Wq, dtype=np.float32),
        "wk_in": np.ascontiguousarray(L0_attn_Wk, dtype=np.float32),
        "wv_in": np.ascontiguousarray(L0_attn_Wv, dtype=np.float32),
        "wo_in": np.ascontiguousarray(L0_attn_Wo, dtype=np.float32),
        "wg1_in": np.ascontiguousarray(L0_attn_Wg1, dtype=np.float32),
        "wg2_in": np.ascontiguousarray(L0_attn_Wg2, dtype=np.float32),
        "nw_in": np.asarray(L0_attn_norm_w, dtype=np.float32).reshape(1, D),
        "gw_in": np.ascontiguousarray(L0_ffn_gate_w, dtype=np.float32),
        "gb_in": np.asarray(L0_ffn_gate_b, dtype=np.float32).reshape(1, E),
        "w1_in": np.ascontiguousarray(L0_ffn_W1, dtype=np.float32).reshape(E * I, D),
        "w2_in": np.ascontiguousarray(L0_ffn_W2, dtype=np.float32).reshape(E * D, I),
        "w3_in": np.ascontiguousarray(L0_ffn_W3, dtype=np.float32).reshape(E * I, D),
        "ws1_in": np.ascontiguousarray(L0_ffn_Ws1, dtype=np.float32),
        "ws2_in": np.ascontiguousarray(L0_ffn_Ws2, dtype=np.float32),
        "ws3_in": np.ascontiguousarray(L0_ffn_Ws3, dtype=np.float32),
        "ident_in": ident, "ones_in": ones, "difft_in": difft,
        "cj1_in": cj1, "cj127_in": cj127,
    }
    nc = build()
    in_maps = [dict(shared, x_in=np.ascontiguousarray(x[b])) for b in range(8)]
    res = run_bass_kernel_spmd(nc, in_maps, core_ids=list(range(8)))
    out = np.stack([np.asarray(r["y_out"], dtype=np.float32) for r in res.results], axis=0)
    return out


# version 08 (final)
# speedup vs baseline: 1.1577x; 1.0244x over previous
import contextlib
import numpy as np
import concourse.bass as bass
import concourse.mybir as mybir
from concourse.bass_utils import run_bass_kernel_spmd

F32 = mybir.dt.float32
BF16 = mybir.dt.bfloat16
AF = mybir.ActivationFunctionType
ALU = mybir.AluOpType
AX = mybir.AxisListType

N = 4096
D = 2048
H = 16
HD = 128
E = 8
I = 1024
TP = 512
NPASS = N // TP
EPS = 1e-5
NEG = -1e30
STAGE = 3


class Sem:
    def __init__(self, handle):
        self.h = handle
        self.n = 0

    def inc(self, instr, k=1):
        instr.then_inc(self.h, k)
        self.n += k
        return self.n

    def wait(self, eng, n=None):
        eng.wait_ge(self.h, self.n if n is None else n)


def build():
    nc = bass.Bass("TRN2", target_bir_lowering=False, debug=False)

    x_in = nc.dram_tensor("x_in", [N, D], F32, kind="ExternalInput").ap()
    ld_in = nc.dram_tensor("ld_in", [1, H], F32, kind="ExternalInput").ap()
    wq_in = nc.dram_tensor("wq_in", [D, D], F32, kind="ExternalInput").ap()
    wk_in = nc.dram_tensor("wk_in", [D, D], F32, kind="ExternalInput").ap()
    wv_in = nc.dram_tensor("wv_in", [D, D], F32, kind="ExternalInput").ap()
    wo_in = nc.dram_tensor("wo_in", [D, D], F32, kind="ExternalInput").ap()
    wg1_in = nc.dram_tensor("wg1_in", [D, HD], F32, kind="ExternalInput").ap()
    wg2_in = nc.dram_tensor("wg2_in", [HD, D], F32, kind="ExternalInput").ap()
    nw_in = nc.dram_tensor("nw_in", [1, D], F32, kind="ExternalInput").ap()
    gw_in = nc.dram_tensor("gw_in", [E, D], F32, kind="ExternalInput").ap()
    gb_in = nc.dram_tensor("gb_in", [1, E], F32, kind="ExternalInput").ap()
    w1_in = nc.dram_tensor("w1_in", [E * I, D], F32, kind="ExternalInput").ap()
    w2_in = nc.dram_tensor("w2_in", [E * D, I], F32, kind="ExternalInput").ap()
    w3_in = nc.dram_tensor("w3_in", [E * I, D], F32, kind="ExternalInput").ap()
    ws1_in = nc.dram_tensor("ws1_in", [I, D], F32, kind="ExternalInput").ap()
    ws2_in = nc.dram_tensor("ws2_in", [D, I], F32, kind="ExternalInput").ap()
    ws3_in = nc.dram_tensor("ws3_in", [I, D], F32, kind="ExternalInput").ap()
    ident_in = nc.dram_tensor("ident_in", [128, 128], BF16, kind="ExternalInput").ap()
    ones_in = nc.dram_tensor("ones_in", [1, 128], F32, kind="ExternalInput").ap()
    difft_in = nc.dram_tensor("difft_in", [128, 128], F32, kind="ExternalInput").ap()
    cj1_in = nc.dram_tensor("cj1_in", [128, 1], F32, kind="ExternalInput").ap()
    cj127_in = nc.dram_tensor("cj127_in", [128, 1], F32, kind="ExternalInput").ap()

    y_out = nc.dram_tensor("y_out", [N, D], F32, kind="ExternalOutput").ap()

    x1s = nc.dram_tensor("x1s", [N, D], F32, kind="Internal").ap()
    x2s = nc.dram_tensor("x2s", [N, D], F32, kind="Internal").ap()
    w1ts = nc.dram_tensor("w1ts", [9 * D, I], BF16, kind="Internal").ap()
    w3ts = nc.dram_tensor("w3ts", [9 * D, I], BF16, kind="Internal").ap()
    w2ts = nc.dram_tensor("w2ts", [9 * I, D], BF16, kind="Internal").ap()
    gws = nc.dram_tensor("gws", [E, D], BF16, kind="Internal").ap()
    wqs = nc.dram_tensor("wqs", [D, D], BF16, kind="Internal").ap()
    wks = nc.dram_tensor("wks", [D, D], BF16, kind="Internal").ap()
    wvs = nc.dram_tensor("wvs", [D, D], BF16, kind="Internal").ap()
    wos = nc.dram_tensor("wos", [D, D], BF16, kind="Internal").ap()
    wg1s = nc.dram_tensor("wg1s", [D, HD], BF16, kind="Internal").ap()
    wg2s = nc.dram_tensor("wg2s", [HD, D], BF16, kind="Internal").ap()

    es = contextlib.ExitStack()
    with es:
        def sb(name, shape, dt=F32, stack=None):
            return (stack or es).enter_context(nc.sbuf_tensor(name, shape, dt))

        def psum(name, shape, dt=F32):
            return es.enter_context(nc.psum_tensor(name, shape, dt))

        def sem(name):
            return Sem(es.enter_context(nc.semaphore(name)))

        ident = sb("ident", [128, 128], BF16)
        onesf = sb("onesf", [1, 128], F32)
        epst = sb("epst", [128, 1], F32)
        dwS = sb("dwS", [128, 1], F32)
        gbb = sb("gbb", [128, E], F32)
        gwT = sb("gwT", [128, 16 * E], BF16)
        nwb = sb("nwb", [128, D], BF16)
        ldb = sb("ldb", [128, H], F32)
        dt16 = sb("dt16", [128, H * 128], BF16)
        qdec = sb("qdec", [128, H], F32)
        kdec = sb("kdec", [128, H], F32)
        cdec = sb("cdec", [128, H], F32)
        xacc = sb("xacc", [128, 4 * D], F32)
        u_fm = sb("u_fm", [128, 16 * TP], BF16)
        u_tm = sb("u_tm", [128, D], BF16)
        dwt = sb("dwt", [128, 4 * E], F32)
        lgs4 = sb("lgs4", [128, 4 * E], F32)
        sm4 = sb("sm4", [128, 4 * E], F32)
        mg4 = sb("mg4", [128, 4], F32)
        rg4 = sb("rg4", [128, 4], F32)
        lgs = sb("lgs", [128, E], F32)
        msk = sb("msk", [128, E], F32)
        lg2 = sb("lg2", [128, E], F32)
        smv = sb("smv", [128, E], F32)
        sm1 = sb("sm1", [128, 1], F32)
        sm1n = sb("sm1n", [128, 1], F32)
        sm2 = sb("sm2", [128, 1], F32)
        bnb = sb("bnb", [128, 24], F32)
        mvb = sb("mvb", [128, 2], F32)
        rstd = sb("rstd", [128, 1], F32)
        negb = sb("negb", [128, 1], F32)
        sTbf = sb("sTbf", [128, 2 * 128], BF16)
        o_tmp = sb("o_tmp", [128, 2 * 128], F32)
        msq = sb("msq", [128, 4 * H], F32)
        rsg = sb("rsg", [128, 4 * H], F32)
        sqd = sb("sqd", [128, 128], F32)

        pA = psum("pA", [128, 512], F32)
        pB = psum("pB", [128, 512], F32)
        pC = psum("pC", [128, 512], F32)
        pD = psum("pD", [128, 512], F32)
        pE_ = psum("pE_", [128, 512], F32)
        pF_ = psum("pF_", [128, 512], F32)
        pG = psum("pG", [128, 1024], BF16)
        pH = psum("pH", [128, 1024], BF16)

        s_cin = sem("s_cin")
        s_cinO = sem("s_cinO")
        s_cout = sem("s_cout")
        s_coutO = sem("s_coutO")
        s_x = sem("s_x")
        s_wl = sem("s_wl")
        s_w2l = sem("s_w2l")
        s_mm = sem("s_mm")
        s_ev = sem("s_ev")
        s_sl = sem("s_sl")
        s_st = sem("s_st")
        s_tp = sem("s_tp")
        s_te = sem("s_te")
        s_wr0 = sem("s_wr0")
        s_wr1 = sem("s_wr1")
        s_p0c = sem("s_p0c")

        bank_free = {}

        def wait_bank(eng, bank):
            if bank in bank_free:
                s, n = bank_free[bank]
                eng.wait_ge(s.h, n)

        def set_bank(bank, s):
            bank_free[bank] = (s, s.n)

        p0 = contextlib.ExitStack()
        cbuf = [sb("cbuf0", [128, 2048], BF16, p0), sb("cbuf1", [128, 2048], BF16, p0)]
        cast_jobs = []
        for csrc, cdst, rows, cols in [
            (gw_in, gws, E, D),
            (wq_in, wqs, D, D), (wk_in, wks, D, D), (wv_in, wvs, D, D),
            (wg1_in, wg1s, D, HD), (wg2_in, wg2s, HD, D), (wo_in, wos, D, D),
        ]:
            if rows < 128:
                cast_jobs.append((csrc[0:rows, :], cdst[0:rows, :], rows, cols))
            else:
                for r in range(0, rows, 128):
                    cast_jobs.append((csrc[r:r + 128, :], cdst[r:r + 128, :], 128, cols))

        n_cast_out = [0, 0]
        with nc.Block() as blk:
            @blk.gpsimd
            def _(g):
                for t, (sa, da, rr, cc) in enumerate(cast_jobs):
                    cin = [s_cin, s_cinO][t % 2]
                    cout = [s_cout, s_coutO][t % 2]
                    if t >= 2:
                        g.wait_ge(cout.h, 16 * (t // 2))
                    cin.inc(g.dma_start(cbuf[t % 2][0:rr, 0:cc], sa), 16)

            @blk.sync
            def _(sy):
                for t, (sa, da, rr, cc) in enumerate(cast_jobs):
                    cin = [s_cin, s_cinO][t % 2]
                    cout = [s_cout, s_coutO][t % 2]
                    sy.wait_ge(cin.h, 16 * (t // 2 + 1))
                    cout.inc(sy.dma_start(da, cbuf[t % 2][0:rr, 0:cc]), 16)
                    n_cast_out[t % 2] += 16

            @blk.vector
            def _(v):
                v.wait_ge(s_cout.h, n_cast_out[0])
                v.wait_ge(s_coutO.h, n_cast_out[1])
        p0.close()

        su = contextlib.ExitStack()
        gwsb = sb("gwsb", [E, D], BF16, su)
        ldrow = sb("ldrow", [1, H], F32, su)
        nwrow = sb("nwrow", [1, D], F32, su)
        gbrow = sb("gbrow", [1, E], F32, su)
        difft = sb("difft", [128, 128], F32, su)
        cj1 = sb("cj1", [128, 1], F32, su)
        cj127 = sb("cj127", [128, 1], F32, su)

        with nc.Block() as blk:
            @blk.gpsimd
            def _(g):
                g.wait_ge(s_cout.h, n_cast_out[0])
                g.wait_ge(s_coutO.h, n_cast_out[1])
                for dst, srca in [(ident, ident_in), (onesf, ones_in), (ldrow, ld_in),
                                  (nwrow, nw_in), (gbrow, gb_in), (difft, difft_in),
                                  (cj1, cj1_in), (cj127, cj127_in), (gwsb, gws)]:
                    s_x.inc(g.dma_start(dst[:], srca), 16)
                g.memset(epst[:], EPS)
                g.memset(dwS[:], 0.5)

            @blk.vector
            def _(v):
                s_x.wait(v)

        gw_te = 0
        with nc.Block() as blk:
            gw_te = s_te.n
            gw_tp = s_tp.n
            @blk.tensor
            def _(t):
                s_mm.inc(t.matmul(pA[:, 0:E], onesf[:], gbrow[:], start=True, stop=True))
                s_mm.inc(t.matmul(pA[:, 16:16 + H], onesf[:], ldrow[:], start=True, stop=True))
                for i in range(4):
                    s_mm.inc(t.matmul([pB, pC, pD, pE_][i][:], onesf[:],
                                      nwrow[:, i * 512:(i + 1) * 512], start=True, stop=True))
                t.wait_ge(s_te.h, gw_te)
                for dc in range(16):
                    dstp = [pG, pH][dc % 2][:, 0:8]
                    if dc >= 2:
                        t.wait_ge(s_te.h, gw_te + dc - 1)
                    s_tp.inc(t.transpose(dstp, gwsb[0:E, dc * 128:(dc + 1) * 128], ident[0:E, 0:E]))

            @blk.vector
            def _(v):
                s_mm.wait(v)
                v.tensor_copy(gbb[:], pA[:, 0:E])
                v.tensor_copy(ldb[:], pA[:, 16:16 + H])
                for i in range(4):
                    ni = v.tensor_copy(nwb[:, i * 512:(i + 1) * 512], [pB, pC, pD, pE_][i][:])
                s_ev.inc(ni)
                for dc in range(16):
                    srcp = [pG, pH][dc % 2][:, 0:8]
                    v.wait_ge(s_tp.h, gw_tp + dc + 1)
                    ci = v.tensor_copy(gwT[:, dc * E:(dc + 1) * E], srcp)
                    s_te.inc(ci)

            @blk.scalar
            def _(sc):
                s_te.wait(sc)
                s_ev.wait(sc)
                for h in range(H):
                    sc.activation(dt16[:, h * 128:(h + 1) * 128], difft[:], AF.Exp,
                                  scale=ldb[:, h:h + 1])
                    sc.activation(qdec[:, h:h + 1], cj1[:], AF.Exp, scale=ldb[:, h:h + 1])
                    sc.activation(kdec[:, h:h + 1], cj127[:], AF.Exp, scale=ldb[:, h:h + 1])
                    sc.activation(cdec[:, h:h + 1], ldb[:, h:h + 1], AF.Exp, scale=float(HD))
        su.close()

        p0c_mark = {}
        chunk_tp_done = {}
        asm_free = [0]
        p0c = contextlib.ExitStack()
        natb = [sb("natb0", [128, D], BF16, p0c), sb("natb1", [128, D], BF16, p0c)]
        asmb = sb("asmb", [128, 16 * I], BF16, p0c)
        p0c_cnt = [0, 0]
        with nc.Block() as blk:
            groups = []
            for e in range(9):
                w1src = w1_in[e * I:(e + 1) * I, :] if e < E else ws1_in
                w3src = w3_in[e * I:(e + 1) * I, :] if e < E else ws3_in
                w2src = w2_in[e * D:(e + 1) * D, :] if e < E else ws2_in
                groups.append((e, "w1", w1src, 8, D, 16, I, w1ts, e * 16))
                groups.append((e, "w3", w3src, 8, D, 16, I, w3ts, e * 16))
                groups.append((e, "w2", w2src, 16, I, 8, D, w2ts, e * 8))
            tp0 = s_tp.n
            cum = tp0
            for (_, _, _, nch_, _, och_, _, _, _) in groups:
                for c in range(nch_):
                    chunk_tp_done[len(chunk_tp_done)] = cum + och_ * (c + 1)
                cum += nch_ * och_
            for gi, (e, nm, fsrc, nch, ncol, och, ocol, tdst, orow0) in enumerate(groups):
                tp_base = s_tp.n
                te_base = s_te.n
                ld_marks = []

                @blk.gpsimd
                def _(g):
                    for c in range(nch):
                        t_idx = sum(x[3] for x in groups[:gi]) + c
                        if t_idx >= 2:
                            g.wait_ge(s_tp.h, chunk_tp_done[t_idx - 2])
                        cin = [s_cin, s_cinO][t_idx % 2]
                        cin.inc(g.dma_start(natb[t_idx % 2][:, 0:ncol],
                                            fsrc[c * 128:(c + 1) * 128, :]), 16)
                        ld_marks.append((cin, cin.n))

                @blk.tensor
                def _(t):
                    t.wait_ge(s_te.h, te_base)
                    for c in range(nch):
                        t_idx = sum(x[3] for x in groups[:gi]) + c
                        cs, cn = ld_marks[c]
                        t.wait_ge(cs.h, cn)
                        for k in range(och):
                            i = c * och + k
                            bank = [pG, pH][i % 2]
                            if i >= 2:
                                t.wait_ge(s_te.h, te_base + i - 1)
                            s_tp.inc(t.transpose(bank[:, 0:128],
                                                 natb[t_idx % 2][:, k * 128:(k + 1) * 128],
                                                 ident[:]))

                @blk.vector
                def _(v):
                    v.wait_ge(s_p0c.h, asm_free[0])
                    for c in range(nch):
                        for k in range(och):
                            i = c * och + k
                            bank = [pG, pH][i % 2]
                            v.wait_ge(s_tp.h, tp_base + i + 1)
                            s_te.inc(v.tensor_copy(
                                asmb[:, k * ocol + c * 128:k * ocol + (c + 1) * 128],
                                bank[:, 0:128]))
                ev_end = s_te.n

                @blk.sync
                def _(sy):
                    sy.wait_ge(s_te.h, ev_end)
                    sy.wait_ge(s_p0c.h, s_p0c.n)
                    for k in range(och):
                        s_p0c.inc(sy.dma_start(
                            tdst[(orow0 + k) * 128:(orow0 + k + 1) * 128, :],
                            asmb[:, k * ocol:(k + 1) * ocol]), 16)
                asm_free[0] = s_p0c.n
                if nm == "w2":
                    p0c_mark[e] = s_p0c.n
        p0c.close()

        def emit_ln_tile(blk, j, xw_done):
            @blk.vector
            def _(v):
                v.wait_ge(s_x.h, xw_done)
                if j > 0:
                    v.wait_ge(s_sl.h, s_sl.n)
                for k in range(4):
                    v.bn_stats(bnb[:, k * 6:(k + 1) * 6],
                               xacc[:, j * D + k * 512:j * D + (k + 1) * 512])
                v.drain()
                s_ev.inc(v.bn_aggr(mvb[:], bnb[:]))
                v.drain()
            stats_j = s_ev.n

            @blk.scalar
            def _(sc):
                sc.wait_ge(s_ev.h, stats_j)
                sc.activation(rstd[:], mvb[:, 1:2], AF.Ln, bias=epst[:])
                sc.drain()
                s_sl.inc(sc.activation(rstd[:], rstd[:], AF.Exp, scale=-0.5))
            rstd_j = s_sl.n

            @blk.vector
            def _(v):
                v.wait_ge(s_sl.h, rstd_j)
                v.drain()
                s_ev.inc(v.tensor_scalar(negb[:], mvb[:, 0:1], rstd[:], -1.0,
                                         op0=ALU.mult, op1=ALU.mult))
            negb_j = s_ev.n

            @blk.scalar
            def _(sc):
                sc.wait_ge(s_ev.h, negb_j)
                sc.wait_ge(s_te.h, s_te.n)
                s_sl.inc(sc.activation(u_tm[:], xacc[:, j * D:(j + 1) * D], AF.Identity,
                                       scale=rstd[:], bias=negb[:]))
            u_j = s_sl.n

            te_base = s_te.n
            tp_base = s_tp.n

            @blk.tensor
            def _(t):
                t.wait_ge(s_sl.h, u_j)
                t.wait_ge(s_te.h, te_base)
                for dc in range(16):
                    bank = [pG, pH][dc % 2]
                    if dc >= 2:
                        t.wait_ge(s_te.h, te_base + dc - 1)
                    s_tp.inc(t.transpose(bank[:, 0:128], u_tm[:, dc * 128:(dc + 1) * 128],
                                         ident[:]))

            @blk.vector
            def _(v):
                for dc in range(16):
                    bank = [pG, pH][dc % 2]
                    v.wait_ge(s_tp.h, tp_base + dc + 1)
                    s_te.inc(v.tensor_copy(u_fm[:, dc * TP + j * 128:dc * TP + (j + 1) * 128],
                                           bank[:, 0:128]))

        def emit_b1(fb, blk, x_src, p, X, dwtX, ufm_free, gate_free):
            t0 = p * TP
            st_prev = st_done[p % 2]

            @blk.sync
            def _(sy):
                sy.wait_ge(s_st.h, st_prev)
                for j in range(4):
                    s_wr0.inc(sy.dma_start(X[:, j * D:(j + 1) * D],
                                           x_src[t0 + j * 128:t0 + (j + 1) * 128, :]), 16)
            xw = s_wr0.n
            ev0 = s_ev.n
            sl0 = s_sl.n
            te0 = s_te.n
            tp0 = s_tp.n
            mm0 = s_mm.n
            stats_j = lambda j: ev0 + 2 * j + 1
            negb_j = lambda j: ev0 + 2 * j + 2
            sub4_mark = ev0 + 10
            dwt_mark = ev0 + 11
            rstd_j = lambda j: sl0 + 2 * j + 1
            apply_j = lambda j: sl0 + 2 * j + 2
            exp4_mark = sl0 + 9
            logit_j = lambda j: mm0 + j + 1

            @blk.scalar
            def _(sc):
                for j in range(4):
                    sc.wait_ge(s_ev.h, stats_j(j))
                    sc.activation(rstd[:], mvb[:, 1:2], AF.Ln, bias=epst[:])
                    sc.drain()
                    s_sl.inc(sc.activation(rstd[:], rstd[:], AF.Exp, scale=-0.5))
                    sc.wait_ge(s_ev.h, negb_j(j))
                    sc.wait_ge(s_te.h, te0 + 16 * j)
                    s_sl.inc(sc.activation(u_tm[:], X[:, j * D:(j + 1) * D],
                                           AF.Identity, scale=rstd[:], bias=negb[:]))
                sc.wait_ge(s_ev.h, sub4_mark)
                s_sl.inc(sc.activation(sm4[:], sm4[:], AF.Exp))

            @blk.tensor
            def _(t):
                for j in range(4):
                    t.wait_ge(s_sl.h, apply_j(j))
                    for dc in range(16):
                        bank = [pG, pH][dc % 2]
                        t.wait_ge(s_te.h, te0 + 16 * j + max(0, dc - 1))
                        s_tp.inc(t.transpose(bank[:, 0:128],
                                             u_tm[:, dc * 128:(dc + 1) * 128], ident[:]))
                    if j == 0:
                        t.wait_ge(s_ev.h, gate_free)
                    t.wait_ge(s_te.h, te0 + 16 * (j + 1))
                    for dc in range(16):
                        mi = t.matmul(pE_[:, j * E:(j + 1) * E],
                                      u_fm[:, dc * TP + j * 128:dc * TP + (j + 1) * 128],
                                      gwT[:, dc * E:(dc + 1) * E],
                                      start=(dc == 0), stop=(dc == 15))
                    s_mm.inc(mi)

            @blk.vector
            def _(v):
                v.wait_ge(s_wr0.h, xw)
                for j in range(4):
                    if j > 0:
                        v.wait_ge(s_sl.h, rstd_j(j - 1))
                    for k in range(4):
                        v.bn_stats(bnb[:, k * 6:(k + 1) * 6],
                                   X[:, j * D + k * 512:j * D + (k + 1) * 512])
                    v.drain()
                    s_ev.inc(v.bn_aggr(mvb[:], bnb[:]))
                    v.drain()
                    v.wait_ge(s_sl.h, rstd_j(j))
                    v.drain()
                    s_ev.inc(v.tensor_scalar(negb[:], mvb[:, 0:1], rstd[:], -1.0,
                                             op0=ALU.mult, op1=ALU.mult))
                    if j == 0:
                        v.wait_ge(s_mm.h, ufm_free)
                    for dc in range(16):
                        bank = [pG, pH][dc % 2]
                        v.wait_ge(s_tp.h, tp0 + 16 * j + dc + 1)
                        s_te.inc(v.tensor_copy(
                            u_fm[:, dc * TP + j * 128:dc * TP + (j + 1) * 128],
                            bank[:, 0:128]))
                gexp = lambda a: a.rearrange("p (g e) -> p g e", e=E)
                gbx = lambda a: a.rearrange("p (g one) -> p g one", one=1).broadcast_to((128, 4, E))
                v.wait_ge(s_mm.h, logit_j(3))
                s_ev.inc(v.tensor_copy(lgs4[:], pE_[:, 0:4 * E]))
                v.drain()
                v.tensor_reduce(mg4[:], gexp(lgs4[:]), axis=AX.X, op=ALU.max)
                v.drain()
                s_ev.inc(v.tensor_tensor(gexp(sm4[:]), gexp(lgs4[:]), gbx(mg4[:]),
                                         op=ALU.subtract))
                v.drain()
                v.wait_ge(s_sl.h, exp4_mark)
                v.tensor_reduce(mg4[:], gexp(sm4[:]), axis=AX.X, op=ALU.add)
                v.drain()
                v.reciprocal(rg4[:], mg4[:])
                v.drain()
                v.tensor_tensor(gexp(sm4[:]), gexp(sm4[:]), gbx(rg4[:]), op=ALU.mult)
                v.drain()
                v.tensor_tensor(gexp(lgs4[:]), gexp(sm4[:]),
                                gbb[:].rearrange("p (one e) -> p one e", one=1)
                                .broadcast_to((128, 4, E)), op=ALU.add)
                v.drain()
                v.tensor_reduce(mg4[:], gexp(lgs4[:]), axis=AX.X, op=ALU.max)
                v.drain()
                v.tensor_tensor(gexp(dwtX[:]), gexp(lgs4[:]), gbx(mg4[:]), op=ALU.is_ge)
                v.drain()
                v.scalar_tensor_tensor(dwtX[:], dwtX[:], NEG, lgs4[:],
                                       op0=ALU.mult, op1=ALU.add)
                v.drain()
                v.tensor_reduce(mg4[:], gexp(dwtX[:]), axis=AX.X, op=ALU.max)
                v.drain()
                v.tensor_tensor(gexp(dwtX[:]), gexp(lgs4[:]), gbx(mg4[:]), op=ALU.is_ge)
                v.drain()
                v.tensor_tensor(dwtX[:], dwtX[:], sm4[:], op=ALU.mult)
                v.drain()
                s_ev.inc(v.tensor_scalar(dwtX[:], dwtX[:], 0.5, None, op0=ALU.mult))

        def emit_b2(fb, blk, x_dst, p, X, dwtX, first_pass):
            w13, w2t, h_e, asl = fb["w13"], fb["w2t"], fb["h_e"], fb["asl"]
            t0 = p * TP
            mmB = s_mm.n
            slB = s_sl.n
            evB = s_ev.n
            wlB = s_wl.n
            w2lB = s_w2l.n
            h1_mark = lambda e, ic: mmB + 32 * e + 2 * ic + 1
            h3_mark = lambda e, ic: mmB + 32 * e + 2 * ic + 2
            hend = lambda e: mmB + 32 * e + 16
            y_mark = lambda e, k: mmB + 32 * e + 16 + k + 1
            yend = lambda e: mmB + 32 * (e + 1)
            silu_mark = lambda e, ic: slB + 8 * e + ic + 1
            hmul_mark = lambda e, ic: evB + 24 * e + ic + 1
            hmul_end = lambda e: evB + 24 * e + 8
            yev_mark = lambda e, k: evB + 24 * e + 8 + k + 1
            yev_end = lambda e: evB + 24 * (e + 1)
            wl_mark = lambda e: wlB + 512 * (e + 1)
            w2l_mark = lambda e: w2lB + 128 * (e + 1)
            gate_p0c = first_pass

            @blk.sync
            def _(sy):
                for e in range(9):
                    if e >= 1:
                        sy.wait_ge(s_mm.h, hend(e - 1))
                    for dc in range(16):
                        if gate_p0c and e == 0:
                            sy.wait_ge(s_p0c.h, s_p0c.n)
                        s_wl.inc(sy.dma_start(w13[:, dc * I:(dc + 1) * I],
                                              w1ts[(e * 16 + dc) * 128:(e * 16 + dc + 1) * 128, :]), 16)
                    for dc in range(16):
                        if gate_p0c and e == 0:
                            sy.wait_ge(s_p0c.h, s_p0c.n)
                        s_wl.inc(sy.dma_start(w13[:, (16 + dc) * I:(17 + dc) * I],
                                              w3ts[(e * 16 + dc) * 128:(e * 16 + dc + 1) * 128, :]), 16)
                    if e >= 1:
                        sy.wait_ge(s_mm.h, yend(e - 1))
                    for ic in range(8):
                        if gate_p0c and e == 0:
                            sy.wait_ge(s_p0c.h, s_p0c.n)
                        s_w2l.inc(sy.dma_start(w2t[:, ic * D:(ic + 1) * D],
                                               w2ts[(e * 8 + ic) * 128:(e * 8 + ic + 1) * 128, :]), 16)

            @blk.tensor
            def _(t):
                for e in range(9):
                    t.wait_ge(s_wl.h, wl_mark(e))
                    if e == 0:
                        t.wait_ge(s_ev.h, evB)
                    for ic in range(8):
                        pa = [pA, pC][ic % 2]
                        pb = [pB, pD][ic % 2]
                        if ic >= 2:
                            t.wait_ge(s_sl.h, silu_mark(e, ic - 2))
                            t.wait_ge(s_ev.h, hmul_mark(e, ic - 2))
                        elif e >= 1:
                            t.wait_ge(s_sl.h, silu_mark(e - 1, 7))
                            t.wait_ge(s_ev.h, yev_end(e - 1))
                        for dc in range(16):
                            mi = t.matmul(pa[:], w13[:, dc * I + ic * 128:dc * I + (ic + 1) * 128],
                                          u_fm[:, dc * TP:(dc + 1) * TP],
                                          start=(dc == 0), stop=(dc == 15))
                        s_mm.inc(mi)
                        for dc in range(16):
                            mi = t.matmul(pb[:], w13[:, (16 + dc) * I + ic * 128:(16 + dc) * I + (ic + 1) * 128],
                                          u_fm[:, dc * TP:(dc + 1) * TP],
                                          start=(dc == 0), stop=(dc == 15))
                        s_mm.inc(mi)
                    t.wait_ge(s_w2l.h, w2l_mark(e))
                    t.wait_ge(s_ev.h, hmul_end(e))
                    for k, (tb, d2s) in enumerate([(a, b) for a in range(4) for b in range(4)]):
                        py = [pE_, pF_][k % 2]
                        if k >= 2:
                            t.wait_ge(s_ev.h, yev_mark(e, k - 2))
                        elif e >= 1:
                            t.wait_ge(s_ev.h, yev_end(e - 1))
                        for ic in range(8):
                            mi = t.matmul(py[:],
                                          h_e[:, ic * TP + tb * 128:ic * TP + (tb + 1) * 128],
                                          w2t[:, ic * D + d2s * 512:ic * D + (d2s + 1) * 512],
                                          start=(ic == 0), stop=(ic == 7))
                        s_mm.inc(mi)

            @blk.scalar
            def _(sc):
                for e in range(9):
                    for ic in range(8):
                        pa = [pA, pC][ic % 2]
                        sc.wait_ge(s_mm.h, h1_mark(e, ic))
                        if ic >= 2:
                            sc.wait_ge(s_ev.h, hmul_mark(e, ic - 2))
                        sc.wait_ge(s_ev.h, hmul_mark(e - 1, 7) if (ic < 2 and e >= 1) else 0)
                        s_sl.inc(sc.activation(asl[:, (ic % 2) * TP:(ic % 2 + 1) * TP],
                                               pa[:], AF.Silu))

            @blk.vector
            def _(v):
                for e in range(9):
                    for ic in range(8):
                        pb = [pB, pD][ic % 2]
                        v.wait_ge(s_sl.h, silu_mark(e, ic))
                        v.wait_ge(s_mm.h, h3_mark(e, ic))
                        s_ev.inc(v.tensor_tensor(h_e[:, ic * TP:(ic + 1) * TP],
                                                 asl[:, (ic % 2) * TP:(ic % 2 + 1) * TP],
                                                 pb[:], op=ALU.mult))
                    for k, (tb, d2s) in enumerate([(a, b) for a in range(4) for b in range(4)]):
                        py = [pE_, pF_][k % 2]
                        v.wait_ge(s_mm.h, y_mark(e, k))
                        dcol = dwS[:] if e == E else dwtX[:, tb * E + e:tb * E + e + 1]
                        xs_ = X[:, tb * D + d2s * 512:tb * D + (d2s + 1) * 512]
                        s_ev.inc(v.scalar_tensor_tensor(xs_, py[:], dcol, xs_,
                                                        op0=ALU.mult, op1=ALU.add))

            fin = s_ev.n
            return fin, yend(8), yev_end(8)

        aw_free = {}

        st_done = [0, 0]

        def run_ffn_phase(x_src, x_dst, first):
            st_done[0] = s_st.n
            st_done[1] = s_st.n
            sfx = "a" if first else "b"
            fs = contextlib.ExitStack()
            fb = {
                "w13": sb("w13" + sfx, [128, 32 * I], BF16, fs),
                "w2t": sb("w2t" + sfx, [128, 8 * D], BF16, fs),
                "h_e": sb("h_e" + sfx, [128, 8 * TP], BF16, fs),
                "asl": sb("asl" + sfx, [128, 2 * TP], F32, fs),
                "xacc2": sb("xacc2" + sfx, [128, 4 * D], F32, fs),
                "dwt2": sb("dwt2" + sfx, [128, 4 * E], F32, fs),
            }
            X = [xacc, fb["xacc2"]]
            DW = [dwt, fb["dwt2"]]
            bank_free.clear()
            with nc.Block() as blk:
                emit_b1(fb, blk, x_src, 0, X[0], DW[0], s_mm.n, s_ev.n)
            for p in range(NPASS):
                bank_free.clear()
                with nc.Block() as blk:
                    fin, ufm_free, gate_free = emit_b2(fb, blk, x_dst, p, X[p % 2],
                                                       DW[p % 2], first and p == 0)
                    if p + 1 < NPASS:
                        emit_b1(fb, blk, x_src, p + 1, X[(p + 1) % 2], DW[(p + 1) % 2],
                                ufm_free, gate_free)
                    t0 = p * TP

                    @blk.gpsimd
                    def _(g):
                        g.wait_ge(s_ev.h, fin)
                        for j in range(4):
                            s_st.inc(g.dma_start(x_dst[t0 + j * 128:t0 + (j + 1) * 128, :],
                                                 X[p % 2][:, j * D:(j + 1) * D]), 16)
                    st_done[p % 2] = s_st.n
            fs.close()

        def emit_attention(x_src, x_dst):
            ats = contextlib.ExitStack()
            Sst = sb("Sst", [128, H * HD], F32, ats)
            Sbf = sb("Sbf", [128, H * HD], BF16, ats)
            S_ready = [0]
            qk_fm = sb("qk_fm", [128, 2 * 16 * TP], BF16, ats)
            vk2 = sb("vk2", [128, 2 * 4 * D], BF16, ats)
            wring = sb("wring", [128, 2 * 16 * 512], BF16, ats)
            o_tm = sb("o_tm", [128, 4 * D], BF16, ats)
            wg1b = sb("wg1b", [128, 16 * 128], BF16, ats)
            wg2b = sb("wg2b", [128, D], BF16, ats)
            g1f = sb("g1f", [128, TP], BF16, ats)
            gsc = sb("gsc", [128, 512], F32, ats)
            u_tm2 = sb("u_tm2", [128, D], BF16, ats)

            def q_at(m, tj=None):
                c0 = m * TP
                return qk_fm[:, c0 + (0 if tj is None else tj * 128):
                             c0 + (TP if tj is None else (tj + 1) * 128)]

            def k_at(m, tj=None):
                c0 = 16 * TP + m * TP
                return qk_fm[:, c0 + (0 if tj is None else tj * 128):
                             c0 + (TP if tj is None else (tj + 1) * 128)]

            ogn_at = k_at

            def v_at(tj, m):
                return vk2[:, tj * D + m * 128:tj * D + (m + 1) * 128]

            def k2_at(tj, m):
                return vk2[:, 4 * D + tj * D + m * 128:4 * D + tj * D + (m + 1) * 128]

            ring_free = [(s_mm, 0), (s_mm, 0)]

            for sc_i in range(NPASS):
                t0 = sc_i * TP
                bank_free.clear()
                with nc.Block() as blk:
                    st_prev = s_st.n

                    @blk.gpsimd
                    def _(g):
                        if sc_i == 0:
                            g.memset(Sst[:], 0.0)
                            S_ready[0] = s_ev.inc(g.memset(Sbf[:], 0.0))
                        g.wait_ge(s_st.h, st_prev)
                        for j in range(4):
                            s_x.inc(g.dma_start(xacc[:, j * D:(j + 1) * D],
                                                x_src[t0 + j * 128:t0 + (j + 1) * 128, :]), 16)
                    xw = s_x.n

                    if sc_i == 0:
                        @blk.sync
                        def _(sy):
                            for dc in range(16):
                                s_w2l.inc(sy.dma_start(wg1b[:, dc * 128:(dc + 1) * 128],
                                                      wg1s[dc * 128:(dc + 1) * 128, :]), 16)
                            s_w2l.inc(sy.dma_start(wg2b[:], wg2s), 16)
                    wg_done = s_w2l.n

                    for j in range(4):
                        emit_ln_tile(blk, j, xw)

                    vstage_marks = []
                    for pi, (wsrc, kind) in enumerate([(wqs, "q"), (wks, "k"), (wvs, "v")]):
                        for mb in range(4):
                            slot = (pi * 4 + mb) % 2

                            wr = [s_wr0, s_wr1][slot]

                            @blk.sync
                            def _(sy):
                                s, nmark = ring_free[slot]
                                sy.wait_ge(s.h, nmark)
                                for dc in range(16):
                                    wr.inc(sy.dma_start(
                                        wring[:, slot * 8192 + dc * 512:slot * 8192 + (dc + 1) * 512],
                                        wsrc[dc * 128:(dc + 1) * 128, mb * 512:(mb + 1) * 512]), 16)
                            wl_mark = wr.n
                            sl_base = s_sl.n

                            @blk.tensor
                            def _(t):
                                t.wait_ge(wr.h, wl_mark)
                                if pi == 0 and mb == 0:
                                    t.wait_ge(s_te.h, s_te.n)
                                for mc in range(4):
                                    m = mb * 4 + mc
                                    pp = [pA, pB, pC, pD][mc]
                                    wait_bank(t, ("proj", mc))
                                    for dc in range(16):
                                        mi = t.matmul(pp[:],
                                                      wring[:, slot * 8192 + dc * 512 + mc * 128:
                                                            slot * 8192 + dc * 512 + (mc + 1) * 128],
                                                      u_fm[:, dc * TP:(dc + 1) * TP],
                                                      start=(dc == 0), stop=(dc == 15))
                                    s_mm.inc(mi)
                            ring_free[slot] = (s_mm, s_mm.n)
                            mm_marks = [s_mm.n - 3, s_mm.n - 2, s_mm.n - 1, s_mm.n]

                            if kind != "v":
                                @blk.scalar
                                def _(sc):
                                    for mc in range(4):
                                        m = mb * 4 + mc
                                        pp = [pA, pB, pC, pD][mc]
                                        sc.wait_ge(s_mm.h, mm_marks[mc])
                                        dst = q_at(m) if kind == "q" else k_at(m)
                                        s_sl.inc(sc.activation(dst, pp[:], AF.Silu))
                                        set_bank(("proj", mc), s_sl)
                            else:
                                te_base = s_te.n
                                tp_base = s_tp.n
                                for mc in range(4):
                                    m = mb * 4 + mc
                                    pp = [pA, pB, pC, pD][mc]

                                    @blk.scalar
                                    def _(sc):
                                        sc.wait_ge(s_mm.h, mm_marks[mc])
                                        if mc >= 2:
                                            sc.wait_ge(s_te.h, te_base + 4 * mc - 4)
                                        else:
                                            sc.wait_ge(s_te.h, te_base)
                                        s_sl.inc(sc.activation(
                                            u_tm[:, (mc % 2) * 512:(mc % 2 + 1) * 512],
                                            pp[:], AF.Silu))
                                        set_bank(("proj", mc), s_sl)
                                    silu_mc = s_sl.n

                                    @blk.tensor
                                    def _(t):
                                        t.wait_ge(s_sl.h, silu_mc)
                                        t.wait_ge(s_te.h, te_base)
                                        for tj in range(4):
                                            i = mc * 4 + tj
                                            bank = [pG, pH][i % 2]
                                            if i >= 2:
                                                t.wait_ge(s_te.h, te_base + i - 1)
                                            s_tp.inc(t.transpose(
                                                bank[:, 0:128],
                                                u_tm[:, (mc % 2) * 512 + tj * 128:(mc % 2) * 512 + (tj + 1) * 128],
                                                ident[:]))

                                    @blk.vector
                                    def _(v):
                                        for tj in range(4):
                                            i = mc * 4 + tj
                                            bank = [pG, pH][i % 2]
                                            v.wait_ge(s_tp.h, tp_base + i + 1)
                                            s_te.inc(v.tensor_copy(v_at(tj, m), bank[:, 0:128]))

                    @blk.tensor
                    def _(t):
                        if sc_i == 0:
                            t.wait_ge(s_w2l.h, wg_done)
                        wait_bank(t, ("proj", 0))
                        for dc in range(16):
                            mi = t.matmul(pA[:], wg1b[:, dc * 128:(dc + 1) * 128],
                                          u_fm[:, dc * TP:(dc + 1) * TP],
                                          start=(dc == 0), stop=(dc == 15))
                        s_mm.inc(mi)
                    g1_mark = s_mm.n

                    @blk.vector
                    def _(v):
                        v.wait_ge(s_mm.h, g1_mark)
                        s_ev.inc(v.tensor_copy(g1f[:], pA[:]))
                        set_bank(("proj", 0), s_ev)
                    g1f_done = s_ev.n
                    v_copies_done = s_te.n
                    ksilu_done = s_sl.n

                    u2 = [u_tm, u_tm2]
                    k2e_mark = {}
                    Lmm = {}
                    scT = {}
                    evT = {}
                    Tend = {}
                    Tcend = {}

                    def emit_k2(tj):
                        tpb = s_tp.n
                        slb = s_sl.n

                        @blk.tensor
                        def _(t):
                            if tj == 0:
                                t.wait_ge(s_te.h, v_copies_done)
                                t.wait_ge(s_sl.h, ksilu_done)
                            for h in range(H):
                                if h >= 2:
                                    t.wait_ge(s_sl.h, slb + h - 1)
                                else:
                                    wait_bank(t, ("T", h % 2))
                                s_tp.inc(t.transpose([pG, pH][h % 2][:, 0:128],
                                                     k_at(h, tj), ident[:]))

                        @blk.scalar
                        def _(sc):
                            for h in range(H):
                                sc.wait_ge(s_tp.h, tpb + h + 1)
                                s_sl.inc(sc.activation(k2_at(tj, h), [pG, pH][h % 2][:, 0:128],
                                                       AF.Identity, scale=kdec[:, h:h + 1]))
                                set_bank(("T", h % 2), s_sl)
                        k2e_mark[tj] = s_sl.n

                    def emit_L(tj):
                        mmT = s_mm.n
                        evTj = s_ev.n
                        scTj = s_sl.n
                        Lmm[tj] = mmT
                        evT[tj] = evTj
                        scT[tj] = scTj

                        @blk.tensor
                        def _(t):
                            t.wait_ge(s_sl.h, k2e_mark[tj])
                            if tj == 0:
                                t.wait_ge(s_ev.h, g1f_done)
                                if sc_i == 0:
                                    t.wait_ge(s_ev.h, S_ready[0])
                            else:
                                t.wait_ge(s_ev.h, evTj)
                            for h in range(H):
                                if h >= 1:
                                    t.wait_ge(s_ev.h, evTj + 4 * (h - 1) + 1)
                                mi = t.matmul(pA[:, 0:128], k_at(h, tj), q_at(h, tj),
                                              start=True, stop=True)
                                s_mm.inc(mi)
                                if h >= 1:
                                    t.wait_ge(s_sl.h, scTj + h)
                                mi = t.matmul(pB[:, 0:128], q_at(h, tj),
                                              Sbf[:, h * HD:(h + 1) * HD],
                                              start=True, stop=True)
                                s_mm.inc(mi)
                                t.wait_ge(s_ev.h, evTj + 4 * h + 1)
                                mi = t.matmul(pC[:, 0:128], sTbf[:, (h % 2) * 128:(h % 2 + 1) * 128],
                                              v_at(tj, h), start=True, stop=True)
                                s_mm.inc(mi)
                                if h >= 1:
                                    t.wait_ge(s_ev.h, evTj + 4 * (h - 1) + 3)
                                mi = t.matmul(pD[:, 0:128], k2_at(tj, h), v_at(tj, h),
                                              start=True, stop=True)
                                s_mm.inc(mi)
                            for ns in range(4):
                                if tj == 0 and ns == 0:
                                    t.wait_ge(s_ev.h, g1f_done)
                                if ns >= 2:
                                    t.wait_ge(s_ev.h, evTj + 64 + (ns - 2) + 1)
                                elif tj > 0:
                                    t.wait_ge(s_ev.h, evT[tj - 1] + 64 + 4)
                                else:
                                    wait_bank(t, ("gate", ns % 2))
                                s_mm.inc(t.matmul([pE_, pF_][ns % 2][:], g1f[:, tj * 128:(tj + 1) * 128],
                                                  wg2b[:, ns * 512:(ns + 1) * 512],
                                                  start=True, stop=True))

                    def emit_sc(tj):
                        scTj = scT[tj]
                        mmT = Lmm[tj]
                        evTj = evT[tj]

                        @blk.scalar
                        def _(sc):
                            for h in range(H):
                                sc.wait_ge(s_mm.h, mmT + 4 * h + 2)
                                if h >= 2:
                                    sc.wait_ge(s_ev.h, evTj + 4 * (h - 2) + 2)
                                s_sl.inc(sc.activation(o_tmp[:, (h % 2) * 128:(h % 2 + 1) * 128],
                                                       pB[:, 0:128], AF.Identity,
                                                       scale=qdec[:, h:h + 1]))
                            for ns in range(4):
                                sc.wait_ge(s_mm.h, mmT + 64 + ns + 1)
                                if ns >= 1:
                                    sc.wait_ge(s_ev.h, evTj + 64 + ns)
                                s_sl.inc(sc.activation(gsc[:], [pE_, pF_][ns % 2][:],
                                                       AF.Tanh, scale=0.5))
                            sc.wait_ge(s_ev.h, evTj + 64 + 4)
                            if tj >= 2:
                                sc.wait_ge(s_tp.h, s_tp.n + 16)
                            sc.drain()
                            for h in range(H):
                                sc.activation(u2[tj % 2][:, h * 128:(h + 1) * 128],
                                              o_tm[:, tj * D + h * 128:tj * D + (h + 1) * 128],
                                              AF.Square,
                                              accum_out=msq[:, tj * H + h:tj * H + h + 1])
                            sc.drain()
                            sc.activation(rsg[:, tj * H:(tj + 1) * H], msq[:, tj * H:(tj + 1) * H],
                                          AF.Ln, scale=1.0 / HD, bias=epst[:])
                            sc.drain()
                            s_sl.inc(sc.activation(rsg[:, tj * H:(tj + 1) * H],
                                                   rsg[:, tj * H:(tj + 1) * H], AF.Exp, scale=-0.5))
                            sc.drain()
                            for h in range(H):
                                mi = sc.activation(u2[tj % 2][:, h * 128:(h + 1) * 128],
                                                   o_tm[:, tj * D + h * 128:tj * D + (h + 1) * 128],
                                                   AF.Identity, scale=rsg[:, tj * H + h:tj * H + h + 1])
                            s_sl.inc(mi)

                    def emit_vec(tj):
                        scTj = scT[tj]
                        mmT = Lmm[tj]

                        @blk.vector
                        def _(v):
                            for h in range(H):
                                v.wait_ge(s_mm.h, mmT + 4 * h + 1)
                                s_ev.inc(v.tensor_tensor(sTbf[:, (h % 2) * 128:(h % 2 + 1) * 128],
                                                         pA[:, 0:128],
                                                         dt16[:, h * 128:(h + 1) * 128],
                                                         op=ALU.mult))
                                v.wait_ge(s_mm.h, mmT + 4 * h + 3)
                                v.wait_ge(s_sl.h, scTj + h + 1)
                                s_ev.inc(v.tensor_tensor(o_tm[:, tj * D + h * 128:tj * D + (h + 1) * 128],
                                                         pC[:, 0:128],
                                                         o_tmp[:, (h % 2) * 128:(h % 2 + 1) * 128],
                                                         op=ALU.add))
                                v.wait_ge(s_mm.h, mmT + 4 * h + 4)
                                s_ev.inc(v.scalar_tensor_tensor(Sst[:, h * HD:(h + 1) * HD],
                                                                Sst[:, h * HD:(h + 1) * HD],
                                                                cdec[:, h:h + 1],
                                                                pD[:, 0:128],
                                                                op0=ALU.mult, op1=ALU.add))
                                v.drain()
                                s_ev.inc(v.tensor_copy(Sbf[:, h * HD:(h + 1) * HD],
                                                       Sst[:, h * HD:(h + 1) * HD]))
                            for ns in range(4):
                                v.wait_ge(s_sl.h, scTj + 16 + ns + 1)
                                v.tensor_scalar(gsc[:], gsc[:], 0.5, 0.5,
                                                op0=ALU.mult, op1=ALU.add)
                                v.drain()
                                s_ev.inc(v.tensor_tensor(
                                    o_tm[:, tj * D + ns * 512:tj * D + (ns + 1) * 512],
                                    o_tm[:, tj * D + ns * 512:tj * D + (ns + 1) * 512],
                                    gsc[:], op=ALU.mult))

                    def emit_vecB(tj):
                        scTj = scT[tj]

                        @blk.vector
                        def _(v):
                            v.wait_ge(s_sl.h, scTj + 22)
                            s_ev.inc(v.tensor_tensor(u2[tj % 2][:], u2[tj % 2][:], nwb[:],
                                                     op=ALU.mult))

                    def emit_T(tj):
                        tpb = s_tp.n

                        teb = s_te.n

                        @blk.tensor
                        def _(t):
                            t.wait_ge(s_ev.h, evT[tj] + 69)
                            for dc in range(16):
                                if dc >= 2:
                                    t.wait_ge(s_te.h, teb + dc - 1)
                                else:
                                    wait_bank(t, ("T", dc % 2))
                                s_tp.inc(t.transpose([pG, pH][dc % 2][:, 0:128],
                                                     u2[tj % 2][:, dc * 128:(dc + 1) * 128],
                                                     ident[:]))
                        Tend[tj] = s_tp.n

                        @blk.vector
                        def _(v):
                            for dc in range(16):
                                v.wait_ge(s_tp.h, tpb + dc + 1)
                                s_te.inc(v.tensor_copy(ogn_at(dc, tj), [pG, pH][dc % 2][:, 0:128]))
                                set_bank(("T", dc % 2), s_te)
                        Tcend[tj] = s_te.n

                    emit_k2(0)
                    emit_k2(1)
                    emit_L(0); emit_sc(0); emit_vec(0); emit_vecB(0)
                    emit_k2(2)
                    emit_L(1); emit_sc(1); emit_vec(1); emit_vecB(1)
                    emit_k2(3)
                    emit_L(2); emit_sc(2); emit_vec(2)
                    emit_T(0)
                    emit_vecB(2)
                    emit_L(3); emit_sc(3); emit_vec(3)
                    emit_T(1)
                    emit_vecB(3)
                    emit_T(2)
                    emit_T(3)

                    for ns in range(4):
                        slot = ns % 2
                        wr = [s_wr0, s_wr1][slot]

                        @blk.sync
                        def _(sy):
                            s, nmark = ring_free[slot]
                            sy.wait_ge(s.h, nmark)
                            for dc in range(16):
                                wr.inc(sy.dma_start(
                                    wring[:, slot * 8192 + dc * 512:slot * 8192 + (dc + 1) * 512],
                                    wos[dc * 128:(dc + 1) * 128, ns * 512:(ns + 1) * 512]), 16)
                        wl_mark = wr.n

                        for tj in range(4):
                            k = ns * 4 + tj
                            pp = [pE_, pF_][k % 2]

                            @blk.tensor
                            def _(t):
                                if tj == 0:
                                    t.wait_ge(wr.h, wl_mark)
                                if ns == 0 and tj == 0:
                                    t.wait_ge(s_te.h, s_te.n)
                                wait_bank(t, ("gate", k % 2))
                                for dc in range(16):
                                    mi = t.matmul(pp[:], ogn_at(dc, tj),
                                                  wring[:, slot * 8192 + dc * 512:slot * 8192 + (dc + 1) * 512],
                                                  start=(dc == 0), stop=(dc == 15))
                                s_mm.inc(mi)
                            omm = s_mm.n

                            @blk.vector
                            def _(v):
                                v.wait_ge(s_mm.h, omm)
                                xs_ = xacc[:, tj * D + ns * 512:tj * D + (ns + 1) * 512]
                                s_ev.inc(v.tensor_tensor(xs_, xs_, pp[:], op=ALU.add))
                                set_bank(("gate", k % 2), s_ev)
                        ring_free[slot] = (s_mm, s_mm.n)

                    fin = s_ev.n

                    @blk.gpsimd
                    def _(g):
                        g.wait_ge(s_ev.h, fin)
                        for j in range(4):
                            s_st.inc(g.dma_start(x_dst[t0 + j * 128:t0 + (j + 1) * 128, :],
                                                 xacc[:, j * D:(j + 1) * D]), 16)
            ats.close()

        run_ffn_phase(x_in, x1s if STAGE > 1 else y_out, True)

        if STAGE > 1:
            emit_attention(x1s, x2s if STAGE > 2 else y_out)

        if STAGE > 2:
            run_ffn_phase(x2s, y_out, False)

        with nc.Block() as blk:
            @blk.gpsimd
            def _(g):
                g.wait_ge(s_st.h, s_st.n)

    return nc


def _consts():
    ident = np.eye(128, dtype=np.float32)
    import ml_dtypes
    ident = ident.astype(ml_dtypes.bfloat16)
    ones = np.ones((1, 128), dtype=np.float32)
    jj = np.arange(128, dtype=np.float32)
    diff = jj[None, :] - jj[:, None]
    difft = np.where(diff >= 0, diff, 1e9).astype(np.float32)
    cj1 = (jj + 1.0).reshape(128, 1).astype(np.float32)
    cj127 = (127.0 - jj).reshape(128, 1).astype(np.float32)
    return ident, ones, difft, cj1, cj127


def kernel(L0_attn_x, L0_attn_log_decay, L0_attn_Wq, L0_attn_Wk, L0_attn_Wv,
           L0_attn_Wo, L0_attn_Wg1, L0_attn_Wg2, L0_attn_norm_w,
           L0_ffn_gate_w, L0_ffn_gate_b, L0_ffn_W1, L0_ffn_W2, L0_ffn_W3,
           L0_ffn_Ws1, L0_ffn_Ws2, L0_ffn_Ws3):
    ident, ones, difft, cj1, cj127 = _consts()
    x = np.ascontiguousarray(np.asarray(L0_attn_x, dtype=np.float32))
    shared = {
        "ld_in": np.asarray(L0_attn_log_decay, dtype=np.float32).reshape(1, H),
        "wq_in": np.ascontiguousarray(L0_attn_Wq, dtype=np.float32),
        "wk_in": np.ascontiguousarray(L0_attn_Wk, dtype=np.float32),
        "wv_in": np.ascontiguousarray(L0_attn_Wv, dtype=np.float32),
        "wo_in": np.ascontiguousarray(L0_attn_Wo, dtype=np.float32),
        "wg1_in": np.ascontiguousarray(L0_attn_Wg1, dtype=np.float32),
        "wg2_in": np.ascontiguousarray(L0_attn_Wg2, dtype=np.float32),
        "nw_in": np.asarray(L0_attn_norm_w, dtype=np.float32).reshape(1, D),
        "gw_in": np.ascontiguousarray(L0_ffn_gate_w, dtype=np.float32),
        "gb_in": np.asarray(L0_ffn_gate_b, dtype=np.float32).reshape(1, E),
        "w1_in": np.ascontiguousarray(L0_ffn_W1, dtype=np.float32).reshape(E * I, D),
        "w2_in": np.ascontiguousarray(L0_ffn_W2, dtype=np.float32).reshape(E * D, I),
        "w3_in": np.ascontiguousarray(L0_ffn_W3, dtype=np.float32).reshape(E * I, D),
        "ws1_in": np.ascontiguousarray(L0_ffn_Ws1, dtype=np.float32),
        "ws2_in": np.ascontiguousarray(L0_ffn_Ws2, dtype=np.float32),
        "ws3_in": np.ascontiguousarray(L0_ffn_Ws3, dtype=np.float32),
        "ident_in": ident, "ones_in": ones, "difft_in": difft,
        "cj1_in": cj1, "cj127_in": cj127,
    }
    nc = build()
    in_maps = [dict(shared, x_in=np.ascontiguousarray(x[b])) for b in range(8)]
    res = run_bass_kernel_spmd(nc, in_maps, core_ids=list(range(8)))
    out = np.stack([np.asarray(r["y_out"], dtype=np.float32) for r in res.results], axis=0)
    return out


# version 09
# speedup vs baseline: 1.2097x; 1.0449x over previous
import contextlib
import numpy as np
import concourse.bass as bass
import concourse.mybir as mybir
from concourse.bass_utils import run_bass_kernel_spmd

F32 = mybir.dt.float32
BF16 = mybir.dt.bfloat16
AF = mybir.ActivationFunctionType
ALU = mybir.AluOpType
AX = mybir.AxisListType

N = 4096
D = 2048
H = 16
HD = 128
E = 8
I = 1024
TP = 512
NPASS = N // TP
EPS = 1e-5
NEG = -1e30
STAGE = 3


class Sem:
    def __init__(self, handle):
        self.h = handle
        self.n = 0

    def inc(self, instr, k=1):
        instr.then_inc(self.h, k)
        self.n += k
        return self.n

    def wait(self, eng, n=None):
        eng.wait_ge(self.h, self.n if n is None else n)


def build():
    nc = bass.Bass("TRN2", target_bir_lowering=False, debug=False)

    x_in = nc.dram_tensor("x_in", [N, D], F32, kind="ExternalInput").ap()
    ld_in = nc.dram_tensor("ld_in", [1, H], F32, kind="ExternalInput").ap()
    wq_in = nc.dram_tensor("wq_in", [D, D], F32, kind="ExternalInput").ap()
    wk_in = nc.dram_tensor("wk_in", [D, D], F32, kind="ExternalInput").ap()
    wv_in = nc.dram_tensor("wv_in", [D, D], F32, kind="ExternalInput").ap()
    wo_in = nc.dram_tensor("wo_in", [D, D], F32, kind="ExternalInput").ap()
    wg1_in = nc.dram_tensor("wg1_in", [D, HD], F32, kind="ExternalInput").ap()
    wg2_in = nc.dram_tensor("wg2_in", [HD, D], F32, kind="ExternalInput").ap()
    nw_in = nc.dram_tensor("nw_in", [1, D], F32, kind="ExternalInput").ap()
    gw_in = nc.dram_tensor("gw_in", [E, D], F32, kind="ExternalInput").ap()
    gb_in = nc.dram_tensor("gb_in", [1, E], F32, kind="ExternalInput").ap()
    w1_in = nc.dram_tensor("w1_in", [E * I, D], F32, kind="ExternalInput").ap()
    w2_in = nc.dram_tensor("w2_in", [E * D, I], F32, kind="ExternalInput").ap()
    w3_in = nc.dram_tensor("w3_in", [E * I, D], F32, kind="ExternalInput").ap()
    ws1_in = nc.dram_tensor("ws1_in", [I, D], F32, kind="ExternalInput").ap()
    ws2_in = nc.dram_tensor("ws2_in", [D, I], F32, kind="ExternalInput").ap()
    ws3_in = nc.dram_tensor("ws3_in", [I, D], F32, kind="ExternalInput").ap()
    ident_in = nc.dram_tensor("ident_in", [128, 128], BF16, kind="ExternalInput").ap()
    ones_in = nc.dram_tensor("ones_in", [1, 128], F32, kind="ExternalInput").ap()
    difft_in = nc.dram_tensor("difft_in", [128, 128], F32, kind="ExternalInput").ap()
    cj1_in = nc.dram_tensor("cj1_in", [128, 1], F32, kind="ExternalInput").ap()
    cj127_in = nc.dram_tensor("cj127_in", [128, 1], F32, kind="ExternalInput").ap()

    y_out = nc.dram_tensor("y_out", [N, D], F32, kind="ExternalOutput").ap()

    x1s = nc.dram_tensor("x1s", [N, D], F32, kind="Internal").ap()
    x2s = nc.dram_tensor("x2s", [N, D], F32, kind="Internal").ap()
    w1ts = nc.dram_tensor("w1ts", [9 * D, I], BF16, kind="Internal").ap()
    w3ts = nc.dram_tensor("w3ts", [9 * D, I], BF16, kind="Internal").ap()
    w2ts = nc.dram_tensor("w2ts", [9 * I, D], BF16, kind="Internal").ap()

    es = contextlib.ExitStack()
    with es:
        def sb(name, shape, dt=F32, stack=None):
            return (stack or es).enter_context(nc.sbuf_tensor(name, shape, dt))

        def psum(name, shape, dt=F32):
            return es.enter_context(nc.psum_tensor(name, shape, dt))

        def sem(name):
            return Sem(es.enter_context(nc.semaphore(name)))

        ident = sb("ident", [128, 128], BF16)
        onesf = sb("onesf", [1, 128], F32)
        epst = sb("epst", [128, 1], F32)
        dwS = sb("dwS", [128, 1], F32)
        gbb = sb("gbb", [128, E], F32)
        gwT = sb("gwT", [128, 16 * E], BF16)
        nwb = sb("nwb", [128, D], BF16)
        ldb = sb("ldb", [128, H], F32)
        dt16 = sb("dt16", [128, H * 128], BF16)
        qdec = sb("qdec", [128, H], F32)
        kdec = sb("kdec", [128, H], F32)
        cdec = sb("cdec", [128, H], F32)
        xacc = sb("xacc", [128, 4 * D], F32)
        u_fm = sb("u_fm", [128, 16 * TP], BF16)
        u_tm = sb("u_tm", [128, D], BF16)
        dwt = sb("dwt", [128, 4 * E], F32)
        lgs4 = sb("lgs4", [128, 4 * E], F32)
        sm4 = sb("sm4", [128, 4 * E], F32)
        mg4 = sb("mg4", [128, 4], F32)
        rg4 = sb("rg4", [128, 4], F32)
        lgs = sb("lgs", [128, E], F32)
        msk = sb("msk", [128, E], F32)
        lg2 = sb("lg2", [128, E], F32)
        smv = sb("smv", [128, E], F32)
        sm1 = sb("sm1", [128, 1], F32)
        sm1n = sb("sm1n", [128, 1], F32)
        sm2 = sb("sm2", [128, 1], F32)
        bnb = sb("bnb", [128, 24], F32)
        mvb = sb("mvb", [128, 2], F32)
        rstd = sb("rstd", [128, 1], F32)
        negb = sb("negb", [128, 1], F32)
        sTbf = sb("sTbf", [128, 2 * 128], BF16)
        o_tmp = sb("o_tmp", [128, 2 * 128], F32)
        msq = sb("msq", [128, 4 * H], F32)
        rsg = sb("rsg", [128, 4 * H], F32)
        sqd = sb("sqd", [128, 128], F32)

        pA = psum("pA", [128, 512], F32)
        pB = psum("pB", [128, 512], F32)
        pC = psum("pC", [128, 512], F32)
        pD = psum("pD", [128, 512], F32)
        pE_ = psum("pE_", [128, 512], F32)
        pF_ = psum("pF_", [128, 512], F32)
        pG = psum("pG", [128, 1024], BF16)
        pH = psum("pH", [128, 1024], BF16)

        s_cin = sem("s_cin")
        s_cinO = sem("s_cinO")
        s_x = sem("s_x")
        s_wl = sem("s_wl")
        s_w3l = sem("s_w3l")
        s_w2l = sem("s_w2l")
        s_mm = sem("s_mm")
        s_ev = sem("s_ev")
        s_sl = sem("s_sl")
        s_st = sem("s_st")
        s_tp = sem("s_tp")
        s_te = sem("s_te")
        s_wr0 = sem("s_wr0")
        s_wr1 = sem("s_wr1")
        s_p0c = sem("s_p0c")
        s_p0cO = sem("s_p0cO")

        bank_free = {}

        def wait_bank(eng, bank):
            if bank in bank_free:
                s, n = bank_free[bank]
                eng.wait_ge(s.h, n)

        def set_bank(bank, s):
            bank_free[bank] = (s, s.n)

        su = contextlib.ExitStack()
        gwsb = sb("gwsb", [E, D], BF16, su)
        ldrow = sb("ldrow", [1, H], F32, su)
        nwrow = sb("nwrow", [1, D], F32, su)
        gbrow = sb("gbrow", [1, E], F32, su)
        difft = sb("difft", [128, 128], F32, su)
        cj1 = sb("cj1", [128, 1], F32, su)
        cj127 = sb("cj127", [128, 1], F32, su)

        with nc.Block() as blk:
            @blk.gpsimd
            def _(g):
                for dst, srca in [(ident, ident_in), (onesf, ones_in), (ldrow, ld_in),
                                  (nwrow, nw_in), (gbrow, gb_in), (difft, difft_in),
                                  (cj1, cj1_in), (cj127, cj127_in), (gwsb, gw_in)]:
                    s_x.inc(g.dma_start(dst[:], srca), 16)
                g.memset(epst[:], EPS)
                g.memset(dwS[:], 0.5)

            @blk.vector
            def _(v):
                s_x.wait(v)

        gw_te = 0
        with nc.Block() as blk:
            gw_te = s_te.n
            gw_tp = s_tp.n
            @blk.tensor
            def _(t):
                s_mm.inc(t.matmul(pA[:, 0:E], onesf[:], gbrow[:], start=True, stop=True))
                s_mm.inc(t.matmul(pA[:, 16:16 + H], onesf[:], ldrow[:], start=True, stop=True))
                for i in range(4):
                    s_mm.inc(t.matmul([pB, pC, pD, pE_][i][:], onesf[:],
                                      nwrow[:, i * 512:(i + 1) * 512], start=True, stop=True))
                t.wait_ge(s_te.h, gw_te)
                for dc in range(16):
                    dstp = [pG, pH][dc % 2][:, 0:8]
                    if dc >= 2:
                        t.wait_ge(s_te.h, gw_te + dc - 1)
                    s_tp.inc(t.transpose(dstp, gwsb[0:E, dc * 128:(dc + 1) * 128], ident[0:E, 0:E]))

            @blk.vector
            def _(v):
                s_mm.wait(v)
                v.tensor_copy(gbb[:], pA[:, 0:E])
                v.tensor_copy(ldb[:], pA[:, 16:16 + H])
                for i in range(4):
                    ni = v.tensor_copy(nwb[:, i * 512:(i + 1) * 512], [pB, pC, pD, pE_][i][:])
                s_ev.inc(ni)
                for dc in range(16):
                    srcp = [pG, pH][dc % 2][:, 0:8]
                    v.wait_ge(s_tp.h, gw_tp + dc + 1)
                    ci = v.tensor_copy(gwT[:, dc * E:(dc + 1) * E], srcp)
                    s_te.inc(ci)

            @blk.scalar
            def _(sc):
                s_te.wait(sc)
                s_ev.wait(sc)
                for h in range(H):
                    sc.activation(dt16[:, h * 128:(h + 1) * 128], difft[:], AF.Exp,
                                  scale=ldb[:, h:h + 1])
                    sc.activation(qdec[:, h:h + 1], cj1[:], AF.Exp, scale=ldb[:, h:h + 1])
                    sc.activation(kdec[:, h:h + 1], cj127[:], AF.Exp, scale=ldb[:, h:h + 1])
                    sc.activation(cdec[:, h:h + 1], ldb[:, h:h + 1], AF.Exp, scale=float(HD))
        su.close()

        p0c_marks = {e: [] for e in range(9)}
        chunk_tp_done = {}
        p0c = contextlib.ExitStack()
        natb = [sb("natb0", [128, D], BF16, p0c), sb("natb1", [128, D], BF16, p0c)]
        asmb = [sb("asmb0", [128, 16 * I], BF16, p0c),
                sb("asmb1", [128, 16 * I], BF16, p0c)]
        store_sems = [s_p0c, s_p0cO]
        asm_free = [0, 0]
        with nc.Block() as blk:
            groups = []
            for e in range(9):
                w1src = w1_in[e * I:(e + 1) * I, :] if e < E else ws1_in
                w3src = w3_in[e * I:(e + 1) * I, :] if e < E else ws3_in
                w2src = w2_in[e * D:(e + 1) * D, :] if e < E else ws2_in
                groups.append((e, "w1", w1src, 8, D, 16, I, w1ts, e * 16))
                groups.append((e, "w3", w3src, 8, D, 16, I, w3ts, e * 16))
                groups.append((e, "w2", w2src, 16, I, 8, D, w2ts, e * 8))
            tp0 = s_tp.n
            cum = tp0
            for (_, _, _, nch_, _, och_, _, _, _) in groups:
                for c in range(nch_):
                    chunk_tp_done[len(chunk_tp_done)] = cum + och_ * (c + 1)
                cum += nch_ * och_
            for gi, (e, nm, fsrc, nch, ncol, och, ocol, tdst, orow0) in enumerate(groups):
                tp_base = s_tp.n
                te_base = s_te.n
                ld_marks = []
                qsel = gi % 2
                asmX = asmb[qsel]
                ssem = store_sems[qsel]

                @blk.gpsimd
                def _(g):
                    for c in range(nch):
                        t_idx = sum(x[3] for x in groups[:gi]) + c
                        if t_idx >= 2:
                            g.wait_ge(s_tp.h, chunk_tp_done[t_idx - 2])
                        cin = [s_cin, s_cinO][t_idx % 2]
                        cin.inc(g.dma_start(natb[t_idx % 2][:, 0:ncol],
                                            fsrc[c * 128:(c + 1) * 128, :]), 16)
                        ld_marks.append((cin, cin.n))

                @blk.tensor
                def _(t):
                    t.wait_ge(s_te.h, te_base)
                    for c in range(nch):
                        t_idx = sum(x[3] for x in groups[:gi]) + c
                        cs, cn = ld_marks[c]
                        t.wait_ge(cs.h, cn)
                        for k in range(och):
                            i = c * och + k
                            bank = [pG, pH][i % 2]
                            if i >= 2:
                                t.wait_ge(s_te.h, te_base + i - 1)
                            s_tp.inc(t.transpose(bank[:, 0:128],
                                                 natb[t_idx % 2][:, k * 128:(k + 1) * 128],
                                                 ident[:]))

                @blk.vector
                def _(v):
                    v.wait_ge(ssem.h, asm_free[qsel])
                    for c in range(nch):
                        for k in range(och):
                            i = c * och + k
                            bank = [pG, pH][i % 2]
                            v.wait_ge(s_tp.h, tp_base + i + 1)
                            s_te.inc(v.tensor_copy(
                                asmX[:, k * ocol + c * 128:k * ocol + (c + 1) * 128],
                                bank[:, 0:128]))
                ev_end = s_te.n

                def _store(eng):
                    eng.wait_ge(s_te.h, ev_end)
                    for k in range(och):
                        ssem.inc(eng.dma_start(
                            tdst[(orow0 + k) * 128:(orow0 + k + 1) * 128, :],
                            asmX[:, k * ocol:(k + 1) * ocol]), 16)

                if qsel == 0:
                    @blk.sync
                    def _(sy):
                        _store(sy)
                else:
                    @blk.scalar
                    def _(sc):
                        _store(sc)
                asm_free[qsel] = ssem.n
                p0c_marks[e].append((ssem, ssem.n))
        p0c.close()
        p0c_all = [(s, s.n) for s in store_sems]

        def emit_ln_tile(blk, j, xw_done):
            @blk.vector
            def _(v):
                v.wait_ge(s_x.h, xw_done)
                if j > 0:
                    v.wait_ge(s_sl.h, s_sl.n)
                for k in range(4):
                    v.bn_stats(bnb[:, k * 6:(k + 1) * 6],
                               xacc[:, j * D + k * 512:j * D + (k + 1) * 512])
                v.drain()
                s_ev.inc(v.bn_aggr(mvb[:], bnb[:]))
                v.drain()
            stats_j = s_ev.n

            @blk.scalar
            def _(sc):
                sc.wait_ge(s_ev.h, stats_j)
                sc.activation(rstd[:], mvb[:, 1:2], AF.Ln, bias=epst[:])
                sc.drain()
                s_sl.inc(sc.activation(rstd[:], rstd[:], AF.Exp, scale=-0.5))
            rstd_j = s_sl.n

            @blk.vector
            def _(v):
                v.wait_ge(s_sl.h, rstd_j)
                v.drain()
                s_ev.inc(v.tensor_scalar(negb[:], mvb[:, 0:1], rstd[:], -1.0,
                                         op0=ALU.mult, op1=ALU.mult))
            negb_j = s_ev.n

            @blk.scalar
            def _(sc):
                sc.wait_ge(s_ev.h, negb_j)
                sc.wait_ge(s_te.h, s_te.n)
                s_sl.inc(sc.activation(u_tm[:], xacc[:, j * D:(j + 1) * D], AF.Identity,
                                       scale=rstd[:], bias=negb[:]))
            u_j = s_sl.n

            te_base = s_te.n
            tp_base = s_tp.n

            @blk.tensor
            def _(t):
                t.wait_ge(s_sl.h, u_j)
                t.wait_ge(s_te.h, te_base)
                for dc in range(16):
                    bank = [pG, pH][dc % 2]
                    if dc >= 2:
                        t.wait_ge(s_te.h, te_base + dc - 1)
                    s_tp.inc(t.transpose(bank[:, 0:128], u_tm[:, dc * 128:(dc + 1) * 128],
                                         ident[:]))

            @blk.vector
            def _(v):
                for dc in range(16):
                    bank = [pG, pH][dc % 2]
                    v.wait_ge(s_tp.h, tp_base + dc + 1)
                    s_te.inc(v.tensor_copy(u_fm[:, dc * TP + j * 128:dc * TP + (j + 1) * 128],
                                           bank[:, 0:128]))

        def emit_b1(fb, blk, x_src, p, X, dwtX, ufm_free, gate_free):
            t0 = p * TP
            st_prev = st_done[p % 2]

            @blk.sync
            def _(sy):
                sy.wait_ge(s_st.h, st_prev)
                for j in range(4):
                    s_wr0.inc(sy.dma_start(X[:, j * D:(j + 1) * D],
                                           x_src[t0 + j * 128:t0 + (j + 1) * 128, :]), 16)
            xw = s_wr0.n
            ev0 = s_ev.n
            sl0 = s_sl.n
            te0 = s_te.n
            tp0 = s_tp.n
            mm0 = s_mm.n
            stats_j = lambda j: ev0 + 2 * j + 1
            negb_j = lambda j: ev0 + 2 * j + 2
            sub4_mark = ev0 + 10
            dwt_mark = ev0 + 11
            rstd_j = lambda j: sl0 + 2 * j + 1
            apply_j = lambda j: sl0 + 2 * j + 2
            exp4_mark = sl0 + 9
            logit_j = lambda j: mm0 + j + 1

            @blk.scalar
            def _(sc):
                for j in range(4):
                    sc.wait_ge(s_ev.h, stats_j(j))
                    sc.activation(rstd[:], mvb[:, 1:2], AF.Ln, bias=epst[:])
                    sc.drain()
                    s_sl.inc(sc.activation(rstd[:], rstd[:], AF.Exp, scale=-0.5))
                    sc.wait_ge(s_ev.h, negb_j(j))
                    sc.wait_ge(s_te.h, te0 + 16 * j)
                    s_sl.inc(sc.activation(u_tm[:], X[:, j * D:(j + 1) * D],
                                           AF.Identity, scale=rstd[:], bias=negb[:]))
                sc.wait_ge(s_ev.h, sub4_mark)
                s_sl.inc(sc.activation(sm4[:], sm4[:], AF.Exp))

            @blk.tensor
            def _(t):
                for j in range(4):
                    t.wait_ge(s_sl.h, apply_j(j))
                    for dc in range(16):
                        bank = [pG, pH][dc % 2]
                        t.wait_ge(s_te.h, te0 + 16 * j + max(0, dc - 1))
                        s_tp.inc(t.transpose(bank[:, 0:128],
                                             u_tm[:, dc * 128:(dc + 1) * 128], ident[:]))
                    if j == 0:
                        t.wait_ge(s_ev.h, gate_free)
                    t.wait_ge(s_te.h, te0 + 16 * (j + 1))
                    for dc in range(16):
                        mi = t.matmul(pE_[:, j * E:(j + 1) * E],
                                      u_fm[:, dc * TP + j * 128:dc * TP + (j + 1) * 128],
                                      gwT[:, dc * E:(dc + 1) * E],
                                      start=(dc == 0), stop=(dc == 15))
                    s_mm.inc(mi)

            @blk.vector
            def _(v):
                v.wait_ge(s_wr0.h, xw)
                for j in range(4):
                    if j > 0:
                        v.wait_ge(s_sl.h, rstd_j(j - 1))
                    for k in range(4):
                        v.bn_stats(bnb[:, k * 6:(k + 1) * 6],
                                   X[:, j * D + k * 512:j * D + (k + 1) * 512])
                    v.drain()
                    s_ev.inc(v.bn_aggr(mvb[:], bnb[:]))
                    v.drain()
                    v.wait_ge(s_sl.h, rstd_j(j))
                    v.drain()
                    s_ev.inc(v.tensor_scalar(negb[:], mvb[:, 0:1], rstd[:], -1.0,
                                             op0=ALU.mult, op1=ALU.mult))
                    if j == 0:
                        v.wait_ge(s_mm.h, ufm_free)
                    for dc in range(16):
                        bank = [pG, pH][dc % 2]
                        v.wait_ge(s_tp.h, tp0 + 16 * j + dc + 1)
                        s_te.inc(v.tensor_copy(
                            u_fm[:, dc * TP + j * 128:dc * TP + (j + 1) * 128],
                            bank[:, 0:128]))
                gexp = lambda a: a.rearrange("p (g e) -> p g e", e=E)
                gbx = lambda a: a.rearrange("p (g one) -> p g one", one=1).broadcast_to((128, 4, E))
                v.wait_ge(s_mm.h, logit_j(3))
                s_ev.inc(v.tensor_copy(lgs4[:], pE_[:, 0:4 * E]))
                v.drain()
                v.tensor_reduce(mg4[:], gexp(lgs4[:]), axis=AX.X, op=ALU.max)
                v.drain()
                s_ev.inc(v.tensor_tensor(gexp(sm4[:]), gexp(lgs4[:]), gbx(mg4[:]),
                                         op=ALU.subtract))
                v.drain()
                v.wait_ge(s_sl.h, exp4_mark)
                v.tensor_reduce(mg4[:], gexp(sm4[:]), axis=AX.X, op=ALU.add)
                v.drain()
                v.reciprocal(rg4[:], mg4[:])
                v.drain()
                v.tensor_tensor(gexp(sm4[:]), gexp(sm4[:]), gbx(rg4[:]), op=ALU.mult)
                v.drain()
                v.tensor_tensor(gexp(lgs4[:]), gexp(sm4[:]),
                                gbb[:].rearrange("p (one e) -> p one e", one=1)
                                .broadcast_to((128, 4, E)), op=ALU.add)
                v.drain()
                v.tensor_reduce(mg4[:], gexp(lgs4[:]), axis=AX.X, op=ALU.max)
                v.drain()
                v.tensor_tensor(gexp(dwtX[:]), gexp(lgs4[:]), gbx(mg4[:]), op=ALU.is_ge)
                v.drain()
                v.scalar_tensor_tensor(dwtX[:], dwtX[:], NEG, lgs4[:],
                                       op0=ALU.mult, op1=ALU.add)
                v.drain()
                v.tensor_reduce(mg4[:], gexp(dwtX[:]), axis=AX.X, op=ALU.max)
                v.drain()
                v.tensor_tensor(gexp(dwtX[:]), gexp(lgs4[:]), gbx(mg4[:]), op=ALU.is_ge)
                v.drain()
                v.tensor_tensor(dwtX[:], dwtX[:], sm4[:], op=ALU.mult)
                v.drain()
                s_ev.inc(v.tensor_scalar(dwtX[:], dwtX[:], 0.5, None, op0=ALU.mult))

        def emit_b2(fb, blk, x_dst, p, X, dwtX, first_pass, prev_hend, prev_yend,
                    phase_gate):
            w13, w2t, h_e, asl = fb["w13"], fb["w2t"], fb["h_e"], fb["asl"]
            t0 = p * TP
            mmB = s_mm.n
            slB = s_sl.n
            evB = s_ev.n
            wlB = s_wl.n
            w3lB = s_w3l.n
            w2lB = s_w2l.n
            h1_mark = lambda e, ic: mmB + 32 * e + 2 * ic + 1
            h3_mark = lambda e, ic: mmB + 32 * e + 2 * ic + 2
            hend = lambda e: mmB + 32 * e + 16
            y_mark = lambda e, k: mmB + 32 * e + 16 + k + 1
            yend = lambda e: mmB + 32 * (e + 1)
            silu_mark = lambda e, ic: slB + 8 * e + ic + 1
            hmul_mark = lambda e, ic: evB + 24 * e + ic + 1
            hmul_end = lambda e: evB + 24 * e + 8
            yev_mark = lambda e, k: evB + 24 * e + 8 + k + 1
            yev_end = lambda e: evB + 24 * (e + 1)
            wl_mark = lambda e: wlB + 256 * (e + 1)
            w3l_mark = lambda e: w3lB + 256 * (e + 1)
            w2l_mark = lambda e: w2lB + 128 * (e + 1)

            @blk.sync
            def _(sy):
                for e in range(9):
                    if e == 0 and phase_gate is not None:
                        for sm_, n_ in phase_gate:
                            sy.wait_ge(sm_.h, n_)
                    if e >= 1:
                        sy.wait_ge(s_mm.h, hend(e - 1))
                    elif prev_hend is not None:
                        sy.wait_ge(s_mm.h, prev_hend)
                    for dc in range(16):
                        s_wl.inc(sy.dma_start(w13[:, dc * I:(dc + 1) * I],
                                              w1ts[(e * 16 + dc) * 128:(e * 16 + dc + 1) * 128, :]), 16)
                    if e >= 1:
                        sy.wait_ge(s_mm.h, yend(e - 1))
                    elif prev_yend is not None:
                        sy.wait_ge(s_mm.h, prev_yend)
                    for ic in range(8):
                        s_w2l.inc(sy.dma_start(w2t[:, ic * D:(ic + 1) * D],
                                               w2ts[(e * 8 + ic) * 128:(e * 8 + ic + 1) * 128, :]), 16)

            @blk.gpsimd
            def _(g):
                for e in range(9):
                    if e == 0 and phase_gate is not None:
                        for sm_, n_ in phase_gate:
                            g.wait_ge(sm_.h, n_)
                    if e >= 1:
                        g.wait_ge(s_mm.h, hend(e - 1))
                    elif prev_hend is not None:
                        g.wait_ge(s_mm.h, prev_hend)
                    for dc in range(16):
                        s_w3l.inc(g.dma_start(w13[:, (16 + dc) * I:(17 + dc) * I],
                                              w3ts[(e * 16 + dc) * 128:(e * 16 + dc + 1) * 128, :]), 16)

            @blk.tensor
            def _(t):
                for e in range(9):
                    t.wait_ge(s_wl.h, wl_mark(e))
                    t.wait_ge(s_w3l.h, w3l_mark(e))
                    if e == 0:
                        t.wait_ge(s_ev.h, evB)
                    for ic in range(8):
                        pa = [pA, pC][ic % 2]
                        pb = [pB, pD][ic % 2]
                        if ic >= 2:
                            t.wait_ge(s_ev.h, hmul_mark(e, ic - 2))
                        elif e >= 1:
                            t.wait_ge(s_ev.h, yev_end(e - 1))
                        for dc in range(16):
                            mi = t.matmul(pa[:], w13[:, dc * I + ic * 128:dc * I + (ic + 1) * 128],
                                          u_fm[:, dc * TP:(dc + 1) * TP],
                                          start=(dc == 0), stop=(dc == 15))
                        s_mm.inc(mi)
                        for dc in range(16):
                            mi = t.matmul(pb[:], w13[:, (16 + dc) * I + ic * 128:(16 + dc) * I + (ic + 1) * 128],
                                          u_fm[:, dc * TP:(dc + 1) * TP],
                                          start=(dc == 0), stop=(dc == 15))
                        s_mm.inc(mi)
                    t.wait_ge(s_w2l.h, w2l_mark(e))
                    t.wait_ge(s_ev.h, hmul_end(e))
                    for k, (tb, d2s) in enumerate([(a, b) for a in range(4) for b in range(4)]):
                        py = [pE_, pF_][k % 2]
                        if k >= 2:
                            t.wait_ge(s_ev.h, yev_mark(e, k - 2))
                        elif e >= 1:
                            t.wait_ge(s_ev.h, yev_end(e - 1))
                        for ic in range(8):
                            mi = t.matmul(py[:],
                                          h_e[:, ic * TP + tb * 128:ic * TP + (tb + 1) * 128],
                                          w2t[:, ic * D + d2s * 512:ic * D + (d2s + 1) * 512],
                                          start=(ic == 0), stop=(ic == 7))
                        s_mm.inc(mi)

            @blk.scalar
            def _(sc):
                for e in range(9):
                    for ic in range(8):
                        pa = [pA, pC][ic % 2]
                        sc.wait_ge(s_mm.h, h1_mark(e, ic))
                        if ic >= 2:
                            sc.wait_ge(s_ev.h, hmul_mark(e, ic - 2))
                        elif e >= 1:
                            sc.wait_ge(s_ev.h, hmul_mark(e - 1, 7))
                        s_sl.inc(sc.activation(asl[:, (ic % 2) * TP:(ic % 2 + 1) * TP],
                                               pa[:], AF.Silu))

            @blk.vector
            def _(v):
                for e in range(9):
                    for ic in range(8):
                        pb = [pB, pD][ic % 2]
                        v.wait_ge(s_sl.h, silu_mark(e, ic))
                        v.wait_ge(s_mm.h, h3_mark(e, ic))
                        s_ev.inc(v.tensor_tensor(h_e[:, ic * TP:(ic + 1) * TP],
                                                 asl[:, (ic % 2) * TP:(ic % 2 + 1) * TP],
                                                 pb[:], op=ALU.mult))
                    for k, (tb, d2s) in enumerate([(a, b) for a in range(4) for b in range(4)]):
                        py = [pE_, pF_][k % 2]
                        v.wait_ge(s_mm.h, y_mark(e, k))
                        dcol = dwS[:] if e == E else dwtX[:, tb * E + e:tb * E + e + 1]
                        xs_ = X[:, tb * D + d2s * 512:tb * D + (d2s + 1) * 512]
                        s_ev.inc(v.scalar_tensor_tensor(xs_, py[:], dcol, xs_,
                                                        op0=ALU.mult, op1=ALU.add))

            fin = s_ev.n
            return fin, hend(8), yend(8), yev_end(8)

        aw_free = {}

        st_done = [0, 0]

        def run_ffn_phase(x_src, x_dst, first):
            if first:
                phase_gate = p0c_all
            else:
                phase_gate = [(s_mm, s_mm.n), (s_ev, s_ev.n)]
            st_done[0] = s_st.n
            st_done[1] = s_st.n
            sfx = "a" if first else "b"
            fs = contextlib.ExitStack()
            fb = {
                "w13": sb("w13" + sfx, [128, 32 * I], BF16, fs),
                "w2t": sb("w2t" + sfx, [128, 8 * D], BF16, fs),
                "h_e": sb("h_e" + sfx, [128, 8 * TP], BF16, fs),
                "asl": sb("asl" + sfx, [128, 2 * TP], F32, fs),
                "xacc2": sb("xacc2" + sfx, [128, 4 * D], F32, fs),
                "dwt2": sb("dwt2" + sfx, [128, 4 * E], F32, fs),
            }
            X = [xacc, fb["xacc2"]]
            DW = [dwt, fb["dwt2"]]
            bank_free.clear()
            with nc.Block() as blk:
                emit_b1(fb, blk, x_src, 0, X[0], DW[0], s_mm.n, s_ev.n)
            prev_hend = None
            prev_yend = None
            for p in range(NPASS):
                bank_free.clear()
                with nc.Block() as blk:
                    fin, hend8, yend8, yevend8 = emit_b2(
                        fb, blk, x_dst, p, X[p % 2], DW[p % 2],
                        first and p == 0, prev_hend, prev_yend,
                        phase_gate if p == 0 else None)
                    prev_hend, prev_yend = hend8, yend8
                    if p + 1 < NPASS:
                        emit_b1(fb, blk, x_src, p + 1, X[(p + 1) % 2], DW[(p + 1) % 2],
                                hend8, yevend8)
                    t0 = p * TP

                    @blk.scalar
                    def _(sc):
                        sc.wait_ge(s_ev.h, fin)
                        for j in range(4):
                            s_st.inc(sc.dma_start(x_dst[t0 + j * 128:t0 + (j + 1) * 128, :],
                                                  X[p % 2][:, j * D:(j + 1) * D]), 16)
                    st_done[p % 2] = s_st.n
            fs.close()

        def emit_attention(x_src, x_dst):
            ats = contextlib.ExitStack()
            Sst = sb("Sst", [128, H * HD], F32, ats)
            Sbf = sb("Sbf", [128, H * HD], BF16, ats)
            S_ready = [0]
            qk_fm = sb("qk_fm", [128, 2 * 16 * TP], BF16, ats)
            vk2 = sb("vk2", [128, 2 * 4 * D], BF16, ats)
            wring = sb("wring", [128, 2 * 16 * 512], BF16, ats)
            o_tm = sb("o_tm", [128, 4 * D], BF16, ats)
            wg1b = sb("wg1b", [128, 16 * 128], BF16, ats)
            wg2b = sb("wg2b", [128, D], BF16, ats)
            g1f = sb("g1f", [128, TP], BF16, ats)
            gsc = sb("gsc", [128, 512], F32, ats)
            u_tm2 = sb("u_tm2", [128, D], BF16, ats)

            def q_at(m, tj=None):
                c0 = m * TP
                return qk_fm[:, c0 + (0 if tj is None else tj * 128):
                             c0 + (TP if tj is None else (tj + 1) * 128)]

            def k_at(m, tj=None):
                c0 = 16 * TP + m * TP
                return qk_fm[:, c0 + (0 if tj is None else tj * 128):
                             c0 + (TP if tj is None else (tj + 1) * 128)]

            ogn_at = k_at

            def v_at(tj, m):
                return vk2[:, tj * D + m * 128:tj * D + (m + 1) * 128]

            def k2_at(tj, m):
                return vk2[:, 4 * D + tj * D + m * 128:4 * D + tj * D + (m + 1) * 128]

            ring_free = [(s_mm, 0), (s_mm, 0)]

            for sc_i in range(NPASS):
                t0 = sc_i * TP
                bank_free.clear()
                with nc.Block() as blk:
                    st_prev = s_st.n

                    @blk.gpsimd
                    def _(g):
                        g.wait_ge(s_st.h, st_prev)
                        if sc_i == 0:
                            g.memset(Sst[:], 0.0)
                            S_ready[0] = s_ev.inc(g.memset(Sbf[:], 0.0))
                        for j in range(4):
                            s_x.inc(g.dma_start(xacc[:, j * D:(j + 1) * D],
                                                x_src[t0 + j * 128:t0 + (j + 1) * 128, :]), 16)
                    xw = s_x.n

                    if sc_i == 0:
                        @blk.gpsimd
                        def _(g):
                            for dc in range(16):
                                s_w2l.inc(g.dma_start(wg1b[:, dc * 128:(dc + 1) * 128],
                                                      wg1_in[dc * 128:(dc + 1) * 128, :]), 16)
                            s_w2l.inc(g.dma_start(wg2b[:], wg2_in), 16)
                    wg_done = s_w2l.n

                    for j in range(4):
                        emit_ln_tile(blk, j, xw)

                    vstage_marks = []
                    for pi, (wsrc, kind) in enumerate([(wq_in, "q"), (wk_in, "k"), (wv_in, "v")]):
                        for mb in range(4):
                            slot = (pi * 4 + mb) % 2

                            wr = [s_wr0, s_wr1][slot]

                            @blk.gpsimd
                            def _(g, slot=slot, wsrc=wsrc, mb=mb, wr=wr):
                                s, nmark = ring_free[slot]
                                g.wait_ge(s.h, nmark)
                                for dc in range(16):
                                    wr.inc(g.dma_start(
                                        wring[:, slot * 8192 + dc * 512:slot * 8192 + (dc + 1) * 512],
                                        wsrc[dc * 128:(dc + 1) * 128, mb * 512:(mb + 1) * 512]), 16)
                            wl_mark = wr.n
                            sl_base = s_sl.n

                            @blk.tensor
                            def _(t):
                                t.wait_ge(wr.h, wl_mark)
                                if pi == 0 and mb == 0:
                                    t.wait_ge(s_te.h, s_te.n)
                                for mc in range(4):
                                    m = mb * 4 + mc
                                    pp = [pA, pB, pC, pD][mc]
                                    wait_bank(t, ("proj", mc))
                                    for dc in range(16):
                                        mi = t.matmul(pp[:],
                                                      wring[:, slot * 8192 + dc * 512 + mc * 128:
                                                            slot * 8192 + dc * 512 + (mc + 1) * 128],
                                                      u_fm[:, dc * TP:(dc + 1) * TP],
                                                      start=(dc == 0), stop=(dc == 15))
                                    s_mm.inc(mi)
                            ring_free[slot] = (s_mm, s_mm.n)
                            mm_marks = [s_mm.n - 3, s_mm.n - 2, s_mm.n - 1, s_mm.n]

                            if kind != "v":
                                @blk.scalar
                                def _(sc):
                                    for mc in range(4):
                                        m = mb * 4 + mc
                                        pp = [pA, pB, pC, pD][mc]
                                        sc.wait_ge(s_mm.h, mm_marks[mc])
                                        dst = q_at(m) if kind == "q" else k_at(m)
                                        s_sl.inc(sc.activation(dst, pp[:], AF.Silu))
                                        set_bank(("proj", mc), s_sl)
                            else:
                                te_base = s_te.n
                                tp_base = s_tp.n
                                for mc in range(4):
                                    m = mb * 4 + mc
                                    pp = [pA, pB, pC, pD][mc]

                                    @blk.scalar
                                    def _(sc):
                                        sc.wait_ge(s_mm.h, mm_marks[mc])
                                        if mc >= 2:
                                            sc.wait_ge(s_te.h, te_base + 4 * mc - 4)
                                        else:
                                            sc.wait_ge(s_te.h, te_base)
                                        s_sl.inc(sc.activation(
                                            u_tm[:, (mc % 2) * 512:(mc % 2 + 1) * 512],
                                            pp[:], AF.Silu))
                                        set_bank(("proj", mc), s_sl)
                                    silu_mc = s_sl.n

                                    @blk.tensor
                                    def _(t):
                                        t.wait_ge(s_sl.h, silu_mc)
                                        t.wait_ge(s_te.h, te_base)
                                        for tj in range(4):
                                            i = mc * 4 + tj
                                            bank = [pG, pH][i % 2]
                                            if i >= 2:
                                                t.wait_ge(s_te.h, te_base + i - 1)
                                            s_tp.inc(t.transpose(
                                                bank[:, 0:128],
                                                u_tm[:, (mc % 2) * 512 + tj * 128:(mc % 2) * 512 + (tj + 1) * 128],
                                                ident[:]))

                                    @blk.vector
                                    def _(v):
                                        for tj in range(4):
                                            i = mc * 4 + tj
                                            bank = [pG, pH][i % 2]
                                            v.wait_ge(s_tp.h, tp_base + i + 1)
                                            s_te.inc(v.tensor_copy(v_at(tj, m), bank[:, 0:128]))

                    @blk.tensor
                    def _(t):
                        if sc_i == 0:
                            t.wait_ge(s_w2l.h, wg_done)
                        wait_bank(t, ("proj", 0))
                        for dc in range(16):
                            mi = t.matmul(pA[:], wg1b[:, dc * 128:(dc + 1) * 128],
                                          u_fm[:, dc * TP:(dc + 1) * TP],
                                          start=(dc == 0), stop=(dc == 15))
                        s_mm.inc(mi)
                    g1_mark = s_mm.n

                    @blk.vector
                    def _(v):
                        v.wait_ge(s_mm.h, g1_mark)
                        s_ev.inc(v.tensor_copy(g1f[:], pA[:]))
                        set_bank(("proj", 0), s_ev)
                    g1f_done = s_ev.n
                    v_copies_done = s_te.n
                    ksilu_done = s_sl.n

                    u2 = [u_tm, u_tm2]
                    k2e_mark = {}
                    Lmm = {}
                    scT = {}
                    evT = {}
                    Tend = {}
                    Tcend = {}

                    def emit_k2(tj):
                        tpb = s_tp.n
                        slb = s_sl.n

                        @blk.tensor
                        def _(t):
                            if tj == 0:
                                t.wait_ge(s_te.h, v_copies_done)
                                t.wait_ge(s_sl.h, ksilu_done)
                            for h in range(H):
                                if h >= 2:
                                    t.wait_ge(s_sl.h, slb + h - 1)
                                else:
                                    wait_bank(t, ("T", h % 2))
                                s_tp.inc(t.transpose([pG, pH][h % 2][:, 0:128],
                                                     k_at(h, tj), ident[:]))

                        @blk.scalar
                        def _(sc):
                            for h in range(H):
                                sc.wait_ge(s_tp.h, tpb + h + 1)
                                s_sl.inc(sc.activation(k2_at(tj, h), [pG, pH][h % 2][:, 0:128],
                                                       AF.Identity, scale=kdec[:, h:h + 1]))
                                set_bank(("T", h % 2), s_sl)
                        k2e_mark[tj] = s_sl.n

                    def emit_L(tj):
                        mmT = s_mm.n
                        evTj = s_ev.n
                        scTj = s_sl.n
                        Lmm[tj] = mmT
                        evT[tj] = evTj
                        scT[tj] = scTj

                        @blk.tensor
                        def _(t):
                            t.wait_ge(s_sl.h, k2e_mark[tj])
                            if tj == 0:
                                t.wait_ge(s_ev.h, g1f_done)
                                if sc_i == 0:
                                    t.wait_ge(s_ev.h, S_ready[0])
                            else:
                                t.wait_ge(s_ev.h, evTj)
                            for h in range(H):
                                if h >= 1:
                                    t.wait_ge(s_ev.h, evTj + 4 * (h - 1) + 1)
                                mi = t.matmul(pA[:, 0:128], k_at(h, tj), q_at(h, tj),
                                              start=True, stop=True)
                                s_mm.inc(mi)
                                if h >= 1:
                                    t.wait_ge(s_sl.h, scTj + h)
                                mi = t.matmul(pB[:, 0:128], q_at(h, tj),
                                              Sbf[:, h * HD:(h + 1) * HD],
                                              start=True, stop=True)
                                s_mm.inc(mi)
                                t.wait_ge(s_ev.h, evTj + 4 * h + 1)
                                mi = t.matmul(pC[:, 0:128], sTbf[:, (h % 2) * 128:(h % 2 + 1) * 128],
                                              v_at(tj, h), start=True, stop=True)
                                s_mm.inc(mi)
                                if h >= 1:
                                    t.wait_ge(s_ev.h, evTj + 4 * (h - 1) + 3)
                                mi = t.matmul(pD[:, 0:128], k2_at(tj, h), v_at(tj, h),
                                              start=True, stop=True)
                                s_mm.inc(mi)
                            for ns in range(4):
                                if tj == 0 and ns == 0:
                                    t.wait_ge(s_ev.h, g1f_done)
                                if ns >= 2:
                                    t.wait_ge(s_ev.h, evTj + 64 + (ns - 2) + 1)
                                elif tj > 0:
                                    t.wait_ge(s_ev.h, evT[tj - 1] + 64 + 4)
                                else:
                                    wait_bank(t, ("gate", ns % 2))
                                s_mm.inc(t.matmul([pE_, pF_][ns % 2][:], g1f[:, tj * 128:(tj + 1) * 128],
                                                  wg2b[:, ns * 512:(ns + 1) * 512],
                                                  start=True, stop=True))

                    def emit_sc(tj):
                        scTj = scT[tj]
                        mmT = Lmm[tj]
                        evTj = evT[tj]

                        @blk.scalar
                        def _(sc):
                            for h in range(H):
                                sc.wait_ge(s_mm.h, mmT + 4 * h + 2)
                                if h >= 2:
                                    sc.wait_ge(s_ev.h, evTj + 4 * (h - 2) + 2)
                                s_sl.inc(sc.activation(o_tmp[:, (h % 2) * 128:(h % 2 + 1) * 128],
                                                       pB[:, 0:128], AF.Identity,
                                                       scale=qdec[:, h:h + 1]))
                            for ns in range(4):
                                sc.wait_ge(s_mm.h, mmT + 64 + ns + 1)
                                if ns >= 1:
                                    sc.wait_ge(s_ev.h, evTj + 64 + ns)
                                s_sl.inc(sc.activation(gsc[:], [pE_, pF_][ns % 2][:],
                                                       AF.Tanh, scale=0.5))
                            sc.wait_ge(s_ev.h, evTj + 64 + 4)
                            if tj >= 2:
                                sc.wait_ge(s_tp.h, s_tp.n + 16)
                            sc.drain()
                            for h in range(H):
                                sc.activation(u2[tj % 2][:, h * 128:(h + 1) * 128],
                                              o_tm[:, tj * D + h * 128:tj * D + (h + 1) * 128],
                                              AF.Square,
                                              accum_out=msq[:, tj * H + h:tj * H + h + 1])
                            sc.drain()
                            sc.activation(rsg[:, tj * H:(tj + 1) * H], msq[:, tj * H:(tj + 1) * H],
                                          AF.Ln, scale=1.0 / HD, bias=epst[:])
                            sc.drain()
                            s_sl.inc(sc.activation(rsg[:, tj * H:(tj + 1) * H],
                                                   rsg[:, tj * H:(tj + 1) * H], AF.Exp, scale=-0.5))
                            sc.drain()
                            for h in range(H):
                                mi = sc.activation(u2[tj % 2][:, h * 128:(h + 1) * 128],
                                                   o_tm[:, tj * D + h * 128:tj * D + (h + 1) * 128],
                                                   AF.Identity, scale=rsg[:, tj * H + h:tj * H + h + 1])
                            s_sl.inc(mi)

                    def emit_vec(tj):
                        scTj = scT[tj]
                        mmT = Lmm[tj]

                        @blk.vector
                        def _(v):
                            for h in range(H):
                                v.wait_ge(s_mm.h, mmT + 4 * h + 1)
                                s_ev.inc(v.tensor_tensor(sTbf[:, (h % 2) * 128:(h % 2 + 1) * 128],
                                                         pA[:, 0:128],
                                                         dt16[:, h * 128:(h + 1) * 128],
                                                         op=ALU.mult))
                                v.wait_ge(s_mm.h, mmT + 4 * h + 3)
                                v.wait_ge(s_sl.h, scTj + h + 1)
                                s_ev.inc(v.tensor_tensor(o_tm[:, tj * D + h * 128:tj * D + (h + 1) * 128],
                                                         pC[:, 0:128],
                                                         o_tmp[:, (h % 2) * 128:(h % 2 + 1) * 128],
                                                         op=ALU.add))
                                v.wait_ge(s_mm.h, mmT + 4 * h + 4)
                                s_ev.inc(v.scalar_tensor_tensor(Sst[:, h * HD:(h + 1) * HD],
                                                                Sst[:, h * HD:(h + 1) * HD],
                                                                cdec[:, h:h + 1],
                                                                pD[:, 0:128],
                                                                op0=ALU.mult, op1=ALU.add))
                                v.drain()
                                s_ev.inc(v.tensor_copy(Sbf[:, h * HD:(h + 1) * HD],
                                                       Sst[:, h * HD:(h + 1) * HD]))
                            for ns in range(4):
                                v.wait_ge(s_sl.h, scTj + 16 + ns + 1)
                                v.tensor_scalar(gsc[:], gsc[:], 0.5, 0.5,
                                                op0=ALU.mult, op1=ALU.add)
                                v.drain()
                                s_ev.inc(v.tensor_tensor(
                                    o_tm[:, tj * D + ns * 512:tj * D + (ns + 1) * 512],
                                    o_tm[:, tj * D + ns * 512:tj * D + (ns + 1) * 512],
                                    gsc[:], op=ALU.mult))

                    def emit_vecB(tj):
                        scTj = scT[tj]

                        @blk.vector
                        def _(v):
                            v.wait_ge(s_sl.h, scTj + 22)
                            s_ev.inc(v.tensor_tensor(u2[tj % 2][:], u2[tj % 2][:], nwb[:],
                                                     op=ALU.mult))

                    def emit_T(tj):
                        tpb = s_tp.n

                        teb = s_te.n

                        @blk.tensor
                        def _(t):
                            t.wait_ge(s_ev.h, evT[tj] + 69)
                            for dc in range(16):
                                if dc >= 2:
                                    t.wait_ge(s_te.h, teb + dc - 1)
                                else:
                                    wait_bank(t, ("T", dc % 2))
                                s_tp.inc(t.transpose([pG, pH][dc % 2][:, 0:128],
                                                     u2[tj % 2][:, dc * 128:(dc + 1) * 128],
                                                     ident[:]))
                        Tend[tj] = s_tp.n

                        @blk.vector
                        def _(v):
                            for dc in range(16):
                                v.wait_ge(s_tp.h, tpb + dc + 1)
                                s_te.inc(v.tensor_copy(ogn_at(dc, tj), [pG, pH][dc % 2][:, 0:128]))
                                set_bank(("T", dc % 2), s_te)
                        Tcend[tj] = s_te.n

                    emit_k2(0)
                    emit_k2(1)
                    emit_L(0); emit_sc(0); emit_vec(0); emit_vecB(0)
                    emit_k2(2)
                    emit_L(1); emit_sc(1); emit_vec(1); emit_vecB(1)
                    emit_k2(3)
                    emit_L(2); emit_sc(2); emit_vec(2)
                    emit_T(0)
                    emit_vecB(2)
                    emit_L(3); emit_sc(3); emit_vec(3)
                    emit_T(1)
                    emit_vecB(3)
                    emit_T(2)
                    emit_T(3)

                    for ns in range(4):
                        slot = ns % 2
                        wr = [s_wr0, s_wr1][slot]

                        @blk.gpsimd
                        def _(g, slot=slot, ns=ns, wr=wr):
                            s, nmark = ring_free[slot]
                            g.wait_ge(s.h, nmark)
                            for dc in range(16):
                                wr.inc(g.dma_start(
                                    wring[:, slot * 8192 + dc * 512:slot * 8192 + (dc + 1) * 512],
                                    wo_in[dc * 128:(dc + 1) * 128, ns * 512:(ns + 1) * 512]), 16)
                        wl_mark = wr.n

                        for tj in range(4):
                            k = ns * 4 + tj
                            pp = [pE_, pF_][k % 2]

                            @blk.tensor
                            def _(t):
                                if tj == 0:
                                    t.wait_ge(wr.h, wl_mark)
                                if ns == 0 and tj == 0:
                                    t.wait_ge(s_te.h, s_te.n)
                                wait_bank(t, ("gate", k % 2))
                                for dc in range(16):
                                    mi = t.matmul(pp[:], ogn_at(dc, tj),
                                                  wring[:, slot * 8192 + dc * 512:slot * 8192 + (dc + 1) * 512],
                                                  start=(dc == 0), stop=(dc == 15))
                                s_mm.inc(mi)
                            omm = s_mm.n

                            @blk.vector
                            def _(v):
                                v.wait_ge(s_mm.h, omm)
                                xs_ = xacc[:, tj * D + ns * 512:tj * D + (ns + 1) * 512]
                                s_ev.inc(v.tensor_tensor(xs_, xs_, pp[:], op=ALU.add))
                                set_bank(("gate", k % 2), s_ev)
                        ring_free[slot] = (s_mm, s_mm.n)

                    fin = s_ev.n

                    @blk.scalar
                    def _(sc):
                        sc.wait_ge(s_ev.h, fin)
                        for j in range(4):
                            s_st.inc(sc.dma_start(x_dst[t0 + j * 128:t0 + (j + 1) * 128, :],
                                                  xacc[:, j * D:(j + 1) * D]), 16)
            ats.close()

        run_ffn_phase(x_in, x1s if STAGE > 1 else y_out, True)

        if STAGE > 1:
            emit_attention(x1s, x2s if STAGE > 2 else y_out)

        if STAGE > 2:
            run_ffn_phase(x2s, y_out, False)

        with nc.Block() as blk:
            @blk.gpsimd
            def _(g):
                g.wait_ge(s_st.h, s_st.n)

    return nc


def _consts():
    ident = np.eye(128, dtype=np.float32)
    import ml_dtypes
    ident = ident.astype(ml_dtypes.bfloat16)
    ones = np.ones((1, 128), dtype=np.float32)
    jj = np.arange(128, dtype=np.float32)
    diff = jj[None, :] - jj[:, None]
    difft = np.where(diff >= 0, diff, 1e9).astype(np.float32)
    cj1 = (jj + 1.0).reshape(128, 1).astype(np.float32)
    cj127 = (127.0 - jj).reshape(128, 1).astype(np.float32)
    return ident, ones, difft, cj1, cj127


def kernel(L0_attn_x, L0_attn_log_decay, L0_attn_Wq, L0_attn_Wk, L0_attn_Wv,
           L0_attn_Wo, L0_attn_Wg1, L0_attn_Wg2, L0_attn_norm_w,
           L0_ffn_gate_w, L0_ffn_gate_b, L0_ffn_W1, L0_ffn_W2, L0_ffn_W3,
           L0_ffn_Ws1, L0_ffn_Ws2, L0_ffn_Ws3):
    ident, ones, difft, cj1, cj127 = _consts()
    x = np.ascontiguousarray(np.asarray(L0_attn_x, dtype=np.float32))
    shared = {
        "ld_in": np.asarray(L0_attn_log_decay, dtype=np.float32).reshape(1, H),
        "wq_in": np.ascontiguousarray(L0_attn_Wq, dtype=np.float32),
        "wk_in": np.ascontiguousarray(L0_attn_Wk, dtype=np.float32),
        "wv_in": np.ascontiguousarray(L0_attn_Wv, dtype=np.float32),
        "wo_in": np.ascontiguousarray(L0_attn_Wo, dtype=np.float32),
        "wg1_in": np.ascontiguousarray(L0_attn_Wg1, dtype=np.float32),
        "wg2_in": np.ascontiguousarray(L0_attn_Wg2, dtype=np.float32),
        "nw_in": np.asarray(L0_attn_norm_w, dtype=np.float32).reshape(1, D),
        "gw_in": np.ascontiguousarray(L0_ffn_gate_w, dtype=np.float32),
        "gb_in": np.asarray(L0_ffn_gate_b, dtype=np.float32).reshape(1, E),
        "w1_in": np.ascontiguousarray(L0_ffn_W1, dtype=np.float32).reshape(E * I, D),
        "w2_in": np.ascontiguousarray(L0_ffn_W2, dtype=np.float32).reshape(E * D, I),
        "w3_in": np.ascontiguousarray(L0_ffn_W3, dtype=np.float32).reshape(E * I, D),
        "ws1_in": np.ascontiguousarray(L0_ffn_Ws1, dtype=np.float32),
        "ws2_in": np.ascontiguousarray(L0_ffn_Ws2, dtype=np.float32),
        "ws3_in": np.ascontiguousarray(L0_ffn_Ws3, dtype=np.float32),
        "ident_in": ident, "ones_in": ones, "difft_in": difft,
        "cj1_in": cj1, "cj127_in": cj127,
    }
    nc = build()
    in_maps = [dict(shared, x_in=np.ascontiguousarray(x[b])) for b in range(8)]
    res = run_bass_kernel_spmd(nc, in_maps, core_ids=list(range(8)))
    out = np.stack([np.asarray(r["y_out"], dtype=np.float32) for r in res.results], axis=0)
    return out
